# version 01
# baseline (speedup 1.0000x reference)
import numpy as np
import ml_dtypes
import concourse.bass as bass
import concourse.mybir as mybir
from concourse.bass_utils import run_bass_kernel_spmd

F32 = mybir.dt.float32
BF16 = mybir.dt.bfloat16
I32 = mybir.dt.int32
AF = mybir.ActivationFunctionType
ALU = mybir.AluOpType
AX = mybir.AxisListType
BF = ml_dtypes.bfloat16

ENGINES = ("sync", "scalar", "vector", "tensor", "gpsimd")
DMAQ = {"sync": "qsync", "gpsimd": "qpool", "scalar": "qact"}


class Tok:
    __slots__ = ("sem", "val")

    def __init__(self, sem, val):
        self.sem = sem
        self.val = val


class Plan:

    def __init__(self, nc):
        self.nc = nc
        self.ops = []
        self.ecnt = {e: 0 for e in ENGINES}
        self.ecnt["sync"] = 1
        self.qcnt = {}

    SERIAL = ("vector", "scalar", "gpsimd")

    def op(self, eng, fn, after=(), inc=False):
        waits = [t for t in after if t is not None]
        if eng in self.SERIAL:
            inc = True
        tok = None
        if inc:
            self.ecnt[eng] += 1
            tok = Tok("e" + eng, self.ecnt[eng])
        self.ops.append(
            (eng, fn, waits,
             ("e" + eng, 1, self.ecnt[eng]) if inc else None))
        return tok

    def dma(self, eng, out, in_, after=(), stream=None):
        q = stream or ("q" + eng)
        self.qcnt[q] = self.qcnt.get(q, 0) + 16
        tok = Tok(q, self.qcnt[q])
        waits = [t for t in after if t is not None]
        self.ops.append(
            (eng, lambda e, out=out, in_=in_: e.dma_start(out=out, in_=in_),
             waits, (q, 16, self.qcnt[q])))
        return tok

    def now(self, eng):
        if eng in ENGINES:
            return Tok("e" + eng, self.ecnt[eng])
        return Tok(eng, self.qcnt[eng])

    def build(self):
        nc = self.nc
        sems = {}
        names = ["e" + e for e in ENGINES] + sorted(self.qcnt)
        for n in names:
            sems[n] = nc.alloc_semaphore(name="s_" + n)
        for e in ENGINES:
            if e != "sync":
                self.ecnt[e] += 1

        with nc.Block() as block:
            def mk_body(name):
                def body(eng):
                    if name == "sync":
                        for s in sems.values():
                            eng.sem_clear(s)
                        eng.nop().then_inc(sems["esync"], 1)
                    else:
                        eng.wait_op(sems["esync"], 1, "sem-eq")
                    hwm = {}
                    for (oe, fn, waits, inc) in self.ops:
                        if oe != name:
                            continue
                        need = {}
                        for t in waits:
                            need[t.sem] = max(need.get(t.sem, 0), t.val)
                        if (name in self.SERIAL and inc is not None
                                and inc[0] == "e" + name):
                            pv = inc[2] - 1
                            if pv > 0:
                                need["e" + name] = max(
                                    need.get("e" + name, 0), pv)
                        for s, v in need.items():
                            if hwm.get(s, 0) >= v:
                                continue
                            hwm[s] = v
                            eng.wait_ge(sems[s], v)
                        inst = fn(eng)
                        if inc is not None:
                            inst.then_inc(sems[inc[0]], inc[1])
                    if name == "sync":
                        for e in ENGINES:
                            if e != "sync":
                                eng.wait_ge(sems["e" + e], self.ecnt[e])
                        for q in sorted(self.qcnt):
                            if self.qcnt[q]:
                                eng.wait_ge(sems[q], self.qcnt[q])
                        eng.nop().then_inc(sems["esync"], 1)
                    else:
                        eng.wait_ge(sems["e" + name], self.ecnt[name] - 1)
                        eng.nop().then_inc(sems["e" + name], 1)
                return body
            for name in ENGINES:
                getattr(block, name)(mk_body(name))


def cdiv(a, b):
    return -(-a // b)



class Common:
    def __init__(self, nc, p, T, ps_shapes=(1024, 1024, 1024, 1024)):
        self.nc, self.p, self.T = nc, p, T
        self.xb = nc.alloc_sbuf_tensor("c_xb", [128, T], BF16)
        self.sq = nc.alloc_sbuf_tensor("c_sq", [128, T], BF16)
        self.r_mu = nc.alloc_sbuf_tensor("c_rmu", [1, T], F32)
        self.r_v = nc.alloc_sbuf_tensor("c_rv", [1, T], F32)
        self.lntmp = nc.alloc_sbuf_tensor("c_lntmp", [128, T], F32)
        self.r_b = self.lntmp
        self.onescol = nc.alloc_sbuf_tensor("c_onescol", [128, 1], BF16)
        self.onesrow_f = nc.alloc_sbuf_tensor("c_onesrow_f", [1, 128], F32)
        self.ident_f = nc.alloc_sbuf_tensor("c_ident_f", [128, 128], F32)
        self.ident_b = nc.alloc_sbuf_tensor("c_ident_b", [128, 128], BF16)
        ident_d = nc.inline_tensor(np.eye(128, dtype=np.float32),
                                   name="c_ident_src")
        identb_d = nc.inline_tensor(np.eye(128, dtype=BF),
                                    name="c_identb_src")
        self.PS = [nc.alloc_psum_tensor(f"c_ps{i}", [128, s], F32)
                   for i, s in enumerate(ps_shapes)]
        self.cbuf = nc.alloc_sbuf_tensor("c_consts", [128, 32], F32)
        self.cmap = {}
        p.op("vector", lambda e: e.memset(self.onescol.ap(), 1.0))
        p.op("vector", lambda e: e.memset(self.onesrow_f.ap(), 1.0))
        for v in (1e-5, 1e-6, -0.5, 0.5, 1.0 / 2048, 1.0 / 4096,
                  1e-5 * 4096, 1e-6 * 2048):
            self.constf(v)
        p.op("vector", lambda e: e.nop(), inc=True)
        t1 = p.dma("sync", self.ident_f.ap(), ident_d.ap(),
                   stream="qconst")
        t2 = p.dma("sync", self.ident_b.ap(), identb_d.ap(),
                   stream="qconst")
        self.const_tok = Tok("evector", p.ecnt["vector"])
        self.ident_tok = t2

    def constf(self, val):
        val = float(val)
        if val not in self.cmap:
            assert not hasattr(self, "const_tok"), \
                f"constf({val}) after init: pre-register it in Common"
            i = len(self.cmap)
            assert i < 32
            self.cmap[val] = i
            ap = self.cbuf.ap()[:, i:i + 1]
            self.p.op("vector", lambda e, ap=ap, val=val: e.memset(ap, val))
            self.nc.const_aps.register(F32, val, ap)
        return val

    def layernorm(self, x_chunks, out_chunks, T=None, eps=1e-5,
                  gamma=None, beta=None, after=()):
        p = self.p
        T = T or self.T
        nf = len(x_chunks)
        D = nf * 128
        if hasattr(self, "ln_ps"):
            PSsum, PSsq, self.sq_off, PSb1, PSb2 = self.ln_ps
        else:
            PSsum, PSsq, PSb1, PSb2 = self.PS
            self.sq_off = 0
        after = list(after) + [self.const_tok]
        mm = []
        for i, xc in enumerate(x_chunks):
            tc = p.op("vector", lambda e, xc=xc: e.tensor_copy(
                out=self.xb.ap()[:, :T], in_=xc), after=after, inc=True)
            ts = p.op("scalar", lambda e, xc=xc: e.activation(
                out=self.sq.ap()[:, :T], in_=xc, func=AF.Square),
                after=after, inc=True)
            mm = []
            for n0 in range(0, T, 512):
                n1 = min(T, n0 + 512)
                mm.append(p.op("tensor", lambda e, i=i, n0=n0, n1=n1:
                    e.matmul(PSsum.ap()[0:1, n0:n1], self.onescol.ap(),
                             self.xb.ap()[:, n0:n1],
                             start=(i == 0), stop=(i == nf - 1)),
                    after=[tc], inc=True))
                mm.append(p.op("tensor", lambda e, i=i, n0=n0, n1=n1:
                    e.matmul(PSsq.ap()[0:1, self.sq_off + n0:
                                       self.sq_off + n1],
                             self.onescol.ap(), self.sq.ap()[:, n0:n1],
                             start=(i == 0), stop=(i == nf - 1)),
                    after=[ts], inc=True))
            if i < nf - 1:
                p.op("vector", lambda e: e.nop(), after=[mm[-2]])
                p.op("scalar", lambda e: e.nop(), after=[mm[-1]])
            after = []
        r_mu, r_v, r_b = self.r_mu, self.r_v, self.r_b
        tmu = p.op("vector", lambda e: e.tensor_scalar_mul(
            out=r_mu.ap()[:, :T], in0=PSsum.ap()[0:1, :T], scalar1=1.0 / D),
            after=mm, inc=True)
        t1 = p.op("vector", lambda e: e.tensor_scalar_mul(
            out=r_v.ap()[:, :T],
            in0=PSsq.ap()[0:1, self.sq_off:self.sq_off + T],
            scalar1=1.0 / D), after=mm, inc=True)
        t2 = p.op("scalar", lambda e: e.activation(
            out=r_b.ap()[0:1, :T], in_=r_mu.ap()[:, :T], func=AF.Square),
            after=[tmu], inc=True)
        t3 = p.op("vector", lambda e: e.tensor_sub(
            out=r_v.ap()[:, :T], in0=r_v.ap()[:, :T],
            in1=r_b.ap()[0:1, :T]), after=[t1, t2], inc=True)
        t4 = p.op("scalar", lambda e: e.activation(
            out=r_v.ap()[:, :T], in_=r_v.ap()[:, :T], func=AF.Ln,
            bias=self.constf(eps)), after=[t3], inc=True)
        t5 = p.op("scalar", lambda e: e.activation(
            out=r_v.ap()[:, :T], in_=r_v.ap()[:, :T], func=AF.Exp,
            scale=-0.5), after=[t4], inc=True)
        t6 = p.op("vector", lambda e: e.tensor_mul(
            out=r_b.ap()[0:1, :T], in0=r_mu.ap()[:, :T],
            in1=r_v.ap()[:, :T]), after=[t5], inc=True)
        t7 = p.op("vector", lambda e: e.tensor_scalar_mul(
            out=r_b.ap()[0:1, :T], in0=r_b.ap()[0:1, :T], scalar1=-1.0),
            after=[t6], inc=True)
        bt = []
        for n0 in range(0, T, 512):
            n1 = min(T, n0 + 512)
            bt.append(p.op("tensor", lambda e, n0=n0, n1=n1: e.matmul(
                PSb1.ap()[:, n0:n1], self.onesrow_f.ap(),
                r_v.ap()[0:1, n0:n1], start=True, stop=True),
                after=[t5, t7], inc=True))
            bt.append(p.op("tensor", lambda e, n0=n0, n1=n1: e.matmul(
                PSb2.ap()[:, n0:n1], self.onesrow_f.ap(),
                r_b.ap()[0:1, n0:n1], start=True, stop=True),
                after=[t7], inc=True))
        toks = []
        for i, (xc, oc) in enumerate(zip(x_chunks, out_chunks)):
            ta = p.op("vector", lambda e, xc=xc: e.tensor_mul(
                out=self.lntmp.ap()[:, :T], in0=xc, in1=PSb1.ap()[:, :T]),
                after=bt, inc=True)
            if gamma is None:
                tl = p.op("vector", lambda e, oc=oc: e.tensor_add(
                    out=oc, in0=self.lntmp.ap()[:, :T],
                    in1=PSb2.ap()[:, :T]), after=[ta], inc=True)
            else:
                p.op("vector", lambda e: e.tensor_add(
                    out=self.lntmp.ap()[:, :T], in0=self.lntmp.ap()[:, :T],
                    in1=PSb2.ap()[:, :T]), after=[ta])
                tl = p.op("vector", lambda e, oc=oc, i=i: e.tensor_scalar(
                    out=oc, in0=self.lntmp.ap()[:, :T], scalar1=gamma[i],
                    scalar2=beta[i], op0=ALU.mult, op1=ALU.add), inc=True)
            toks.append(tl)
        return toks



def emit_f1(nc, p, C, Y, R, T, W, y_toks, name="f1"):
    NF, NH = 16, 11
    PS = C.PS

    gw = nc.alloc_sbuf_tensor(name + "_gw", [128, 16 * 8], BF16)
    gwt_toks = [p.dma("gpsimd", gw.ap()[:, 8 * i:8 * (i + 1)],
                      W["gate_wT"][128 * i:128 * (i + 1), :], stream="qgw")
                for i in range(16)]
    gwt = [gwt_toks[-1]] * 16
    ncht = T // 128
    CMB = [nc.alloc_sbuf_tensor(f"{name}_cmb{e}", [128, T], BF16)
           for e in range(8)]
    sc_sb = nc.alloc_sbuf_tensor(name + "_scsb", [128, 8], F32)
    sc2 = nc.alloc_sbuf_tensor(name + "_sc2", [128, 8], F32)
    mrow = nc.alloc_sbuf_tensor(name + "_mrow", [128, 4], F32)
    cmb_toks = []
    prev_tsg = None
    prev_bm = None
    prev_ev = None
    for tchunk in range(ncht):
        t0 = 128 * tchunk
        tm = None
        for i in range(16):
            deps = [y_toks[i], gwt[i]]
            if i == 0 and prev_tsg is not None:
                deps.append(prev_tsg)
            tm = p.op("tensor", lambda e, i=i, t0=t0: e.matmul(
                PS[0].ap()[:, 0:8], Y[i].ap()[:, t0:t0 + 128],
                gw.ap()[:, 8 * i:8 * (i + 1)],
                start=(i == 0), stop=(i == 15)),
                after=deps, inc=(i == 15))
        tsg = p.op("scalar", lambda e: e.activation(
            out=sc_sb.ap(), in_=PS[0].ap()[:, 0:8], func=AF.Sigmoid),
            after=[tm], inc=True)
        prev_tsg = tsg
        t1 = p.op("vector", lambda e: e.reduce_max(
            out=mrow.ap()[:, 0:1], in_=sc_sb.ap(), axis=AX.X),
            after=[tsg, prev_bm], inc=True)
        t2 = p.op("vector", lambda e: e.tensor_scalar(
            out=sc2.ap(), in0=sc_sb.ap(), scalar1=mrow.ap()[:, 0:1],
            scalar2=-1e30, op0=ALU.is_ge, op1=ALU.mult), after=[t1], inc=True)
        t3 = p.op("vector", lambda e: e.tensor_add(
            out=sc2.ap(), in0=sc2.ap(), in1=sc_sb.ap()), after=[t2], inc=True)
        t4 = p.op("vector", lambda e: e.reduce_max(
            out=mrow.ap()[:, 1:2], in_=sc2.ap(), axis=AX.X),
            after=[t3], inc=True)
        t5 = p.op("vector", lambda e: e.tensor_add(
            out=mrow.ap()[:, 2:3], in0=mrow.ap()[:, 0:1],
            in1=mrow.ap()[:, 1:2]), after=[t4], inc=True)
        p.op("vector", lambda e: e.tensor_scalar_add(
            out=mrow.ap()[:, 2:3], in0=mrow.ap()[:, 2:3], scalar1=1e-20))
        t6 = p.op("vector", lambda e: e.reciprocal(
            out=mrow.ap()[:, 2:3], in_=mrow.ap()[:, 2:3]), inc=True)
        p.op("vector", lambda e: e.tensor_scalar_mul(
            out=mrow.ap()[:, 2:3], in0=mrow.ap()[:, 2:3], scalar1=2.5))
        t7 = p.op("vector", lambda e: e.tensor_scalar(
            out=sc2.ap(), in0=sc_sb.ap(), scalar1=mrow.ap()[:, 1:2],
            scalar2=None, op0=ALU.is_ge), after=[t6], inc=True)
        t8 = p.op("vector", lambda e: e.tensor_mul(
            out=sc2.ap(), in0=sc2.ap(), in1=sc_sb.ap()), after=[t7], inc=True)
        t9 = p.op("vector", lambda e: e.tensor_scalar_mul(
            out=sc2.ap(), in0=sc2.ap(), scalar1=mrow.ap()[:, 2:3]),
            after=[t8], inc=True)
        evs = []
        for e8 in range(8):
            bm = p.op("tensor", lambda e, e8=e8: e.matmul(
                PS[1].ap()[:, 128 * e8:128 * (e8 + 1)],
                sc2.ap()[:, e8:e8 + 1].broadcast_to([128, 128]),
                C.ident_f.ap(), start=True, stop=True),
                after=[t9, C.ident_tok, prev_ev], inc=True)
            prev_bm = bm
            ev = p.op("vector", lambda e, e8=e8, t0=t0: e.tensor_copy(
                out=CMB[e8].ap()[:, t0:t0 + 128],
                in_=PS[1].ap()[:, 128 * e8:128 * (e8 + 1)]),
                after=[bm], inc=True)
            prev_ev = ev
            evs.append(ev)
        cmb_toks.append(evs[-1])

    wbuf = [nc.alloc_sbuf_tensor(f"{name}_wb{i}", [128, 2048], BF16)
            for i in range(4)]
    w2buf = [nc.alloc_sbuf_tensor(f"{name}_w2b{i}", [128, 1408], BF16)
             for i in range(2)]
    g_e = nc.alloc_sbuf_tensor(name + "_g", [128, NH * T], BF16)
    silu_t = nc.alloc_sbuf_tensor(name + "_silu", [128, T], BF16)

    racc_toks = [None] * NF
    prev_racc = None
    experts = list(range(8)) + ["shared"]
    prev_use = {0: None, 1: None, 2: None, 3: None}
    prev_w2use = {0: None, 1: None}
    prev_ydone = None
    prev_ts = None
    prev_tg = None
    for ex in experts:
        hu_done = []
        for m in range(NH):
            if ex == "shared":
                w1ap = W["sw1"][:, 128 * m:128 * (m + 1)]
                w3ap = W["sw3"][:, 128 * m:128 * (m + 1)]
            else:
                w1ap = W["w1"][ex, :, 128 * m:128 * (m + 1)]
                w3ap = W["w3"][ex, :, 128 * m:128 * (m + 1)]
            s1, s3 = (2 * (m % 2)), (2 * (m % 2) + 1)
            dst1 = wbuf[s1].ap().rearrange("p (k c) -> p k c", k=16)
            lt1 = p.dma("gpsimd", dst1,
                        w1ap.rearrange("(k p) c -> p k c", p=128),
                        after=[prev_use[s1]], stream=f"w{s1}")
            dst3 = wbuf[s3].ap().rearrange("p (k c) -> p k c", k=16)
            lt3 = p.dma("gpsimd", dst3,
                        w3ap.rearrange("(k p) c -> p k c", p=128),
                        after=[prev_use[s3]], stream=f"w{s3}")
            pts, ptg = prev_ts, prev_tg
            th = tu = None
            for k in range(16):
                for n0 in (0, 512):
                    hdeps = [lt1, y_toks[k]]
                    udeps = [lt3, y_toks[k]]
                    if k == 0 and n0 == 0:
                        hdeps += [pts]
                        udeps += [ptg]
                        if ex == 0 and m == 0:
                            hdeps.append(prev_tsg)
                            udeps.append(prev_ev)
                    th = p.op("tensor", lambda e, k=k, n0=n0, s1=s1: e.matmul(
                        PS[0].ap()[:, n0:n0 + 512],
                        wbuf[s1].ap()[:, 128 * k:128 * (k + 1)],
                        Y[k].ap()[:, n0:n0 + 512],
                        start=(k == 0), stop=(k == 15)),
                        after=hdeps, inc=(k == 15 and n0 == 512))
                    tu = p.op("tensor", lambda e, k=k, n0=n0, s3=s3: e.matmul(
                        PS[1].ap()[:, n0:n0 + 512],
                        wbuf[s3].ap()[:, 128 * k:128 * (k + 1)],
                        Y[k].ap()[:, n0:n0 + 512],
                        start=(k == 0), stop=(k == 15)),
                        after=udeps, inc=(k == 15 and n0 == 512))
            prev_use[s1] = th
            prev_use[s3] = tu
            ts = p.op("scalar", lambda e: e.activation(
                out=silu_t.ap(), in_=PS[0].ap()[:, :T], func=AF.Silu),
                after=[th], inc=True)
            prev_ts = ts
            tg = p.op("vector", lambda e, m=m: e.tensor_mul(
                out=g_e.ap()[:, m * T:(m + 1) * T], in0=silu_t.ap(),
                in1=PS[1].ap()[:, :T]), after=[ts, tu, prev_ydone], inc=True)
            prev_tg = tg
            if ex != "shared":
                tg = p.op("vector", lambda e, m=m, ex=ex: e.tensor_mul(
                    out=g_e.ap()[:, m * T:(m + 1) * T],
                    in0=g_e.ap()[:, m * T:(m + 1) * T],
                    in1=CMB[ex].ap()), after=[tg] + cmb_toks, inc=True)
            hu_done.append(tg)
        for f in range(NF):
            if ex == "shared":
                w2ap = W["sw2"][:, 128 * f:128 * (f + 1)]
            else:
                w2ap = W["w2"][ex, :, 128 * f:128 * (f + 1)]
            s2 = f % 2
            dst2 = w2buf[s2].ap().rearrange("p (k c) -> p k c", k=NH)
            lt2 = p.dma("gpsimd", dst2,
                        w2ap.rearrange("(k p) c -> p k c", p=128),
                        after=[prev_w2use[s2]], stream=f"w2{s2}")
            tmm = None
            for k in range(NH):
                for n0 in (0, 512):
                    deps = [lt2, hu_done[k]]
                    if k == 0 and n0 == 0:
                        deps.append(prev_racc)
                    tmm = p.op("tensor", lambda e, k=k, n0=n0, s2=s2:
                        e.matmul(PS[2].ap()[:, n0:n0 + 512],
                                 w2buf[s2].ap()[:, 128 * k:128 * (k + 1)],
                                 g_e.ap()[:, k * T + n0:k * T + n0 + 512],
                                 start=(k == 0), stop=(k == NH - 1)),
                        after=deps, inc=(k == NH - 1 and n0 == 512))
            prev_w2use[s2] = tmm
            racc_toks[f] = p.op("vector", lambda e, f=f: e.scalar_tensor_tensor(
                out=R[f].ap(), in0=PS[2].ap()[:, :T], scalar=0.5,
                in1=R[f].ap(), op0=ALU.mult, op1=ALU.add),
                after=[tmm], inc=True)
            prev_racc = racc_toks[f]
        prev_ydone = racc_toks[NF - 1]
    return racc_toks




def emit_f0(nc, p, C, Y, R, T, W, y_toks, name="f0"):
    PS = C.PS
    NF = 16

    xl = nc.alloc_sbuf_tensor(name + "_xl", [128, 64], F32)
    xn = nc.alloc_sbuf_tensor(name + "_xn", [128, 64], BF16)
    xmid = nc.alloc_sbuf_tensor(name + "_xmid", [128, 64], F32)
    selc = nc.alloc_sbuf_tensor(name + "_sel", [4, 1], F32)
    ing = nc.alloc_sbuf_tensor(name + "_ing", [128, 16], F32)
    inb = nc.alloc_sbuf_tensor(name + "_inb", [128, 16], F32)
    gb = nc.alloc_sbuf_tensor(name + "_gb", [128, 16], F32)
    outg = nc.alloc_sbuf_tensor(name + "_outg", [128, 16], F32)
    outb = nc.alloc_sbuf_tensor(name + "_outb", [128, 16], F32)
    b_agg = nc.alloc_sbuf_tensor(name + "_bagg", [4, 2048], BF16)
    b_d1q = nc.alloc_sbuf_tensor(name + "_bd1q", [4, 2048], BF16)
    b_d2 = nc.alloc_sbuf_tensor(name + "_bd2", [4, 2048], BF16)
    b_bc = nc.alloc_sbuf_tensor(name + "_bbc", [4, 2048], BF16)
    png = nc.alloc_sbuf_tensor(name + "_png", [4, 2048], BF16)
    pnb = nc.alloc_sbuf_tensor(name + "_pnb", [4, 2048], BF16)
    lds = []
    lds.append(p.dma("sync", xl.ap(), W["xlast"], stream="qsm"))
    for dst, key in ((selc, "sel"), (ing, "in_g"), (inb, "in_b"),
                     (gb, "gate_b"), (outg, "out_g"), (outb, "out_b")):
        lds.append(p.dma("sync", dst.ap(), W[key], stream="qsm"))
    lds2 = []
    for dst, key in ((b_agg, "agg_b4"), (b_d2, "d2b4"),
                     (b_bc, "bc_b4"), (png, "pn_g4"), (pnb, "pn_b4")):
        lds2.append(p.dma("gpsimd", dst.ap(), W[key], stream="qsmb"))
    sm_tok0 = lds[-1]
    sm_tok = lds2[-1]


    xchunks = [xl.ap()[:, 4 * i:4 * (i + 1)] for i in range(NF)]
    mchunks = [xmid.ap()[:, 4 * i:4 * (i + 1)] for i in range(NF)]
    nchunks = [xn.ap()[:, 4 * i:4 * (i + 1)] for i in range(NF)]
    after0 = [sm_tok0, p.now("vector"), p.now("scalar"), p.now("tensor")]
    t_ln1 = C.layernorm(xchunks, mchunks, T=4, eps=1e-5, after=after0)
    t_ln2 = C.layernorm(mchunks, nchunks, T=4, eps=1e-6,
                        gamma=[ing.ap()[:, i:i + 1] for i in range(NF)],
                        beta=[inb.ap()[:, i:i + 1] for i in range(NF)],
                        after=t_ln1)

    wsl = [nc.alloc_sbuf_tensor(f"{name}_wsl{i}", [128, 2048], BF16)
           for i in range(2)]
    prev_wuse = [None, None]

    th_tm = nc.alloc_sbuf_tensor(name + "_th", [4, 2048], F32)
    t_tm = nc.alloc_sbuf_tensor(name + "_ttm", [4, 2048], F32)
    g_tm = nc.alloc_sbuf_tensor(name + "_gtm", [4, 2048], F32)
    t_fm = nc.alloc_sbuf_tensor(name + "_tfm", [128, 64], BF16)
    g_fm = nc.alloc_sbuf_tensor(name + "_gfm", [128, 256], BF16)
    r4a = nc.alloc_sbuf_tensor(name + "_r4a", [4, 1], F32)
    r4b = nc.alloc_sbuf_tensor(name + "_r4b", [4, 1], F32)
    bcv = nc.alloc_sbuf_tensor(name + "_bcv", [128, 16], F32)

    def mm_tm(src_fm_chunks, w_dram, kin, nout, src_tok, stream, evac_fn,
              after_ps=()):
        guard = list(after_ps)
        all_toks = []
        nq = nout // 2048
        for q in range(nq):
            last = None
            for k in range(kin // 128):
                s = k % 2
                slab = wsl[s]
                lt = p.dma("gpsimd", slab.ap(),
                           w_dram[128 * k:128 * (k + 1),
                                  2048 * q:2048 * (q + 1)],
                           after=[prev_wuse[s]], stream=stream + str(s))
                for n0 in range(0, 2048, 512):
                    reg = PS[0] if n0 < 1024 else PS[1]
                    c0 = n0 % 1024
                    deps = [lt, src_tok]
                    if k == 0 and n0 == 0:
                        deps += guard
                    last = p.op("tensor",
                        lambda e, k=k, n0=n0, reg=reg, c0=c0, slab=slab,
                        kin=kin:
                        e.matmul(reg.ap()[0:4, c0:c0 + 512],
                                 src_fm_chunks[k],
                                 slab.ap()[:, n0:n0 + 512],
                                 start=(k == 0), stop=(k == kin // 128 - 1)),
                        after=deps, inc=True)
                prev_wuse[s] = last
            guard = evac_fn(q, last)
            all_toks += guard
        return all_toks

    xn_chunks = [xn.ap()[:, 4 * i:4 * (i + 1)] for i in range(NF)]

    def agg_evac(q, mtok):
        toks = []
        for half in range(2):
            reg = PS[0] if half == 0 else PS[1]
            toks.append(p.op("vector",
                lambda e, reg=reg, half=half, q=q: e.tensor_add(
                    out=th_tm.ap()[:, 2048 * q + 1024 * half:
                                   2048 * q + 1024 * (half + 1)],
                    in0=reg.ap()[0:4, 0:1024],
                    in1=b_agg.ap()[:, 1024 * half:1024 * (half + 1)]),
                after=[mtok, sm_tok], inc=True))
        return toks

    th_toks = mm_tm(xn_chunks, W["agg_w"], 2048, 2048, t_ln2[-1], "wA",
                    agg_evac, after_ps=[p.now("vector"), p.now("scalar")])
    th_tok = th_toks[-1]
    bd1_use = [None]

    for step in range(4):
        t1 = p.op("vector", lambda e: e.reduce_sum(
            out=r4a.ap(), in_=th_tm.ap(), axis=AX.X), after=[th_tok],
            inc=True)
        t2 = p.op("vector", lambda e: e.tensor_scalar_mul(
            out=r4a.ap(), in0=r4a.ap(), scalar1=1.0 / 2048), after=[t1],
            inc=True)
        t3 = p.op("vector", lambda e: e.tensor_scalar(
            out=t_tm.ap(), in0=th_tm.ap(), scalar1=r4a.ap(),
            scalar2=None, op0=ALU.subtract), after=[t2], inc=True)
        t4 = p.op("vector", lambda e: e.scalar_tensor_tensor(
            out=g_tm.ap(), in0=t_tm.ap(), scalar=1.0, in1=t_tm.ap(),
            op0=ALU.mult, op1=ALU.mult, accum_out=r4b.ap()),
            after=[t3], inc=True)
        t5 = p.op("scalar", lambda e: e.activation(
            out=r4b.ap(), in_=r4b.ap(), func=AF.Ln,
            bias=C.constf(1e-6), scale=C.constf(1.0 / 2048)),
            after=[t4], inc=True)
        t6 = p.op("scalar", lambda e: e.activation(
            out=r4b.ap(), in_=r4b.ap(), func=AF.Exp, scale=-0.5),
            after=[t5], inc=True)
        t7 = p.op("vector", lambda e: e.tensor_scalar_mul(
            out=t_tm.ap(), in0=t_tm.ap(), scalar1=r4b.ap()),
            after=[t6], inc=True)
        t8 = p.op("vector", lambda e: e.tensor_mul(
            out=t_tm.ap(), in0=t_tm.ap(), in1=png.ap()), after=[t7, sm_tok],
            inc=True)
        t9 = p.op("vector", lambda e: e.tensor_add(
            out=t_tm.ap(), in0=t_tm.ap(), in1=pnb.ap()), after=[t8],
            inc=True)
        tf_toks = []
        prev_ev = None
        for i in range(NF):
            tt = p.op("tensor", lambda e, i=i: e.transpose(
                PS[2].ap()[:, 0:4], t_tm.ap()[0:4, 128 * i:128 * (i + 1)],
                C.ident_f.ap()[0:4, 0:4]),
                after=[t9, C.ident_tok, prev_ev], inc=True)
            ev = p.op("vector", lambda e, i=i: e.tensor_copy(
                out=t_fm.ap()[:, 4 * i:4 * (i + 1)], in_=PS[2].ap()[:, 0:4]),
                after=[tt], inc=True)
            prev_ev = ev
            tf_toks.append(ev)
        tf_tok = tf_toks[-1]
        tfc = [t_fm.ap()[:, 4 * i:4 * (i + 1)] for i in range(NF)]
        gf_toks = []

        def d1_evac(q, mtok):
            bt = p.dma("gpsimd", b_d1q.ap(),
                       W["d1b4"][:, 2048 * q:2048 * (q + 1)],
                       after=[bd1_use[0]], stream="qbd1")
            toks = []
            for half in range(2):
                reg = PS[0] if half == 0 else PS[1]
                c0 = 1024 * half
                ta = p.op("vector", lambda e, reg=reg, c0=c0:
                    e.tensor_add(
                        out=g_tm.ap()[:, c0:c0 + 1024],
                        in0=reg.ap()[0:4, 0:1024],
                        in1=b_d1q.ap()[:, c0:c0 + 1024]),
                    after=[mtok, bt], inc=True)
                bd1_use[0] = ta
                toks.append(p.op("scalar", lambda e, c0=c0: e.activation(
                    out=g_tm.ap()[:, c0:c0 + 1024],
                    in_=g_tm.ap()[:, c0:c0 + 1024], func=AF.Gelu),
                    after=[ta], inc=True))
            prev_ev = gf_toks[-1] if gf_toks else None
            for i in range(16):
                gi = 16 * q + i
                tt = p.op("tensor", lambda e, i=i: e.transpose(
                    PS[2].ap()[:, 0:4],
                    g_tm.ap()[0:4, 128 * i:128 * (i + 1)],
                    C.ident_f.ap()[0:4, 0:4]),
                    after=toks + [C.ident_tok, prev_ev], inc=True)
                ev = p.op("vector", lambda e, gi=gi: e.tensor_copy(
                    out=g_fm.ap()[:, 4 * gi:4 * (gi + 1)],
                    in_=PS[2].ap()[:, 0:4]), after=[tt], inc=True)
                prev_ev = ev
                gf_toks.append(ev)
            return [gf_toks[-1]]

        mm_tm(tfc, W["d1_w"], 2048, 8192, tf_tok, "wB", d1_evac,
              after_ps=[th_tok])
        gf_tok = gf_toks[-1]
        tb = p.op("vector", lambda e: e.tensor_add(
            out=th_tm.ap(), in0=th_tm.ap(), in1=b_d2.ap()),
            after=[th_tok, sm_tok], inc=True)
        gfc = [g_fm.ap()[:, 4 * i:4 * (i + 1)] for i in range(64)]

        def d2_evac(q, mtok):
            toks = []
            for half in range(2):
                reg = PS[0] if half == 0 else PS[1]
                c0 = 1024 * half
                toks.append(p.op("vector", lambda e, reg=reg, c0=c0:
                    e.tensor_add(
                        out=th_tm.ap()[:, c0:c0 + 1024],
                        in0=reg.ap()[0:4, 0:1024],
                        in1=th_tm.ap()[:, c0:c0 + 1024]),
                    after=[mtok, tb], inc=True))
            return toks

        th_toks = mm_tm(gfc, W["d2_w"], 8192, 2048, gf_tok, "wB", d2_evac,
                        after_ps=gf_toks[-1:])
        th_tok = th_toks[-1]

    thc = []
    tf_toks = []
    prev_ev = None
    for i in range(NF):
        tt = p.op("tensor", lambda e, i=i: e.transpose(
            PS[2].ap()[:, 0:4], th_tm.ap()[0:4, 128 * i:128 * (i + 1)],
            C.ident_f.ap()[0:4, 0:4]),
            after=[th_tok, C.ident_tok, prev_ev], inc=True)
        ev = p.op("vector", lambda e, i=i: e.tensor_copy(
            out=t_fm.ap()[:, 4 * i:4 * (i + 1)], in_=PS[2].ap()[:, 0:4]),
            after=[tt], inc=True)
        prev_ev = ev
        tf_toks.append(ev)
    tfc = [t_fm.ap()[:, 4 * i:4 * (i + 1)] for i in range(NF)]
    def bc_evac(q, mtok):
        toks = []
        for half in range(2):
            reg = PS[0] if half == 0 else PS[1]
            c0 = 1024 * half
            toks.append(p.op("vector", lambda e, reg=reg, c0=c0:
                e.tensor_add(
                    out=t_tm.ap()[:, c0:c0 + 1024],
                    in0=reg.ap()[0:4, 0:1024],
                    in1=b_bc.ap()[:, c0:c0 + 1024]), after=[mtok], inc=True))
        return toks

    bc_toks = mm_tm(tfc, W["bc_w"], 2048, 2048, tf_toks[-1], "wA",
                    bc_evac, after_ps=[th_tok])
    bcv_toks = []
    prev_ev = None
    for i in range(NF):
        tt = p.op("tensor", lambda e, i=i: e.matmul(
            PS[2].ap()[0:128, 0:1], t_tm.ap()[0:4, 128 * i:128 * (i + 1)],
            selc.ap(), start=True, stop=True),
            after=bc_toks + [sm_tok, prev_ev], inc=True)
        ev = p.op("vector", lambda e, i=i: e.tensor_copy(
            out=bcv.ap()[:, i:i + 1], in_=PS[2].ap()[0:128, 0:1]),
            after=[tt], inc=True)
        prev_ev = ev
        bcv_toks.append(ev)
    bcv_tok = bcv_toks[-1]

    OUTb = [nc.alloc_sbuf_tensor(f"{name}_o{i}", [128, T], BF16)
            for i in range(NF)]
    gsl = wsl
    gate_bf = nc.alloc_sbuf_tensor(name + "_gatebf", [128, T], BF16)
    prev_guse = prev_wuse
    prev_sig = None
    out_toks = []
    for m in range(NF):
        s = m % 2
        dstg = gsl[s].ap().rearrange("p (k c) -> p k c", k=16)
        lt = p.dma("gpsimd", dstg,
                   W["gate_w"][:, 128 * m:128 * (m + 1)]
                   .rearrange("(k p) c -> p k c", p=128),
                   after=[prev_guse[s]], stream=f"wg{s}")
        last = None
        for k in range(NF):
            for n0 in range(0, T, 512):
                deps = [lt, y_toks[k]]
                if k == 0 and n0 == 0:
                    deps += [prev_sig, bcv_tok]
                last = p.op("tensor", lambda e, k=k, n0=n0, s=s: e.matmul(
                    PS[3].ap()[:, n0:n0 + 512],
                    gsl[s].ap()[:, 128 * k:128 * (k + 1)],
                    Y[k].ap()[:, n0:n0 + 512],
                    start=(k == 0), stop=(k == NF - 1)),
                    after=deps, inc=(k == NF - 1 and n0 + 512 >= T))
        prev_guse[s] = last
        tsig = p.op("scalar", lambda e, m=m: e.activation(
            out=gate_bf.ap()[:, :T], in_=PS[3].ap()[:, :T], func=AF.Sigmoid,
            bias=gb.ap()[:, m:m + 1]),
            after=[last, sm_tok0,
                   out_toks[-1] if out_toks else None], inc=True)
        prev_sig = tsig
        out_toks.append(p.op("vector", lambda e, m=m: e.scalar_tensor_tensor(
            out=OUTb[m].ap(), in0=gate_bf.ap()[:, :T],
            scalar=bcv.ap()[:, m:m + 1], in1=Y[m].ap(),
            op0=ALU.mult, op1=ALU.add), after=[tsig, bcv_tok], inc=True))

    ln_after = out_toks + [p.now("vector"), p.now("scalar")]
    lno = C.layernorm([OUTb[i].ap() for i in range(NF)],
                      [Y[i].ap() for i in range(NF)], eps=1e-6,
                      gamma=[outg.ap()[:, i:i + 1] for i in range(NF)],
                      beta=[outb.ap()[:, i:i + 1] for i in range(NF)],
                      after=ln_after)
    r_toks = []
    for i in range(NF):
        r_toks.append(p.op("vector", lambda e, i=i: e.scalar_tensor_tensor(
            out=R[i].ap(), in0=Y[i].ap(), scalar=0.5, in1=R[i].ap(),
            op0=ALU.mult, op1=ALU.add), after=[lno[i]], inc=True))
    return r_toks


def build_f0_launch(T=1024):
    nc = bass.Bass("TRN2", target_bir_lowering=False, debug=False)
    x_d = nc.dram_tensor("x", [2048, T], F32, kind="ExternalInput").ap()
    names = {
        "xlast": [128, 64], "sel": [4, 1],
        "agg_w": [2048, 2048], "d1_w": [2048, 8192], "d2_w": [8192, 2048],
        "bc_w": [2048, 2048], "gate_w": [2048, 2048],
        "agg_b4": [4, 2048], "d1b4": [4, 8192], "d2b4": [4, 2048],
        "bc_b4": [4, 2048], "pn_g4": [4, 2048], "pn_b4": [4, 2048],
        "in_g": [128, 16], "in_b": [128, 16], "gate_b": [128, 16],
        "out_g": [128, 16], "out_b": [128, 16],
    }
    W = {k: nc.dram_tensor(k, sh, F32, kind="ExternalInput").ap()
         for k, sh in names.items()}
    xo_d = nc.dram_tensor("xo", [2048, T], F32, kind="ExternalOutput").ap()
    p = Plan(nc)
    C = Common(nc, p, T)
    NF = 16
    R = [nc.alloc_sbuf_tensor(f"R{i}", [128, T], F32) for i in range(NF)]
    Y = [nc.alloc_sbuf_tensor(f"Yb{i}", [128, T], BF16) for i in range(NF)]
    xt = [p.dma("sync", R[i].ap(), x_d[128 * i:128 * (i + 1), :],
                stream="qx") for i in range(NF)]
    ln_toks = C.layernorm([R[i].ap() for i in range(NF)],
                          [Y[i].ap() for i in range(NF)],
                          eps=1e-5, after=xt)
    emit_f0(nc, p, C, Y, R, T, W, y_toks=ln_toks)
    for i in range(NF):
        p.dma("sync", xo_d[128 * i:128 * (i + 1), :], R[i].ap(),
              after=[p.now("vector")], stream="qout")
    p.build()
    return nc


def f0_inputs(kw, prefix, core):
    b = core // 2
    sel = np.zeros((4, 1), np.float32)
    sel[b, 0] = 1.0
    t4 = lambda v: np.tile(np.asarray(v, np.float32)[None, :], (4, 1))
    fmc = lambda v: np.ascontiguousarray(
        np.asarray(v, np.float32).reshape(16, 128).T)
    return {
        "sel": sel,
        "agg_w": _c(kw[prefix + "agg_w"]),
        "d1_w": _c(kw[prefix + "d1_w"]),
        "d2_w": _c(kw[prefix + "d2_w"]),
        "bc_w": _c(kw[prefix + "bc_w"]),
        "gate_w": _c(kw[prefix + "gate_w"]),
        "agg_b4": t4(kw[prefix + "agg_b"]),
        "d1b4": t4(kw[prefix + "d1_b"]),
        "d2b4": t4(kw[prefix + "d2_b"]),
        "bc_b4": t4(kw[prefix + "bc_b"]),
        "pn_g4": t4(kw[prefix + "pn_gamma"]),
        "pn_b4": t4(kw[prefix + "pn_beta"]),
        "in_g": fmc(kw[prefix + "in_gamma"]),
        "in_b": fmc(kw[prefix + "in_beta"]),
        "gate_b": fmc(kw[prefix + "gate_b"]),
        "out_g": fmc(kw[prefix + "out_gamma"]),
        "out_b": fmc(kw[prefix + "out_beta"]),
    }


def run_f0(x_shards, xlast, kw, prefix="L0_ffn_"):
    nc = build_f0_launch()
    in_maps = []
    for c in range(8):
        m = f0_inputs(kw, prefix, c)
        m["x"] = x_shards[c]
        m["xlast"] = xlast
        in_maps.append(m)
    res = run_bass_kernel_spmd(nc, in_maps, core_ids=list(range(8)))
    return [res.results[c]["xo"] for c in range(8)]




def build_a0_launch():
    nc = bass.Bass("TRN2", target_bir_lowering=False, debug=False)
    x_d = nc.dram_tensor("x", [2048, 1027], F32, kind="ExternalInput").ap()
    names = {
        "w_inT": [2048, 8512], "w_outT": [4096, 2048],
        "conv_w": [128, 136], "conv_b": [128, 34],
        "dt_bias": [64, 1], "A_log": [64, 1],
        "D_fm": [128, 32], "norm_w": [128, 32],
    }
    W = {k: nc.dram_tensor(k, sh, F32, kind="ExternalInput").ap()
         for k, sh in names.items()}
    xo_d = nc.dram_tensor("xo", [2048, 1024], F32, kind="ExternalOutput").ap()
    lt_np = (np.arange(256)[:, None] <= np.arange(256)[None, :]
             ).astype(np.float32)
    ltc_d = [nc.inline_tensor(np.ascontiguousarray(
        lt_np[128 * s:128 * (s + 1)]), name=f"ltc{s}") for s in range(2)]
    mk = []
    for sc in range(2):
        m = (np.arange(256)[None, :] >= (np.arange(128)[:, None] + 128 * sc)
             ).astype(BF)
        mk.append(nc.inline_tensor(np.ascontiguousarray(m), name=f"mkc{sc}"))

    p = Plan(nc)
    C = Common(nc, p, 512, ps_shapes=(1024, 512, 512))
    psA, psB, psC = C.PS[:3]
    C.ln_ps = (psA, psA, 512, psB, psC)
    PSD2 = nc.alloc_psum_tensor("a_psd2", [128, 512], F32)
    PSY = nc.alloc_psum_tensor("a_psy", [128, 1024], F32)
    PSTB = nc.alloc_psum_tensor("a_pstb", [128, 512], BF16)

    NF = 16
    R = [nc.alloc_sbuf_tensor(f"R{i}", [128, 1027], F32) for i in range(NF)]
    Yt = [nc.alloc_sbuf_tensor(f"Yt{i}", [128, 259], BF16) for i in range(NF)]
    hid = [nc.alloc_sbuf_tensor(f"a_hid{i}", [128, 259], BF16)
           for i in range(32)]
    gate = [nc.alloc_sbuf_tensor(f"a_gate{i}", [128, 256], BF16)
            for i in range(32)]
    Bt = nc.alloc_sbuf_tensor("a_Bt", [128, 259], BF16)
    Ct = nc.alloc_sbuf_tensor("a_Ct", [128, 259], BF16)
    dt_raw = nc.alloc_sbuf_tensor("a_dtraw", [64, 256], F32)
    dt_t1 = nc.alloc_sbuf_tensor("a_dtt1", [64, 256], F32)
    dt_t2 = nc.alloc_sbuf_tensor("a_dtt2", [64, 256], F32)
    dtv = nc.alloc_sbuf_tensor("a_dtv", [64, 256], F32)
    At = nc.alloc_sbuf_tensor("a_At", [64, 256], F32)
    acc = nc.alloc_sbuf_tensor("a_acc", [128, 256], F32)
    cw = nc.alloc_sbuf_tensor("a_cw", [128, 136], F32)
    cb = nc.alloc_sbuf_tensor("a_cb", [128, 34], F32)
    dtb = nc.alloc_sbuf_tensor("a_dtb", [64, 1], F32)
    acol = nc.alloc_sbuf_tensor("a_acol", [64, 1], F32)
    dfm = nc.alloc_sbuf_tensor("a_dfm", [128, 32], F32)
    nw = nc.alloc_sbuf_tensor("a_nw", [128, 32], F32)
    ltc = [nc.alloc_sbuf_tensor(f"a_ltc{s}", [128, 256], F32)
           for s in range(2)]
    mkc = [nc.alloc_sbuf_tensor(f"a_mkc{s}", [128, 256], BF16)
           for s in range(2)]
    atm = nc.alloc_sbuf_tensor("a_atm", [128, 128], F32)
    acum = nc.alloc_sbuf_tensor("a_acum", [64, 256], F32)
    nacumT = nc.alloc_sbuf_tensor("a_nacumT", [128, 128], F32)
    dkf = nc.alloc_sbuf_tensor("a_dkf", [64, 256], F32)
    sdf = nc.alloc_sbuf_tensor("a_sdf", [64, 256], F32)
    dkT = nc.alloc_sbuf_tensor("a_dkT", [128, 128], F32)
    sdT = nc.alloc_sbuf_tensor("a_sdT", [128, 128], F32)
    dtvT = nc.alloc_sbuf_tensor("a_dtvT", [128, 128], F32)
    hdtT = [nc.alloc_sbuf_tensor(f"a_hdtT{s}", [128, 4096], BF16)
            for s in range(2)]
    btm = nc.alloc_sbuf_tensor("a_btm", [128, 256], BF16)
    bdtm = nc.alloc_sbuf_tensor("a_bdtm", [128, 256], BF16)
    gmm = [nc.alloc_sbuf_tensor(f"a_gmm{s}", [128, 256], BF16)
           for s in range(2)]
    d2sb = nc.alloc_sbuf_tensor("a_d2sb", [128, 512], F32)
    mtsb = nc.alloc_sbuf_tensor("a_mtsb", [128, 512], BF16)
    pmt = [nc.alloc_sbuf_tensor(f"a_pmt{s}", [128, 512], BF16)
           for s in range(2)]
    stsb = nc.alloc_sbuf_tensor("a_stsb", [128, 64], BF16)
    yosb = nc.alloc_sbuf_tensor("a_yosb", [128, 128], F32)
    ysb = nc.alloc_sbuf_tensor("a_ysb", [128, 32 * 256], BF16)
    wsl = [nc.alloc_sbuf_tensor(f"a_wsl{i}", [128, 2048], BF16)
           for i in range(2)]
    wo = [nc.alloc_sbuf_tensor(f"a_wo{i}", [128, 4096], BF16)
          for i in range(2)]

    xt = [p.dma("sync", R[i].ap(), x_d[128 * i:128 * (i + 1), :],
                stream="qx") for i in range(NF)]
    sm = []
    for dst, key in ((cw, "conv_w"), (cb, "conv_b"), (dtb, "dt_bias"),
                     (acol, "A_log"), (dfm, "D_fm"), (nw, "norm_w")):
        sm.append(p.dma("sync", dst.ap(), W[key], stream="qsm"))
    for s in range(2):
        sm.append(p.dma("sync", ltc[s].ap(), ltc_d[s].ap(), stream="qsm"))
        sm.append(p.dma("sync", mkc[s].ap(), mk[s].ap(), stream="qsm"))
    sm_tok = sm[-1]
    ta = p.op("scalar", lambda e: e.activation(
        out=acol.ap(), in_=acol.ap(), func=AF.Exp), after=[sm_tok], inc=True)
    acol_tok = p.op("vector", lambda e: e.tensor_scalar_mul(
        out=acol.ap(), in0=acol.ap(), scalar1=-1.0), after=[ta], inc=True)

    prev_wuse = [None, None]
    prev_wouse = [None, None]
    r_upd = [Tok("qx", p.qcnt["qx"])] * NF
    YH = nc.alloc_sbuf_tensor("a_YH", [128, 48], BF16)
    yh_toks = C.layernorm(
        [R[i].ap()[:, 0:3] for i in range(NF)],
        [YH.ap()[:, 3 * i:3 * (i + 1)] for i in range(NF)], T=3,
        eps=1e-5, after=xt + [acol_tok])

    for cix in range(4):
        col0 = 3 + 256 * cix
        ln_after = r_upd + [p.now("tensor"), p.now("scalar")]
        if cix == 0:
            ln_after += yh_toks
        yt_toks = C.layernorm(
            [R[i].ap()[:, col0:col0 + 256] for i in range(NF)],
            [Yt[i].ap()[:, 3:259] for i in range(NF)], T=256, eps=1e-5,
            after=ln_after)
        nyt = []
        for i in range(NF):
            c1 = p.op("vector", lambda e, i=i: e.tensor_copy(
                out=Yt[i].ap()[:, 0:3], in_=YH.ap()[:, 3 * i:3 * (i + 1)]),
                after=[yt_toks[i]], inc=True)
            c2 = p.op("vector", lambda e, i=i: e.tensor_copy(
                out=YH.ap()[:, 3 * i:3 * (i + 1)],
                in_=Yt[i].ap()[:, 256:259]), after=[c1], inc=True)
            nyt.append(c2)
        yt_toks = nyt
        ev_toks = {}
        prev_ev = None
        for m in range(67):
            mw = 128 if m < 66 else 64
            s = m % 2
            dstw = wsl[s].ap()[:, 0:16 * mw].rearrange(
                "p (k c) -> p k c", k=16)
            lt = p.dma("gpsimd", dstw,
                       W["w_inT"][:, 128 * m:128 * m + mw]
                       .rearrange("(k p) c -> p k c", p=128),
                       after=[prev_wuse[s]], stream=f"wi{s}")
            last = None
            for k in range(16):
                deps = [lt, yt_toks[k]]
                if k == 0:
                    deps.append(prev_ev)
                    if m == 0:
                        deps.append(yt_toks[-1])
                last = p.op("tensor", lambda e, k=k, s=s, mw=mw: e.matmul(
                    psC.ap()[0:mw, 0:259],
                    wsl[s].ap()[:, mw * k:mw * (k + 1)],
                    Yt[k].ap()[:, 0:259],
                    start=(k == 0), stop=(k == 15)), after=deps,
                    inc=(k == 15))
            prev_wuse[s] = last
            if m < 32:
                ev = p.op("vector", lambda e, m=m: e.tensor_copy(
                    out=gate[m].ap(), in_=psC.ap()[:, 3:259]),
                    after=[last], inc=True)
            elif m < 64:
                ev = p.op("vector", lambda e, m=m: e.tensor_copy(
                    out=hid[m - 32].ap(), in_=psC.ap()[:, 0:259]),
                    after=[last], inc=True)
            elif m == 64:
                ev = p.op("vector", lambda e: e.tensor_copy(
                    out=Bt.ap(), in_=psC.ap()[:, 0:259]), after=[last],
                    inc=True)
            elif m == 65:
                ev = p.op("vector", lambda e: e.tensor_copy(
                    out=Ct.ap(), in_=psC.ap()[:, 0:259]), after=[last],
                    inc=True)
            else:
                ev = p.op("vector", lambda e: e.tensor_copy(
                    out=dt_raw.ap(), in_=psC.ap()[0:64, 3:259]),
                    after=[last], inc=True)
            ev_toks[m] = ev
            prev_ev = ev
        conv_toks = {}
        prev_silu = None
        for f in range(34):
            srcap = hid[f] if f < 32 else (Bt if f == 32 else Ct)
            t0 = p.op("vector", lambda e, f=f, srcap=srcap: e.tensor_scalar(
                out=acc.ap(), in0=srcap.ap()[:, 0:256],
                scalar1=cw.ap()[:, 4 * f:4 * f + 1],
                scalar2=cb.ap()[:, f:f + 1], op0=ALU.mult, op1=ALU.add),
                after=[ev_toks[f + 32], sm_tok, prev_silu], inc=True)
            for i in (1, 2, 3):
                t0 = p.op("vector", lambda e, f=f, i=i, srcap=srcap:
                    e.scalar_tensor_tensor(
                        out=acc.ap(), in0=srcap.ap()[:, i:256 + i],
                        scalar=cw.ap()[:, 4 * f + i:4 * f + i + 1],
                        in1=acc.ap(), op0=ALU.mult, op1=ALU.add),
                    after=[t0], inc=True)
            conv_toks[f] = p.op("scalar", lambda e, srcap=srcap: e.activation(
                out=srcap.ap()[:, 0:256], in_=acc.ap(), func=AF.Silu),
                after=[t0], inc=True)
            prev_silu = conv_toks[f]
        d1 = p.op("vector", lambda e: e.tensor_scalar_add(
            out=dt_t1.ap(), in0=dt_raw.ap(), scalar1=dtb.ap()),
            after=[ev_toks[66], sm_tok], inc=True)
        d2_ = p.op("vector", lambda e: e.tensor_scalar_max(
            out=dt_t2.ap(), in0=dt_t1.ap(), scalar1=0.0), after=[d1],
            inc=True)
        d3 = p.op("scalar", lambda e: e.activation(
            out=dt_t1.ap(), in_=dt_t1.ap(), func=AF.Abs), after=[d1, d2_],
            inc=True)
        d4 = p.op("scalar", lambda e: e.activation(
            out=dt_t1.ap(), in_=dt_t1.ap(), func=AF.Exp, scale=-1.0),
            after=[d3], inc=True)
        d5 = p.op("scalar", lambda e: e.activation(
            out=dt_t1.ap(), in_=dt_t1.ap(), func=AF.Ln, bias=1.0),
            after=[d4], inc=True)
        d6 = p.op("vector", lambda e: e.tensor_add(
            out=dtv.ap(), in0=dt_t2.ap(), in1=dt_t1.ap()),
            after=[d2_, d5], inc=True)
        at_tok = p.op("vector", lambda e: e.tensor_scalar_mul(
            out=At.ap(), in0=dtv.ap(), scalar1=acol.ap()),
            after=[d6, acol_tok], inc=True)
        atm_toks = []
        prev_t = Tok("evector", p.ecnt["vector"])
        for sc in range(2):
            tt = p.op("tensor", lambda e, sc=sc: e.transpose(
                psB.ap()[:, 0:64], At.ap()[0:64, 128 * sc:128 * (sc + 1)],
                C.ident_f.ap()[0:64, 0:64]),
                after=[at_tok, C.ident_tok, prev_t], inc=True)
            ev = p.op("vector", lambda e, sc=sc: e.tensor_copy(
                out=atm.ap()[:, 64 * sc:64 * (sc + 1)],
                in_=psB.ap()[:, 0:64]), after=[tt], inc=True)
            prev_t = ev
            atm_toks.append(ev)
        mm = None
        for sc in range(2):
            mm = p.op("tensor", lambda e, sc=sc: e.matmul(
                psA.ap()[0:64, 0:256], atm.ap()[:, 64 * sc:64 * (sc + 1)],
                ltc[sc].ap(), start=(sc == 0), stop=(sc == 1)),
                after=[atm_toks[sc], sm_tok, p.now("vector")], inc=(sc == 1))
        acum_tok = p.op("vector", lambda e: e.tensor_copy(
            out=acum.ap(), in_=psA.ap()[0:64, 0:256]), after=[mm], inc=True)
        nac_toks = []
        for lc in range(2):
            mm2 = None
            for sc in range(2):
                mm2 = p.op("tensor", lambda e, sc=sc, lc=lc: e.matmul(
                    psB.ap()[:, 0:64],
                    ltc[sc].ap()[:, 128 * lc:128 * (lc + 1)],
                    atm.ap()[:, 64 * sc:64 * (sc + 1)],
                    start=(sc == 0), stop=(sc == 1)),
                    after=[atm_toks[sc], prev_t], inc=(sc == 1))
            ev = p.op("vector", lambda e, lc=lc: e.tensor_scalar_mul(
                out=nacumT.ap()[:, 64 * lc:64 * (lc + 1)],
                in0=psB.ap()[:, 0:64], scalar1=-1.0), after=[mm2], inc=True)
            prev_t = ev
            nac_toks.append(ev)
        dk = p.op("scalar", lambda e: e.activation(
            out=dkf.ap(), in_=acum.ap(), func=AF.Exp, scale=-1.0,
            bias=acum.ap()[:, 255:256]), after=[acum_tok], inc=True)
        sd = p.op("scalar", lambda e: e.activation(
            out=sdf.ap(), in_=acum.ap(), func=AF.Exp), after=[acum_tok],
            inc=True)
        dkT_toks, sdT_toks, dtvT_toks = [], [], []
        for sc in range(2):
            tt = p.op("tensor", lambda e, sc=sc: e.transpose(
                psB.ap()[:, 0:64], dkf.ap()[0:64, 128 * sc:128 * (sc + 1)],
                C.ident_f.ap()[0:64, 0:64]), after=[dk, prev_t], inc=True)
            ev = p.op("vector", lambda e, sc=sc: e.tensor_copy(
                out=dkT.ap()[:, 64 * sc:64 * (sc + 1)],
                in_=psB.ap()[:, 0:64]), after=[tt], inc=True)
            prev_t = ev
            dkT_toks.append(ev)
            tt = p.op("tensor", lambda e, sc=sc: e.transpose(
                psB.ap()[:, 0:64], sdf.ap()[0:64, 128 * sc:128 * (sc + 1)],
                C.ident_f.ap()[0:64, 0:64]), after=[sd, prev_t], inc=True)
            ev = p.op("vector", lambda e, sc=sc: e.tensor_copy(
                out=sdT.ap()[:, 64 * sc:64 * (sc + 1)],
                in_=psB.ap()[:, 0:64]), after=[tt], inc=True)
            prev_t = ev
            sdT_toks.append(ev)
            tt = p.op("tensor", lambda e, sc=sc: e.transpose(
                psB.ap()[:, 0:64], dtv.ap()[0:64, 128 * sc:128 * (sc + 1)],
                C.ident_f.ap()[0:64, 0:64]), after=[d6, prev_t], inc=True)
            ev = p.op("vector", lambda e, sc=sc: e.tensor_copy(
                out=dtvT.ap()[:, 64 * sc:64 * (sc + 1)],
                in_=psB.ap()[:, 0:64]), after=[tt], inc=True)
            prev_t = ev
            dtvT_toks.append(ev)
        dtvT_tok = dtvT_toks[-1]
        hdt_toks = {}
        for f in range(32):
            for sc in range(2):
                tt = p.op("tensor", lambda e, f=f, sc=sc: e.transpose(
                    PSTB.ap()[:, 0:128],
                    hid[f].ap()[:, 128 * sc:128 * (sc + 1)],
                    C.ident_b.ap()),
                    after=[conv_toks[f], prev_t, C.ident_tok], inc=True)
                evs = []
                for hh in range(2):
                    h = 2 * f + hh
                    evs.append(p.op("vector",
                        lambda e, f=f, sc=sc, hh=hh, h=h:
                        e.tensor_scalar_mul(
                            out=hdtT[sc].ap()[:, 128 * f + 64 * hh:
                                              128 * f + 64 * (hh + 1)],
                            in0=PSTB.ap()[:, 64 * hh:64 * (hh + 1)],
                            scalar1=dtvT.ap()[:, 64 * sc + h:
                                              64 * sc + h + 1]
                            if h < 64 else None),
                        after=[tt, dtvT_tok], inc=True))
                prev_t = evs[-1]
                hdt_toks[(f, sc)] = evs
        btm_toks = []
        for sc in range(2):
            tt = p.op("tensor", lambda e, sc=sc: e.transpose(
                PSTB.ap()[:, 128:256], Bt.ap()[:, 128 * sc:128 * (sc + 1)],
                C.ident_b.ap()), after=[conv_toks[32], prev_t, C.ident_tok],
                inc=True)
            ev = p.op("vector", lambda e, sc=sc: e.tensor_copy(
                out=btm.ap()[:, 128 * sc:128 * (sc + 1)],
                in_=PSTB.ap()[:, 128:256]), after=[tt], inc=True)
            prev_t = ev
            btm_toks.append(ev)
        gm_toks = []
        prev_gm = p.now("vector")
        for sc in range(2):
            mmg = p.op("tensor", lambda e, sc=sc: e.matmul(
                psB.ap()[0:128, 0:256], Bt.ap()[:, 128 * sc:128 * (sc + 1)],
                Ct.ap()[:, 0:256], start=True, stop=True),
                after=[conv_toks[32], conv_toks[33], prev_gm], inc=True)
            ev = p.op("vector", lambda e, sc=sc: e.tensor_mul(
                out=gmm[sc].ap(), in0=psB.ap()[0:128, 0:256],
                in1=mkc[sc].ap()), after=[mmg, sm_tok], inc=True)
            prev_gm = ev
            gm_toks.append(ev)
        y_toks = {}
        prev_grp_readers = [p.now("vector"), p.now("scalar")]
        prev_yd = None
        prev_st = None
        prev_yo = None
        for grp in range(32):
            h0 = 2 * grp
            mmd = None
            for j in range(2):
                h = h0 + j
                mmd = p.op("tensor", lambda e, h=h, j=j: e.matmul(
                    PSD2.ap()[:, 256 * j:256 * (j + 1)],
                    C.ident_f.ap()[0:64, h:h + 1].broadcast_to([64, 128]),
                    acum.ap(), start=True, stop=True),
                    after=[acum_tok, C.ident_tok] + prev_grp_readers,
                    inc=True)
            pmts = []
            for sc in range(2):
                bm_toks = []
                for j in range(2):
                    h = h0 + j
                    bm_toks.append(p.op("vector",
                        lambda e, sc=sc, j=j, h=h: e.tensor_scalar(
                            out=d2sb.ap()[:, 256 * j:256 * (j + 1)],
                            in0=PSD2.ap()[:, 256 * j:256 * (j + 1)],
                            scalar1=nacumT.ap()[:, 64 * sc + h:
                                                64 * sc + h + 1]
                            if h < 64 else None,
                            scalar2=0.0, op0=ALU.add, op1=ALU.min),
                        after=[mmd, nac_toks[sc]], inc=True))
                ex = p.op("scalar", lambda e: e.activation(
                    out=mtsb.ap(), in_=d2sb.ap(), func=AF.Exp),
                    after=bm_toks, inc=True)
                pm = p.op("vector", lambda e, sc=sc: e.tensor_mul(
                    out=pmt[sc].ap(), in0=mtsb.ap(),
                    in1=gmm[sc].ap()[:, None, :].broadcast_to([128, 2, 256])),
                    after=[ex, gm_toks[sc]], inc=True)
                pmts.append(pm)
            for j in range(2):
                h = h0 + j
                f, hh = h // 2, h % 2
                for sc in range(2):
                    p.op("tensor", lambda e, sc=sc, f=f, hh=hh, j=j:
                        e.matmul(
                            PSY.ap()[0:64, 0:256],
                            hdtT[sc].ap()[:, 128 * f + 64 * hh:
                                          128 * f + 64 * (hh + 1)],
                            pmt[sc].ap()[:, 256 * j:256 * (j + 1)],
                            start=(sc == 0), stop=False),
                        after=[pmts[sc], hdt_toks[(f, sc)][hh], prev_yd],
                        inc=False)
                bd_toks = []
                for sc in range(2):
                    bd_toks.append(p.op("vector",
                        lambda e, sc=sc, h=h: e.tensor_scalar_mul(
                            out=bdtm.ap()[:, 128 * sc:128 * (sc + 1)],
                            in0=btm.ap()[:, 128 * sc:128 * (sc + 1)],
                            scalar1=dkT.ap()[:, 64 * sc + h:64 * sc + h + 1]
                            if h < 64 else None),
                        after=[btm_toks[sc], dkT_toks[sc], prev_st,
                               prev_yd], inc=True))
                mms = None
                for sc in range(2):
                    mms = p.op("tensor", lambda e, sc=sc, f=f, hh=hh:
                        e.matmul(
                            PSY.ap()[0:128, 512:576],
                            bdtm.ap()[:, 128 * sc:128 * (sc + 1)],
                            hdtT[sc].ap()[:, 128 * f + 64 * hh:
                                          128 * f + 64 * (hh + 1)],
                            start=(sc == 0), stop=(sc == 1)),
                        after=[bd_toks[sc], hdt_toks[(f, sc)][hh]],
                        inc=(sc == 1))
                st_ev = p.op("vector", lambda e: e.tensor_copy(
                    out=stsb.ap(), in_=PSY.ap()[0:128, 512:576]),
                    after=[mms], inc=True)
                prev_st = st_ev
                yo_evs = []
                for lc in range(2):
                    mmo = p.op("tensor", lambda e, lc=lc: e.matmul(
                        PSY.ap()[0:128, 576:640],
                        Ct.ap()[:, 128 * lc:128 * (lc + 1)],
                        stsb.ap(), start=True, stop=True),
                        after=[st_ev, conv_toks[33], prev_yo], inc=True)
                    yo_evs.append(p.op("vector",
                        lambda e, lc=lc, h=h: e.tensor_scalar_mul(
                            out=yosb.ap()[:, 64 * lc:64 * (lc + 1)],
                            in0=PSY.ap()[0:128, 576:640],
                            scalar1=sdT.ap()[:, 64 * lc + h:64 * lc + h + 1]
                            if h < 64 else None),
                        after=[mmo, sdT_toks[lc]], inc=True))
                    prev_yo = yo_evs[-1]
                for lc in range(2):
                    p.op("tensor", lambda e, lc=lc: e.matmul(
                        PSY.ap()[0:64, 128 * lc:128 * (lc + 1)],
                        yosb.ap()[:, 64 * lc:64 * (lc + 1)],
                        C.ident_f.ap(), start=False, stop=(lc == 1),
                        is_transpose=True),
                        after=[yo_evs[lc], C.ident_tok], inc=(lc == 1))
                yev = p.op("vector", lambda e, f=f, hh=hh:
                    e.tensor_copy(
                        out=ysb.ap()[64 * hh:64 * (hh + 1),
                                     256 * f:256 * (f + 1)],
                        in_=PSY.ap()[0:64, 0:256]),
                    after=[p.now("tensor")], inc=True)
                prev_yd = yev
                y_toks[h] = yev
            prev_grp_readers = [p.now("vector"), p.now("scalar")]
        yf_toks = []
        for f in range(32):
            sil_tok = p.op("scalar", lambda e, f=f: e.activation(
                out=gate[f].ap(), in_=gate[f].ap(), func=AF.Silu),
                after=[ev_toks[f], y_toks[2 * f + 1]], inc=True)
            t1_ = p.op("vector", lambda e, f=f: e.scalar_tensor_tensor(
                out=ysb.ap()[:, 256 * f:256 * (f + 1)],
                in0=hid[f].ap()[:, 0:256], scalar=dfm.ap()[:, f:f + 1],
                in1=ysb.ap()[:, 256 * f:256 * (f + 1)],
                op0=ALU.mult, op1=ALU.add),
                after=[conv_toks[f], y_toks[2 * f], y_toks[2 * f + 1],
                       sm_tok], inc=True)
            yf_toks.append(p.op("vector", lambda e, f=f: e.tensor_mul(
                out=ysb.ap()[:, 256 * f:256 * (f + 1)],
                in0=ysb.ap()[:, 256 * f:256 * (f + 1)],
                in1=gate[f].ap()), after=[t1_, sil_tok], inc=True))
        sqs = None
        for f in range(32):
            tq = p.op("scalar", lambda e, f=f: e.activation(
                out=C.sq.ap()[:, 0:256],
                in_=ysb.ap()[:, 256 * f:256 * (f + 1)],
                func=AF.Square), after=[yf_toks[f], sqs], inc=True)
            sqs = p.op("tensor", lambda e, f=f: e.matmul(
                psA.ap()[0:1, 512:768], C.onescol.ap(), C.sq.ap()[:, 0:256],
                start=(f == 0), stop=(f == 31)),
                after=[tq, C.const_tok,
                       acum_tok if f == 0 else None], inc=True)
        v1 = p.op("scalar", lambda e: e.activation(
            out=C.r_v.ap()[0:1, 0:256], in_=psA.ap()[0:1, 512:768],
            func=AF.Ln, bias=C.constf(1e-5 * 4096), scale=1.0),
            after=[sqs], inc=True)
        v2 = p.op("scalar", lambda e: e.activation(
            out=C.r_v.ap()[0:1, 0:256], in_=C.r_v.ap()[0:1, 0:256],
            func=AF.Exp, scale=-0.5), after=[v1], inc=True)
        v3 = p.op("vector", lambda e: e.tensor_scalar_mul(
            out=C.r_v.ap()[0:1, 0:256], in0=C.r_v.ap()[0:1, 0:256],
            scalar1=64.0), after=[v2], inc=True)
        bmv = p.op("tensor", lambda e: e.matmul(
            psB.ap()[:, 0:256], C.onesrow_f.ap(), C.r_v.ap()[0:1, 0:256],
            start=True, stop=True), after=[v3, p.now("vector")], inc=True)
        yn_toks = []
        for f in range(32):
            yn_toks.append(p.op("vector", lambda e, f=f:
                e.scalar_tensor_tensor(
                    out=ysb.ap()[:, 256 * f:256 * (f + 1)],
                    in0=ysb.ap()[:, 256 * f:256 * (f + 1)],
                    scalar=nw.ap()[:, f:f + 1], in1=psB.ap()[:, 0:256],
                    op0=ALU.mult, op1=ALU.mult),
                after=[bmv, yf_toks[f], sm_tok], inc=True))
        prev_rupd = None
        for m in range(16):
            s = m % 2
            dstw = wo[s].ap().rearrange("p (k c) -> p k c", k=32)
            lt = p.dma("gpsimd", dstw,
                       W["w_outT"][:, 128 * m:128 * (m + 1)]
                       .rearrange("(k p) c -> p k c", p=128),
                       after=[prev_wouse[s]], stream=f"wo{s}")
            last = None
            for k in range(32):
                deps = [lt, yn_toks[k]]
                if k == 0:
                    deps.append(r_upd[m])
                    deps.append(prev_rupd)
                    if m < 2:
                        deps.append(ev_toks[66])
                last = p.op("tensor", lambda e, k=k, s=s: e.matmul(
                    psC.ap()[:, 0:256], wo[s].ap()[:, 128 * k:128 * (k + 1)],
                    ysb.ap()[:, 256 * k:256 * (k + 1)],
                    start=(k == 0), stop=(k == 31)), after=deps,
                    inc=(k == 31))
            prev_wouse[s] = last
            r_upd[m] = p.op("vector", lambda e, m=m, cix=cix: e.tensor_add(
                out=R[m].ap()[:, 3 + 256 * cix:3 + 256 * (cix + 1)],
                in0=psC.ap()[:, 0:256],
                in1=R[m].ap()[:, 3 + 256 * cix:3 + 256 * (cix + 1)]),
                after=[last], inc=True)
            prev_rupd = r_upd[m]

    for i in range(NF):
        p.dma("sync", xo_d[128 * i:128 * (i + 1), :], R[i].ap()[:, 3:1027],
              after=[r_upd[i]], stream="qout")
    p.build()
    return nc




def build_a1a_launch():
    nc = bass.Bass("TRN2", target_bir_lowering=False, debug=False)
    x_d = nc.dram_tensor("x", [2048, 2048], F32, kind="ExternalInput").ap()
    wq_d = nc.dram_tensor("wqT", [2048, 1024], F32, kind="ExternalInput").ap()
    wk_d = nc.dram_tensor("wkT", [2048, 1024], F32, kind="ExternalInput").ap()
    wv_d = nc.dram_tensor("wvT", [2048, 1024], F32, kind="ExternalInput").ap()
    sl_d = nc.dram_tensor("slopes8", [1, 8], F32, kind="ExternalInput").ap()
    oraw_d = nc.dram_tensor("oraw", [1024, 2048], BF16,
                            kind="ExternalOutput").ap()
    sqp_d = nc.dram_tensor("sqp", [1, 2048], F32, kind="ExternalOutput").ap()
    ar0 = nc.inline_tensor(np.arange(256, dtype=np.float32)[None, :],
                           name="ar0")
    ar1 = nc.inline_tensor((np.arange(256, dtype=np.float32) + 1)[None, :],
                           name="ar1")
    onesrow256 = nc.inline_tensor(np.ones((1, 256), np.float32), name="o256")
    mknp = []
    for sc in range(2):
        m = (np.arange(256)[None, :] >= (np.arange(128)[:, None] + 128 * sc)
             ).astype(BF)
        mknp.append(nc.inline_tensor(np.ascontiguousarray(m),
                                     name=f"a1mk{sc}"))

    p = Plan(nc)
    C = Common(nc, p, 256, ps_shapes=(512, 512, 512, 512))
    PS = C.PS
    PSTB = nc.alloc_psum_tensor("b_pstb", [128, 512], BF16)

    NF = 16
    Y = [nc.alloc_sbuf_tensor(f"Y{i}", [128, 2048], BF16) for i in range(NF)]
    xblk = [nc.alloc_sbuf_tensor(f"b_x{i}", [128, 256], BF16)
            for i in range(NF)]
    qf = [nc.alloc_sbuf_tensor(f"b_q{i}", [128, 2048], BF16)
          for i in range(8)]
    kf = [nc.alloc_sbuf_tensor(f"b_k{i}", [128, 2048], BF16)
          for i in range(8)]
    vt = [nc.alloc_sbuf_tensor(f"b_v{i}", [128, 1024], BF16)
          for i in range(16)]
    OUT = Y[0:8]
    wsl = [nc.alloc_sbuf_tensor(f"b_wsl{i}", [128, 2048], BF16)
           for i in range(2)]
    dd = [nc.alloc_sbuf_tensor(f"b_dd{i}", [128, 8 * 256], BF16)
          for i in range(2)]
    qd = nc.alloc_sbuf_tensor("b_qd", [128, 8 * 256], BF16)
    kdT = nc.alloc_sbuf_tensor("b_kdT", [128, 16], F32)
    bdc = nc.alloc_sbuf_tensor("b_bdc", [128, 8], F32)
    s8 = nc.alloc_sbuf_tensor("b_s8", [1, 8], F32)
    srow = nc.alloc_sbuf_tensor("b_srow", [1, 256], F32)
    nsrow = nc.alloc_sbuf_tensor("b_nsrow", [1, 256], F32)
    ar0s = nc.alloc_sbuf_tensor("b_ar0", [1, 256], F32)
    ar1s = nc.alloc_sbuf_tensor("b_ar1", [1, 256], F32)
    ones256 = nc.alloc_sbuf_tensor("b_o256", [1, 256], F32)
    l2 = nc.alloc_sbuf_tensor("b_l2", [2, 256], F32)
    r2 = nc.alloc_sbuf_tensor("b_r2", [2, 256], F32)
    mks = [nc.alloc_sbuf_tensor(f"b_mk{i}", [128, 256], BF16)
           for i in range(2)]
    ddscr = nc.alloc_sbuf_tensor("b_ddscr", [128, 256], F32)
    kv = nc.alloc_sbuf_tensor("b_kv", [128, 128], F32)
    kvb = nc.alloc_sbuf_tensor("b_kvb", [128, 128], BF16)
    qs_sb = nc.alloc_sbuf_tensor("b_qs", [128, 256], BF16)
    p_sb = nc.alloc_sbuf_tensor("b_p", [128, 512], BF16)
    kst = nc.alloc_sbuf_tensor("b_kst", [128, 256], BF16)
    sqp = nc.alloc_sbuf_tensor("b_sqp", [1, 512], F32)

    sm = [p.dma("sync", s8.ap(), sl_d, stream="qsm"),
          p.dma("sync", ar0s.ap(), ar0.ap(), stream="qsm"),
          p.dma("sync", ar1s.ap(), ar1.ap(), stream="qsm"),
          p.dma("sync", ones256.ap(), onesrow256.ap(), stream="qsm"),
          p.dma("sync", mks[0].ap(), mknp[0].ap(), stream="qsm"),
          p.dma("sync", mks[1].ap(), mknp[1].ap(), stream="qsm"),
          p.dma("sync", l2.ap()[1:2, :], onesrow256.ap(), stream="qsm"),
          p.dma("sync", r2.ap()[0:1, :], onesrow256.ap(), stream="qsm")]
    sm_tok = sm[-1]

    y_toks = []
    prev_blk = [None] * NF
    for blk in range(8):
        xt = [p.dma("gpsimd", xblk[i].ap(),
                    x_d[128 * i:128 * (i + 1), 256 * blk:256 * (blk + 1)],
                    after=[prev_blk[i]], stream=f"qx{i % 4}")
              for i in range(NF)]
        toks = C.layernorm(
            [xblk[i].ap() for i in range(NF)],
            [Y[i].ap()[:, 256 * blk:256 * (blk + 1)] for i in range(NF)],
            T=256, eps=1e-5, after=xt)
        for i in range(NF):
            prev_blk[i] = toks[i]
        y_toks.append(toks)
    yall = [y_toks[7][i] for i in range(NF)]

    stoks = []
    prev_ps3 = Tok("evector", p.ecnt["vector"])
    prev_mmr = None
    for h in range(8):
        t1 = p.op("vector", lambda e, h=h: e.tensor_scalar_mul(
            out=srow.ap(), in0=ar1s.ap(), scalar1=s8.ap()[0:1, h:h + 1]),
            after=[sm_tok], inc=True)
        t2 = p.op("scalar", lambda e: e.activation(
            out=srow.ap(), in_=srow.ap(), func=AF.Exp, scale=-1.0),
            after=[t1], inc=True)
        bm = p.op("tensor", lambda e: e.matmul(
            PS[3].ap()[:, 0:256], C.onesrow_f.ap(), srow.ap(),
            start=True, stop=True), after=[t2, prev_ps3, C.const_tok],
            inc=True)
        ev = p.op("vector", lambda e, h=h: e.tensor_copy(
            out=qd.ap()[:, 256 * h:256 * (h + 1)], in_=PS[3].ap()[:, 0:256]),
            after=[bm], inc=True)
        prev_ps3 = ev
        t3 = p.op("vector", lambda e, h=h: e.tensor_scalar(
            out=nsrow.ap(), in0=ar0s.ap(), scalar1=-1.0, scalar2=255.0,
            op0=ALU.mult, op1=ALU.add), after=[sm_tok], inc=True)
        t4 = p.op("vector", lambda e, h=h: e.tensor_scalar_mul(
            out=nsrow.ap(), in0=nsrow.ap(), scalar1=s8.ap()[0:1, h:h + 1]),
            after=[t3], inc=True)
        t5 = p.op("scalar", lambda e: e.activation(
            out=nsrow.ap(), in_=nsrow.ap(), func=AF.Exp, scale=-1.0),
            after=[t4], inc=True)
        for sc2 in range(2):
            tt = p.op("tensor", lambda e, sc2=sc2: e.transpose(
                PS[3].ap()[:, 256:257],
                nsrow.ap()[0:1, 128 * sc2:128 * (sc2 + 1)],
                C.ident_f.ap()[0:1, 0:1]),
                after=[t5, prev_ps3, C.ident_tok], inc=True)
            ev = p.op("vector", lambda e, h=h, sc2=sc2: e.tensor_copy(
                out=kdT.ap()[:, 2 * h + sc2:2 * h + sc2 + 1],
                in_=PS[3].ap()[:, 256:257]), after=[tt], inc=True)
            prev_ps3 = ev
        tn = p.op("vector", lambda e, h=h: e.tensor_scalar_mul(
            out=nsrow.ap(), in0=ar0s.ap(), scalar1=s8.ap()[0:1, h:h + 1]),
            after=[prev_ps3, prev_mmr], inc=True)
        tneg = p.op("vector", lambda e: e.tensor_scalar_mul(
            out=srow.ap(), in0=nsrow.ap(), scalar1=-1.0), after=[tn],
            inc=True)
        c1 = p.op("vector", lambda e: e.tensor_copy(
            out=l2.ap()[0:1, :], in_=nsrow.ap()), after=[tneg], inc=True)
        cdma = p.dma("sync", r2.ap()[1:2, :], srow.ap(),
                     after=[tneg, prev_mmr], stream="qr2")
        for sc2 in range(2):
            mmr = p.op("tensor", lambda e, sc2=sc2: e.matmul(
                PS[3].ap()[:, 0:256],
                l2.ap()[:, 128 * sc2:128 * (sc2 + 1)], r2.ap(),
                start=True, stop=True), after=[c1, cdma, sm_tok, prev_ps3],
                inc=True)
            prev_mmr = mmr
            tb = p.op("vector", lambda e: e.tensor_scalar_min(
                out=ddscr.ap(), in0=PS[3].ap()[:, 0:256], scalar1=0.0),
                after=[mmr], inc=True)
            prev_ps3 = tb
            te = p.op("scalar", lambda e: e.activation(
                out=ddscr.ap(), in_=ddscr.ap(), func=AF.Exp), after=[tb],
                inc=True)
            stoks.append(p.op("vector", lambda e, h=h, sc2=sc2: e.tensor_mul(
                out=dd[sc2].ap()[:, 256 * h:256 * (h + 1)],
                in0=ddscr.ap(), in1=mks[sc2].ap()), after=[te, sm_tok],
                inc=True))
    t1 = p.op("vector", lambda e: e.tensor_scalar_mul(
        out=srow.ap()[0:1, 0:8], in0=s8.ap(), scalar1=-256.0),
        after=[sm_tok, prev_mmr], inc=True)
    t2 = p.op("scalar", lambda e: e.activation(
        out=srow.ap()[0:1, 0:8], in_=srow.ap()[0:1, 0:8], func=AF.Exp),
        after=[t1], inc=True)
    bmv = p.op("tensor", lambda e: e.matmul(
        PS[3].ap()[:, 0:8], C.onesrow_f.ap(), srow.ap()[0:1, 0:8],
        start=True, stop=True), after=[t2, prev_ps3, C.const_tok], inc=True)
    bd_tok = p.op("vector", lambda e: e.tensor_copy(
        out=bdc.ap(), in_=PS[3].ap()[:, 0:8]), after=[bmv], inc=True)
    prev_ps3 = bd_tok
    dd_tok = stoks[-1]

    prev_w = [None, None]
    v_toks = [None] * 16
    wv_tok = []
    for k in range(16):
        qb, off = qf[k % 8], 1024 * (k // 8)
        wv_tok.append(p.dma("gpsimd",
                      qb.ap()[:, off:off + 1024],
                      wv_d[128 * k:128 * (k + 1), :], stream="qwv"))
    wv_all = wv_tok[-1]
    for t in range(16):
        last = None
        for k in range(16):
            qb, off = qf[k % 8], 1024 * (k // 8)
            for n0 in (0, 512):
                reg = PS[0] if n0 == 0 else PS[1]
                deps = [wv_all, yall[k]]
                if k == 0 and t > 0:
                    deps.append(v_toks[t - 1])
                last = p.op("tensor",
                    lambda e, k=k, n0=n0, t=t, qb=qb, off=off, reg=reg:
                    e.matmul(
                        reg.ap()[:, 0:512],
                        Y[k].ap()[:, 128 * t:128 * (t + 1)],
                        qb.ap()[:, off + n0:off + n0 + 512],
                        start=(k == 0), stop=(k == 15)),
                    after=deps, inc=(k == 15))
        e1 = p.op("scalar", lambda e, t=t: e.activation(
            out=vt[t].ap()[:, 0:512], in_=PS[0].ap()[:, 0:512],
            func=AF.Silu), after=[last], inc=True)
        v_toks[t] = p.op("scalar", lambda e, t=t: e.activation(
            out=vt[t].ap()[:, 512:1024], in_=PS[1].ap()[:, 0:512],
            func=AF.Silu), after=[e1], inc=True)
    v_all = v_toks[-1]

    q_toks, k_toks = [None] * 8, [None] * 8
    prev_ev = v_toks[15]
    for which, dst, wd, toks in (("q", qf, wq_d, q_toks),
                                 ("k", kf, wk_d, k_toks)):
        for m in range(8):
            s = m % 2
            dstw = wsl[s].ap().rearrange("p (k c) -> p k c", k=16)
            wdeps = [prev_w[s]]
            if which == "q":
                wdeps.append(v_all)
            lt = p.dma("gpsimd", dstw,
                       wd[:, 128 * m:128 * (m + 1)]
                       .rearrange("(k p) c -> p k c", p=128),
                       after=wdeps, stream=f"w{s}")
            for tg in range(4):
                last = None
                for k in range(16):
                    deps = [lt, yall[k]]
                    if k == 0:
                        deps.append(prev_ev)
                    last = p.op("tensor", lambda e, k=k, tg=tg, s=s:
                        e.matmul(
                            PS[0].ap()[:, 0:512],
                            wsl[s].ap()[:, 128 * k:128 * (k + 1)],
                            Y[k].ap()[:, 512 * tg:512 * (tg + 1)],
                            start=(k == 0), stop=(k == 15)),
                        after=deps, inc=(k == 15))
                prev_ev = p.op("scalar", lambda e, m=m, tg=tg, dst=dst:
                    e.activation(
                        out=dst[m].ap()[:, 512 * tg:512 * (tg + 1)],
                        in_=PS[0].ap()[:, 0:512], func=AF.Silu),
                    after=[last], inc=True)
            prev_w[s] = last
            toks[m] = prev_ev

    out_toks = {}
    prev_sc = k_toks[7]
    prev_o = v_toks[15]
    prev_kvps = None
    for h in range(8):
        kv_tok = None
        kvb_tok = None
        for blk in range(8):
            m0 = 256 * blk
            p_toks = []
            for nch in range(2):
                mm = p.op("tensor", lambda e, h=h, nch=nch, m0=m0: e.matmul(
                    PS[0].ap()[:, 0:256],
                    kf[h].ap()[:, m0 + 128 * nch:m0 + 128 * (nch + 1)],
                    qf[h].ap()[:, m0:m0 + 256],
                    start=True, stop=True),
                    after=[q_toks[h], k_toks[h], prev_sc], inc=True)
                pt = p.op("vector", lambda e, h=h, nch=nch: e.tensor_mul(
                    out=p_sb.ap()[:, 256 * nch:256 * (nch + 1)],
                    in0=PS[0].ap()[:, 0:256],
                    in1=dd[nch].ap()[:, 256 * h:256 * (h + 1)]),
                    after=[mm, dd_tok], inc=True)
                prev_sc = pt
                p_toks.append(pt)
            qs_tok = p.op("vector", lambda e, h=h, m0=m0: e.tensor_mul(
                out=qs_sb.ap(), in0=qf[h].ap()[:, m0:m0 + 256],
                in1=qd.ap()[:, 256 * h:256 * (h + 1)]),
                after=[q_toks[h], dd_tok, prev_o], inc=True)
            last = None
            for nch in range(2):
                tch = 2 * blk + nch
                last = p.op("tensor", lambda e, h=h, nch=nch, tch=tch,
                            blk=blk: e.matmul(
                        PS[1].ap()[:, 0:256],
                        vt[tch].ap()[:, 128 * h:128 * (h + 1)],
                        p_sb.ap()[:, 256 * nch:256 * (nch + 1)],
                        start=(nch == 0),
                        stop=(nch == 1 and blk == 0)),
                    after=[p_toks[nch], v_toks[tch], prev_o],
                    inc=(nch == 1 and blk == 0))
            if blk > 0:
                last = p.op("tensor", lambda e, h=h: e.matmul(
                    PS[1].ap()[:, 0:256], kvb.ap(), qs_sb.ap(),
                    start=False, stop=True),
                    after=[qs_tok, kvb_tok], inc=True)
            oev_deps = [last]
            if blk == 0:
                oev_deps += [q_toks[7], k_toks[7]]
            oev = p.op("vector", lambda e, h=h, m0=m0: e.tensor_copy(
                out=OUT[h].ap()[:, m0:m0 + 256], in_=PS[1].ap()[:, 0:256]),
                after=oev_deps, inc=True)
            prev_o = oev
            out_toks[(h, blk)] = oev
            if blk < 7:
                kst_toks = []
                for nch in range(2):
                    tt = p.op("tensor", lambda e, h=h, nch=nch, m0=m0:
                        e.transpose(
                            PSTB.ap()[:, 0:128],
                            kf[h].ap()[:, m0 + 128 * nch:
                                       m0 + 128 * (nch + 1)],
                            C.ident_b.ap()),
                        after=[k_toks[h], C.ident_tok,
                               kst_toks[-1] if kst_toks else prev_kvps],
                        inc=True)
                    kst_toks.append(p.op("vector",
                        lambda e, h=h, nch=nch: e.tensor_scalar_mul(
                            out=kst.ap()[:, 128 * nch:128 * (nch + 1)],
                            in0=PSTB.ap()[:, 0:128],
                            scalar1=kdT.ap()[:, 2 * h + nch:
                                             2 * h + nch + 1]),
                        after=[tt], inc=True))
                mmk = None
                for nch in range(2):
                    tch = 2 * blk + nch
                    mmk = p.op("tensor", lambda e, h=h, nch=nch, tch=tch:
                        e.matmul(
                            PS[2].ap()[:, 0:128],
                            kst.ap()[:, 128 * nch:128 * (nch + 1)],
                            vt[tch].ap()[:, 128 * h:128 * (h + 1)],
                            start=(nch == 0), stop=(nch == 1)),
                        after=[kst_toks[nch], v_toks[tch], prev_kvps],
                        inc=(nch == 1))
                if blk == 0:
                    kv_tok = p.op("vector", lambda e: e.tensor_copy(
                        out=kv.ap(), in_=PS[2].ap()[:, 0:128]),
                        after=[mmk], inc=True)
                else:
                    kv_tok = p.op("vector", lambda e, h=h:
                        e.scalar_tensor_tensor(
                            out=kv.ap(), in0=kv.ap(),
                            scalar=bdc.ap()[:, h:h + 1],
                            in1=PS[2].ap()[:, 0:128],
                            op0=ALU.mult, op1=ALU.add),
                        after=[mmk, bd_tok, kv_tok], inc=True)
                prev_kvps = kv_tok
                kvb_tok = p.op("vector", lambda e: e.tensor_copy(
                    out=kvb.ap(), in_=kv.ap()), after=[kv_tok], inc=True)

    sq_toks = []
    prev_mq = None
    for tg in range(4):
        last = None
        for h in range(8):
            tq = p.op("scalar", lambda e, h=h, tg=tg: e.activation(
                out=C.xb.ap()[:, 0:256],
                in_=OUT[h].ap()[:, 512 * tg:512 * tg + 256],
                func=AF.Square),
                after=[out_toks[(h, 7)], last, prev_mq], inc=True)
            tq2 = p.op("scalar", lambda e, h=h, tg=tg: e.activation(
                out=C.sq.ap()[:, 0:256],
                in_=OUT[h].ap()[:, 512 * tg + 256:512 * (tg + 1)],
                func=AF.Square), after=[tq], inc=True)
            mq = p.op("tensor", lambda e, h=h, tg=tg: e.matmul(
                PS[3].ap()[0:1, 0:256], C.onescol.ap(),
                C.xb.ap()[:, 0:256],
                start=(h == 0), stop=(h == 7)),
                after=[tq, C.const_tok], inc=True)
            prev_mq = mq
            mq2 = p.op("tensor", lambda e, h=h, tg=tg: e.matmul(
                PS[2].ap()[0:1, 0:256], C.onescol.ap(),
                C.sq.ap()[:, 0:256],
                start=(h == 0), stop=(h == 7)),
                after=[tq2, C.const_tok], inc=True)
            last = mq2
        ev0 = p.op("vector", lambda e, tg=tg: e.tensor_copy(
            out=sqp.ap()[0:1, 0:256],
            in_=PS[3].ap()[0:1, 0:256]),
            after=[last, sq_toks[-1] if sq_toks else None], inc=True)
        ev = p.op("vector", lambda e, tg=tg: e.tensor_copy(
            out=sqp.ap()[0:1, 256:512],
            in_=PS[2].ap()[0:1, 0:256]), after=[ev0], inc=True)
        prev_mq = ev
        sq_toks.append(p.dma("sync", sqp_d[0:1, 512 * tg:512 * (tg + 1)],
                             sqp.ap(), after=[ev], stream="qsq"))

    for h in range(8):
        p.dma("sync", oraw_d[128 * h:128 * (h + 1), :], OUT[h].ap(),
              after=[out_toks[(h, 7)], p.now("vector"), p.now("scalar")],
              stream="qout")
    p.build()
    return nc


def build_a1b_launch():
    nc = bass.Bass("TRN2", target_bir_lowering=False, debug=False)
    T = 1024
    x_d = nc.dram_tensor("x", [2048, T], F32, kind="ExternalInput").ap()
    or_d = nc.dram_tensor("oraw", [2048, T], BF16, kind="ExternalInput").ap()
    sqa_d = nc.dram_tensor("sqa", [1, T], F32, kind="ExternalInput").ap()
    sqb_d = nc.dram_tensor("sqb", [1, T], F32, kind="ExternalInput").ap()
    wg_d = nc.dram_tensor("wgT", [2048, 2048], F32, kind="ExternalInput").ap()
    wo_d = nc.dram_tensor("woT", [2048, 2048], F32, kind="ExternalInput").ap()
    nw_d = nc.dram_tensor("norm_w", [128, 16], F32, kind="ExternalInput").ap()
    xo_d = nc.dram_tensor("xo", [2048, T], F32, kind="ExternalOutput").ap()
    p = Plan(nc)
    C = Common(nc, p, T)
    PS = C.PS
    NF = 16
    R = [nc.alloc_sbuf_tensor(f"R{i}", [128, T], F32) for i in range(NF)]
    Y = [nc.alloc_sbuf_tensor(f"Yb{i}", [128, T], BF16) for i in range(NF)]
    ON = [nc.alloc_sbuf_tensor(f"c_on{i}", [128, T], BF16)
          for i in range(NF)]
    g_sb = nc.alloc_sbuf_tensor("c_gsb", [128, T], BF16)
    nw = nc.alloc_sbuf_tensor("c_nw", [128, 16], F32)
    sqr = nc.alloc_sbuf_tensor("c_sqr", [1, T], F32)
    sqr2 = nc.alloc_sbuf_tensor("c_sqr2", [1, T], F32)
    wsl = [nc.alloc_sbuf_tensor(f"c_wsl{i}", [128, 2048], BF16)
           for i in range(2)]

    xt = [p.dma("sync", R[i].ap(), x_d[128 * i:128 * (i + 1), :],
                stream="qx") for i in range(NF)]
    ot = [p.dma("sync", ON[i].ap(), or_d[128 * i:128 * (i + 1), :],
                stream="qor") for i in range(NF)]
    s1 = p.dma("sync", sqr.ap(), sqa_d, stream="qsm")
    s2 = p.dma("sync", sqr2.ap(), sqb_d, stream="qsm")
    s3 = p.dma("sync", nw.ap(), nw_d, stream="qsm")

    ln_toks = C.layernorm([R[i].ap() for i in range(NF)],
                          [Y[i].ap() for i in range(NF)],
                          eps=1e-5, after=xt)
    t1 = p.op("vector", lambda e: e.tensor_add(
        out=sqr.ap(), in0=sqr.ap(), in1=sqr2.ap()),
        after=[s1, s2, s3], inc=True)
    t2 = p.op("scalar", lambda e: e.activation(
        out=sqr.ap(), in_=sqr.ap(), func=AF.Ln,
        bias=C.constf(1e-6 * 2048)), after=[t1], inc=True)
    t3 = p.op("scalar", lambda e: e.activation(
        out=sqr.ap(), in_=sqr.ap(), func=AF.Exp, scale=-0.5),
        after=[t2], inc=True)
    t4 = p.op("vector", lambda e: e.tensor_scalar_mul(
        out=sqr.ap(), in0=sqr.ap(), scalar1=float(np.sqrt(2048.0))),
        after=[t3], inc=True)
    bts = []
    for n0 in range(0, T, 512):
        bts.append(p.op("tensor", lambda e, n0=n0: e.matmul(
            PS[2].ap()[:, n0:n0 + 512], C.onesrow_f.ap(),
            sqr.ap()[0:1, n0:n0 + 512], start=True, stop=True),
            after=[t4, ln_toks[-1], C.const_tok], inc=True))
    on_toks = []
    for i in range(NF):
        on_toks.append(p.op("vector", lambda e, i=i:
            e.scalar_tensor_tensor(
                out=ON[i].ap(), in0=ON[i].ap(), scalar=nw.ap()[:, i:i + 1],
                in1=PS[2].ap()[:, :T], op0=ALU.mult, op1=ALU.mult),
            after=bts + [ot[-1], s3], inc=True))
    prev_w = [None, None]
    prev_sig = None
    go_toks = []
    for m in range(NF):
        s = m % 2
        dstw = wsl[s].ap().rearrange("p (k c) -> p k c", k=16)
        lt = p.dma("gpsimd", dstw,
                   wg_d[:, 128 * m:128 * (m + 1)]
                   .rearrange("(k p) c -> p k c", p=128),
                   after=[prev_w[s]], stream=f"wg{s}")
        last = None
        for k in range(16):
            for n0 in (0, 512):
                deps = [lt, ln_toks[k]]
                if k == 0 and n0 == 0:
                    deps.append(prev_sig)
                last = p.op("tensor", lambda e, k=k, n0=n0, s=s: e.matmul(
                    PS[3].ap()[:, n0:n0 + 512],
                    wsl[s].ap()[:, 128 * k:128 * (k + 1)],
                    Y[k].ap()[:, n0:n0 + 512],
                    start=(k == 0), stop=(k == 15)),
                    after=deps, inc=(k == 15 and n0 == 512))
        prev_w[s] = last
        tsig = p.op("scalar", lambda e: e.activation(
            out=g_sb.ap(), in_=PS[3].ap()[:, :T], func=AF.Sigmoid),
            after=[last], inc=True)
        prev_sig = tsig
        go_toks.append(p.op("vector", lambda e, m=m: e.tensor_mul(
            out=ON[m].ap(), in0=ON[m].ap(), in1=g_sb.ap()),
            after=[tsig, on_toks[m]], inc=True))
        prev_sig = go_toks[-1]
    r_toks = []
    prev_ev = None
    for m in range(NF):
        s = m % 2
        dstw = wsl[s].ap().rearrange("p (k c) -> p k c", k=16)
        lt = p.dma("gpsimd", dstw,
                   wo_d[:, 128 * m:128 * (m + 1)]
                   .rearrange("(k p) c -> p k c", p=128),
                   after=[prev_w[s]], stream=f"wg{s}")
        last = None
        for k in range(16):
            for n0 in (0, 512):
                deps = [lt, go_toks[k]]
                if k == 0 and n0 == 0:
                    deps.append(prev_ev)
                last = p.op("tensor", lambda e, k=k, n0=n0, s=s: e.matmul(
                    PS[0].ap()[:, n0:n0 + 512],
                    wsl[s].ap()[:, 128 * k:128 * (k + 1)],
                    ON[k].ap()[:, n0:n0 + 512],
                    start=(k == 0), stop=(k == 15)),
                    after=deps, inc=(k == 15 and n0 == 512))
        prev_w[s] = last
        prev_ev = p.op("vector", lambda e, m=m: e.tensor_add(
            out=R[m].ap(), in0=PS[0].ap()[:, :T], in1=R[m].ap()),
            after=[last], inc=True)
        r_toks.append(prev_ev)
    for i in range(NF):
        p.dma("sync", xo_d[128 * i:128 * (i + 1), :], R[i].ap(),
              after=[r_toks[i]], stream="qout")
    p.build()
    return nc


def run_a1(x_shards, kw, prefix="L1_attn_"):
    g = lambda k: np.asarray(kw[prefix + k], np.float32)
    wqkvT = _t(g("w_qkv"))
    slopes = np.asarray(kw[prefix + "slope_rate"], np.float32).reshape(-1)
    nc_a = build_a1a_launch()
    in_a = []
    for c in range(8):
        b, hh = c // 2, c % 2
        heads = range(8 * hh, 8 * hh + 8)
        xfull = np.ascontiguousarray(
            np.concatenate([x_shards[2 * b], x_shards[2 * b + 1]], axis=1))
        in_a.append({
            "x": xfull,
            "wqT": np.ascontiguousarray(np.concatenate(
                [wqkvT[:, 384 * h:384 * h + 128] for h in heads], axis=1)),
            "wkT": np.ascontiguousarray(np.concatenate(
                [wqkvT[:, 384 * h + 128:384 * h + 256] for h in heads],
                axis=1)),
            "wvT": np.ascontiguousarray(np.concatenate(
                [wqkvT[:, 384 * h + 256:384 * h + 384] for h in heads],
                axis=1)),
            "slopes8": np.ascontiguousarray(
                slopes[8 * hh:8 * hh + 8][None, :]),
        })
    res = run_bass_kernel_spmd(nc_a, in_a, core_ids=list(range(8)))
    oraw = [res.results[c]["oraw"] for c in range(8)]
    sqp = [res.results[c]["sqp"] for c in range(8)]

    nc_b = build_a1b_launch()
    wgT = _t(g("w_gate"))
    woT = _t(g("w_out"))
    nwfm = np.ascontiguousarray(g("norm_w").reshape(16, 128).T)
    in_b = []
    for c in range(8):
        b, hh = c // 2, c % 2
        sl = slice(1024 * hh, 1024 * (hh + 1))
        in_b.append({
            "x": x_shards[c],
            "oraw": np.ascontiguousarray(np.concatenate(
                [oraw[2 * b][:, sl], oraw[2 * b + 1][:, sl]], axis=0)),
            "sqa": np.ascontiguousarray(sqp[2 * b][:, sl]),
            "sqb": np.ascontiguousarray(sqp[2 * b + 1][:, sl]),
            "wgT": wgT, "woT": woT, "norm_w": nwfm,
        })
    res = run_bass_kernel_spmd(nc_b, in_b, core_ids=list(range(8)))
    return [res.results[c]["xo"] for c in range(8)]


def build_f1_launch(T=1024):
    nc = bass.Bass("TRN2", target_bir_lowering=False, debug=False)
    x_d = nc.dram_tensor("x", [2048, T], F32, kind="ExternalInput").ap()
    W = {
        "gate_wT": nc.dram_tensor("gate_wT", [2048, 8], F32,
                                  kind="ExternalInput").ap(),
        "w1": nc.dram_tensor("w1", [8, 2048, 1408], F32,
                             kind="ExternalInput").ap(),
        "w3": nc.dram_tensor("w3", [8, 2048, 1408], F32,
                             kind="ExternalInput").ap(),
        "w2": nc.dram_tensor("w2", [8, 1408, 2048], F32,
                             kind="ExternalInput").ap(),
        "sw1": nc.dram_tensor("sw1", [2048, 1408], F32,
                              kind="ExternalInput").ap(),
        "sw3": nc.dram_tensor("sw3", [2048, 1408], F32,
                              kind="ExternalInput").ap(),
        "sw2": nc.dram_tensor("sw2", [1408, 2048], F32,
                              kind="ExternalInput").ap(),
    }
    xo_d = nc.dram_tensor("xo", [2048, T], F32, kind="ExternalOutput").ap()

    p = Plan(nc)
    C = Common(nc, p, T)
    NF = 16
    R = [nc.alloc_sbuf_tensor(f"R{i}", [128, T], F32) for i in range(NF)]
    Y = [nc.alloc_sbuf_tensor(f"Yb{i}", [128, T], BF16) for i in range(NF)]

    xt = [p.dma("sync", R[i].ap(), x_d[128 * i:128 * (i + 1), :],
                stream="qx") for i in range(NF)]
    ln_toks = C.layernorm([R[i].ap() for i in range(NF)],
                          [Y[i].ap() for i in range(NF)],
                          eps=1e-5, after=xt)
    emit_f1(nc, p, C, Y, R, T, W, y_toks=ln_toks)
    for i in range(NF):
        p.dma("sync", xo_d[128 * i:128 * (i + 1), :], R[i].ap(),
              after=[p.now("vector")], stream="qout")
    p.build()
    return nc



def _t(a):
    return np.ascontiguousarray(np.asarray(a, dtype=np.float32).T)


def _c(a):
    return np.ascontiguousarray(np.asarray(a, dtype=np.float32))


def shard_tokens_fm(x):
    out = []
    for c in range(8):
        b, h = c // 2, c % 2
        out.append(_t(x[b, 1024 * h:1024 * h + 1024, :]))
    return out


def unshard_tokens_fm(shards):
    x = np.empty((4, 2048, 2048), dtype=np.float32)
    for c in range(8):
        b, h = c // 2, c % 2
        x[b, 1024 * h:1024 * h + 1024, :] = shards[c].T
    return x


def a0_inputs(kw, prefix="L0_attn_"):
    g = lambda k: np.asarray(kw[prefix + k], np.float32)
    conv_w = g("conv_w").reshape(34, 128, 4).transpose(1, 0, 2).reshape(128, 136)
    return {
        "w_inT": _t(g("W_in")),
        "w_outT": _t(g("W_out")),
        "conv_w": np.ascontiguousarray(conv_w),
        "conv_b": np.ascontiguousarray(g("conv_b").reshape(34, 128).T),
        "dt_bias": np.ascontiguousarray(g("dt_bias")[:, None]),
        "A_log": np.ascontiguousarray(g("A_log")[:, None]),
        "D_fm": np.ascontiguousarray(np.repeat(g("D"), 64).reshape(32, 128).T),
        "norm_w": np.ascontiguousarray(g("norm_w").reshape(32, 128).T),
    }


def run_a0(x_shards, kw, prefix="L0_attn_"):
    nc = build_a0_launch()
    wmap = a0_inputs(kw, prefix)
    in_maps = []
    for c in range(8):
        halo = (np.zeros((2048, 3), np.float32) if c % 2 == 0
                else np.ascontiguousarray(x_shards[c - 1][:, 1021:1024]))
        m = dict(wmap)
        m["x"] = np.ascontiguousarray(
            np.concatenate([halo, x_shards[c]], axis=1))
        in_maps.append(m)
    res = run_bass_kernel_spmd(nc, in_maps, core_ids=list(range(8)))
    return [res.results[c]["xo"] for c in range(8)]


def f1_inputs(weights_prefix, kw):
    return {
        "gate_wT": _t(kw[weights_prefix + "gate_w"]),
        "w1": _c(kw[weights_prefix + "w1"]),
        "w3": _c(kw[weights_prefix + "w3"]),
        "w2": _c(kw[weights_prefix + "w2"]),
        "sw1": _c(kw[weights_prefix + "sw1"]),
        "sw3": _c(kw[weights_prefix + "sw3"]),
        "sw2": _c(kw[weights_prefix + "sw2"]),
    }


def run_f1(x_shards, kw, prefix="L1_ffn_"):
    nc = build_f1_launch()
    wmap = f1_inputs(prefix, kw)
    in_maps = [dict(wmap, x=x_shards[c]) for c in range(8)]
    res = run_bass_kernel_spmd(nc, in_maps, core_ids=list(range(8)))
    return [res.results[c]["xo"] for c in range(8)]


def _xlast64(shards):
    xl = np.ascontiguousarray(
        np.stack([shards[2 * b + 1][:, 1023] for b in range(4)], axis=1))
    return np.ascontiguousarray(
        xl.reshape(16, 128, 4).transpose(1, 0, 2).reshape(128, 64))


def kernel(**kw):
    x = np.asarray(kw["L0_attn_input_states"], dtype=np.float32)
    shards = shard_tokens_fm(x)
    shards = run_f0(shards, _xlast64(shards), kw, "L0_ffn_")
    shards = run_a0(shards, kw, "L0_attn_")
    shards = run_f0(shards, _xlast64(shards), kw, "L0_ffn_")
    shards = run_f1(shards, kw, "L1_ffn_")
    shards = run_a1(shards, kw, "L1_attn_")
    shards = run_f1(shards, kw, "L1_ffn_")
    return unshard_tokens_fm(shards)


# version 02
# speedup vs baseline: 1.0903x; 1.0903x over previous
import numpy as np
import ml_dtypes
import concourse.bass as bass
import concourse.mybir as mybir
from concourse.bass_utils import run_bass_kernel_spmd

F32 = mybir.dt.float32
BF16 = mybir.dt.bfloat16
I32 = mybir.dt.int32
AF = mybir.ActivationFunctionType
ALU = mybir.AluOpType
AX = mybir.AxisListType
BF = ml_dtypes.bfloat16

ENGINES = ("sync", "scalar", "vector", "tensor", "gpsimd")
DMAQ = {"sync": "qsync", "gpsimd": "qpool", "scalar": "qact"}


class Tok:
    __slots__ = ("sem", "val")

    def __init__(self, sem, val):
        self.sem = sem
        self.val = val


class Plan:

    def __init__(self, nc):
        self.nc = nc
        self.ops = []
        self.ecnt = {e: 0 for e in ENGINES}
        self.ecnt["sync"] = 1
        self.qcnt = {}

    SERIAL = ("vector", "scalar", "gpsimd")

    def op(self, eng, fn, after=(), inc=False):
        waits = [t for t in after if t is not None]
        if eng in self.SERIAL:
            inc = True
        tok = None
        if inc:
            self.ecnt[eng] += 1
            tok = Tok("e" + eng, self.ecnt[eng])
        self.ops.append(
            (eng, fn, waits,
             ("e" + eng, 1, self.ecnt[eng]) if inc else None))
        return tok

    def dma(self, eng, out, in_, after=(), stream=None):
        q = stream or ("q" + eng)
        self.qcnt[q] = self.qcnt.get(q, 0) + 16
        tok = Tok(q, self.qcnt[q])
        waits = [t for t in after if t is not None]
        self.ops.append(
            (eng, lambda e, out=out, in_=in_: e.dma_start(out=out, in_=in_),
             waits, (q, 16, self.qcnt[q])))
        return tok

    def now(self, eng):
        if eng in ENGINES:
            return Tok("e" + eng, self.ecnt[eng])
        return Tok(eng, self.qcnt[eng])

    def build(self):
        nc = self.nc
        sems = {}
        names = ["e" + e for e in ENGINES] + sorted(self.qcnt)
        for n in names:
            sems[n] = nc.alloc_semaphore(name="s_" + n)
        for e in ENGINES:
            if e != "sync":
                self.ecnt[e] += 1

        with nc.Block() as block:
            def mk_body(name):
                def body(eng):
                    if name == "sync":
                        for s in sems.values():
                            eng.sem_clear(s)
                        eng.nop().then_inc(sems["esync"], 1)
                    else:
                        eng.wait_op(sems["esync"], 1, "sem-eq")
                    hwm = {}
                    for (oe, fn, waits, inc) in self.ops:
                        if oe != name:
                            continue
                        need = {}
                        for t in waits:
                            need[t.sem] = max(need.get(t.sem, 0), t.val)
                        if (name in self.SERIAL and inc is not None
                                and inc[0] == "e" + name):
                            pv = inc[2] - 1
                            if pv > 0:
                                need["e" + name] = max(
                                    need.get("e" + name, 0), pv)
                        for s, v in need.items():
                            if hwm.get(s, 0) >= v:
                                continue
                            hwm[s] = v
                            eng.wait_ge(sems[s], v)
                        inst = fn(eng)
                        if inc is not None:
                            inst.then_inc(sems[inc[0]], inc[1])
                    if name == "sync":
                        for e in ENGINES:
                            if e != "sync":
                                eng.wait_ge(sems["e" + e], self.ecnt[e])
                        for q in sorted(self.qcnt):
                            if self.qcnt[q]:
                                eng.wait_ge(sems[q], self.qcnt[q])
                        eng.nop().then_inc(sems["esync"], 1)
                    else:
                        eng.wait_ge(sems["e" + name], self.ecnt[name] - 1)
                        eng.nop().then_inc(sems["e" + name], 1)
                return body
            for name in ENGINES:
                getattr(block, name)(mk_body(name))


def cdiv(a, b):
    return -(-a // b)



class Common:
    def __init__(self, nc, p, T, ps_shapes=(1024, 1024, 1024, 1024)):
        self.nc, self.p, self.T = nc, p, T
        self.xb = nc.alloc_sbuf_tensor("c_xb", [128, T], BF16)
        self.sq = nc.alloc_sbuf_tensor("c_sq", [128, T], BF16)
        self.r_mu = nc.alloc_sbuf_tensor("c_rmu", [1, T], F32)
        self.r_v = nc.alloc_sbuf_tensor("c_rv", [1, T], F32)
        self.lntmp = nc.alloc_sbuf_tensor("c_lntmp", [128, T], F32)
        self.r_b = self.lntmp
        self.onescol = nc.alloc_sbuf_tensor("c_onescol", [128, 1], BF16)
        self.onesrow_f = nc.alloc_sbuf_tensor("c_onesrow_f", [1, 128], F32)
        self.ident_f = nc.alloc_sbuf_tensor("c_ident_f", [128, 128], F32)
        self.ident_b = nc.alloc_sbuf_tensor("c_ident_b", [128, 128], BF16)
        ident_d = nc.inline_tensor(np.eye(128, dtype=np.float32),
                                   name="c_ident_src")
        identb_d = nc.inline_tensor(np.eye(128, dtype=BF),
                                    name="c_identb_src")
        self.PS = [nc.alloc_psum_tensor(f"c_ps{i}", [128, s], F32)
                   for i, s in enumerate(ps_shapes)]
        self.cbuf = nc.alloc_sbuf_tensor("c_consts", [128, 32], F32)
        self.cmap = {}
        p.op("vector", lambda e: e.memset(self.onescol.ap(), 1.0))
        p.op("vector", lambda e: e.memset(self.onesrow_f.ap(), 1.0))
        for v in (1e-5, 1e-6, -0.5, 0.5, 1.0 / 2048, 1.0 / 4096,
                  1e-5 * 4096, 1e-6 * 2048):
            self.constf(v)
        p.op("vector", lambda e: e.nop(), inc=True)
        t1 = p.dma("sync", self.ident_f.ap(), ident_d.ap(),
                   stream="qconst")
        t2 = p.dma("sync", self.ident_b.ap(), identb_d.ap(),
                   stream="qconst")
        self.const_tok = Tok("evector", p.ecnt["vector"])
        self.ident_tok = t2

    def constf(self, val):
        val = float(val)
        if val not in self.cmap:
            assert not hasattr(self, "const_tok"), \
                f"constf({val}) after init: pre-register it in Common"
            i = len(self.cmap)
            assert i < 32
            self.cmap[val] = i
            ap = self.cbuf.ap()[:, i:i + 1]
            self.p.op("vector", lambda e, ap=ap, val=val: e.memset(ap, val))
            self.nc.const_aps.register(F32, val, ap)
        return val

    def layernorm(self, x_chunks, out_chunks, T=None, eps=1e-5,
                  gamma=None, beta=None, after=()):
        p = self.p
        T = T or self.T
        nf = len(x_chunks)
        D = nf * 128
        if hasattr(self, "ln_ps"):
            PSsum, PSsq, self.sq_off, PSb1, PSb2 = self.ln_ps
        else:
            PSsum, PSsq, PSb1, PSb2 = self.PS
            self.sq_off = 0
        after = list(after) + [self.const_tok]
        mm = []
        for i, xc in enumerate(x_chunks):
            tc = p.op("vector", lambda e, xc=xc: e.tensor_copy(
                out=self.xb.ap()[:, :T], in_=xc), after=after, inc=True)
            ts = p.op("scalar", lambda e, xc=xc: e.activation(
                out=self.sq.ap()[:, :T], in_=xc, func=AF.Square),
                after=after, inc=True)
            mm = []
            for n0 in range(0, T, 512):
                n1 = min(T, n0 + 512)
                mm.append(p.op("tensor", lambda e, i=i, n0=n0, n1=n1:
                    e.matmul(PSsum.ap()[0:1, n0:n1], self.onescol.ap(),
                             self.xb.ap()[:, n0:n1],
                             start=(i == 0), stop=(i == nf - 1)),
                    after=[tc], inc=True))
                mm.append(p.op("tensor", lambda e, i=i, n0=n0, n1=n1:
                    e.matmul(PSsq.ap()[0:1, self.sq_off + n0:
                                       self.sq_off + n1],
                             self.onescol.ap(), self.sq.ap()[:, n0:n1],
                             start=(i == 0), stop=(i == nf - 1)),
                    after=[ts], inc=True))
            if i < nf - 1:
                p.op("vector", lambda e: e.nop(), after=[mm[-2]])
                p.op("scalar", lambda e: e.nop(), after=[mm[-1]])
            after = []
        r_mu, r_v, r_b = self.r_mu, self.r_v, self.r_b
        tmu = p.op("vector", lambda e: e.tensor_scalar_mul(
            out=r_mu.ap()[:, :T], in0=PSsum.ap()[0:1, :T], scalar1=1.0 / D),
            after=mm, inc=True)
        t1 = p.op("vector", lambda e: e.tensor_scalar_mul(
            out=r_v.ap()[:, :T],
            in0=PSsq.ap()[0:1, self.sq_off:self.sq_off + T],
            scalar1=1.0 / D), after=mm, inc=True)
        t2 = p.op("scalar", lambda e: e.activation(
            out=r_b.ap()[0:1, :T], in_=r_mu.ap()[:, :T], func=AF.Square),
            after=[tmu], inc=True)
        t3 = p.op("vector", lambda e: e.tensor_sub(
            out=r_v.ap()[:, :T], in0=r_v.ap()[:, :T],
            in1=r_b.ap()[0:1, :T]), after=[t1, t2], inc=True)
        t4 = p.op("scalar", lambda e: e.activation(
            out=r_v.ap()[:, :T], in_=r_v.ap()[:, :T], func=AF.Ln,
            bias=self.constf(eps)), after=[t3], inc=True)
        t5 = p.op("scalar", lambda e: e.activation(
            out=r_v.ap()[:, :T], in_=r_v.ap()[:, :T], func=AF.Exp,
            scale=-0.5), after=[t4], inc=True)
        t6 = p.op("vector", lambda e: e.tensor_mul(
            out=r_b.ap()[0:1, :T], in0=r_mu.ap()[:, :T],
            in1=r_v.ap()[:, :T]), after=[t5], inc=True)
        t7 = p.op("vector", lambda e: e.tensor_scalar_mul(
            out=r_b.ap()[0:1, :T], in0=r_b.ap()[0:1, :T], scalar1=-1.0),
            after=[t6], inc=True)
        bt = []
        for n0 in range(0, T, 512):
            n1 = min(T, n0 + 512)
            bt.append(p.op("tensor", lambda e, n0=n0, n1=n1: e.matmul(
                PSb1.ap()[:, n0:n1], self.onesrow_f.ap(),
                r_v.ap()[0:1, n0:n1], start=True, stop=True),
                after=[t5, t7], inc=True))
            bt.append(p.op("tensor", lambda e, n0=n0, n1=n1: e.matmul(
                PSb2.ap()[:, n0:n1], self.onesrow_f.ap(),
                r_b.ap()[0:1, n0:n1], start=True, stop=True),
                after=[t7], inc=True))
        toks = []
        for i, (xc, oc) in enumerate(zip(x_chunks, out_chunks)):
            ta = p.op("vector", lambda e, xc=xc: e.tensor_mul(
                out=self.lntmp.ap()[:, :T], in0=xc, in1=PSb1.ap()[:, :T]),
                after=bt, inc=True)
            if gamma is None:
                tl = p.op("vector", lambda e, oc=oc: e.tensor_add(
                    out=oc, in0=self.lntmp.ap()[:, :T],
                    in1=PSb2.ap()[:, :T]), after=[ta], inc=True)
            else:
                p.op("vector", lambda e: e.tensor_add(
                    out=self.lntmp.ap()[:, :T], in0=self.lntmp.ap()[:, :T],
                    in1=PSb2.ap()[:, :T]), after=[ta])
                tl = p.op("vector", lambda e, oc=oc, i=i: e.tensor_scalar(
                    out=oc, in0=self.lntmp.ap()[:, :T], scalar1=gamma[i],
                    scalar2=beta[i], op0=ALU.mult, op1=ALU.add), inc=True)
            toks.append(tl)
        return toks



def emit_f1(nc, p, C, Y, R, T, W, y_toks, name="f1"):
    NF, NH = 16, 11
    PS = C.PS

    gw = nc.alloc_sbuf_tensor(name + "_gw", [128, 16 * 8], BF16)
    gwt_toks = [p.dma("gpsimd", gw.ap()[:, 8 * i:8 * (i + 1)],
                      W["gate_wT"][128 * i:128 * (i + 1), :], stream="qgw")
                for i in range(16)]
    gwt = [gwt_toks[-1]] * 16
    ncht = T // 128
    CMB = [nc.alloc_sbuf_tensor(f"{name}_cmb{e}", [128, T], BF16)
           for e in range(8)]
    sc_sb = nc.alloc_sbuf_tensor(name + "_scsb", [128, 8], F32)
    sc2 = nc.alloc_sbuf_tensor(name + "_sc2", [128, 8], F32)
    mrow = nc.alloc_sbuf_tensor(name + "_mrow", [128, 4], F32)
    cmb_toks = []
    prev_tsg = None
    prev_bm = None
    prev_ev = None
    for tchunk in range(ncht):
        t0 = 128 * tchunk
        tm = None
        for i in range(16):
            deps = [y_toks[i], gwt[i]]
            if i == 0 and prev_tsg is not None:
                deps.append(prev_tsg)
            tm = p.op("tensor", lambda e, i=i, t0=t0: e.matmul(
                PS[0].ap()[:, 0:8], Y[i].ap()[:, t0:t0 + 128],
                gw.ap()[:, 8 * i:8 * (i + 1)],
                start=(i == 0), stop=(i == 15)),
                after=deps, inc=(i == 15))
        tsg = p.op("scalar", lambda e: e.activation(
            out=sc_sb.ap(), in_=PS[0].ap()[:, 0:8], func=AF.Sigmoid),
            after=[tm], inc=True)
        prev_tsg = tsg
        t1 = p.op("vector", lambda e: e.reduce_max(
            out=mrow.ap()[:, 0:1], in_=sc_sb.ap(), axis=AX.X),
            after=[tsg, prev_bm], inc=True)
        t2 = p.op("vector", lambda e: e.tensor_scalar(
            out=sc2.ap(), in0=sc_sb.ap(), scalar1=mrow.ap()[:, 0:1],
            scalar2=-1e30, op0=ALU.is_ge, op1=ALU.mult), after=[t1], inc=True)
        t3 = p.op("vector", lambda e: e.tensor_add(
            out=sc2.ap(), in0=sc2.ap(), in1=sc_sb.ap()), after=[t2], inc=True)
        t4 = p.op("vector", lambda e: e.reduce_max(
            out=mrow.ap()[:, 1:2], in_=sc2.ap(), axis=AX.X),
            after=[t3], inc=True)
        t5 = p.op("vector", lambda e: e.tensor_add(
            out=mrow.ap()[:, 2:3], in0=mrow.ap()[:, 0:1],
            in1=mrow.ap()[:, 1:2]), after=[t4], inc=True)
        p.op("vector", lambda e: e.tensor_scalar_add(
            out=mrow.ap()[:, 2:3], in0=mrow.ap()[:, 2:3], scalar1=1e-20))
        t6 = p.op("vector", lambda e: e.reciprocal(
            out=mrow.ap()[:, 2:3], in_=mrow.ap()[:, 2:3]), inc=True)
        p.op("vector", lambda e: e.tensor_scalar_mul(
            out=mrow.ap()[:, 2:3], in0=mrow.ap()[:, 2:3], scalar1=2.5))
        t7 = p.op("vector", lambda e: e.tensor_scalar(
            out=sc2.ap(), in0=sc_sb.ap(), scalar1=mrow.ap()[:, 1:2],
            scalar2=None, op0=ALU.is_ge), after=[t6], inc=True)
        t8 = p.op("vector", lambda e: e.tensor_mul(
            out=sc2.ap(), in0=sc2.ap(), in1=sc_sb.ap()), after=[t7], inc=True)
        t9 = p.op("vector", lambda e: e.tensor_scalar_mul(
            out=sc2.ap(), in0=sc2.ap(), scalar1=mrow.ap()[:, 2:3]),
            after=[t8], inc=True)
        evs = []
        for e8 in range(8):
            bm = p.op("tensor", lambda e, e8=e8: e.matmul(
                PS[1].ap()[:, 128 * e8:128 * (e8 + 1)],
                sc2.ap()[:, e8:e8 + 1].broadcast_to([128, 128]),
                C.ident_f.ap(), start=True, stop=True),
                after=[t9, C.ident_tok, prev_ev], inc=True)
            prev_bm = bm
            ev = p.op("vector", lambda e, e8=e8, t0=t0: e.tensor_copy(
                out=CMB[e8].ap()[:, t0:t0 + 128],
                in_=PS[1].ap()[:, 128 * e8:128 * (e8 + 1)]),
                after=[bm], inc=True)
            prev_ev = ev
            evs.append(ev)
        cmb_toks.append(evs[-1])

    wbuf = [nc.alloc_sbuf_tensor(f"{name}_wb{i}", [128, 2048], BF16)
            for i in range(4)]
    w2buf = [nc.alloc_sbuf_tensor(f"{name}_w2b{i}", [128, 1408], BF16)
             for i in range(2)]
    g_e = nc.alloc_sbuf_tensor(name + "_g", [128, NH * T], BF16)
    silu_t = nc.alloc_sbuf_tensor(name + "_silu", [128, T], BF16)

    racc_toks = [None] * NF
    prev_racc = None
    experts = list(range(8)) + ["shared"]
    prev_use = {0: None, 1: None, 2: None, 3: None}
    prev_w2use = {0: None, 1: None}
    prev_ydone = None
    prev_ts = None
    prev_tg = None
    for ex in experts:
        hu_done = []
        for m in range(NH):
            if ex == "shared":
                w1ap = W["sw1"][:, 128 * m:128 * (m + 1)]
                w3ap = W["sw3"][:, 128 * m:128 * (m + 1)]
            else:
                w1ap = W["w1"][ex, :, 128 * m:128 * (m + 1)]
                w3ap = W["w3"][ex, :, 128 * m:128 * (m + 1)]
            s1, s3 = (2 * (m % 2)), (2 * (m % 2) + 1)
            dst1 = wbuf[s1].ap().rearrange("p (k c) -> p k c", k=16)
            lt1 = p.dma("gpsimd", dst1,
                        w1ap.rearrange("(k p) c -> p k c", p=128),
                        after=[prev_use[s1]], stream=f"w{s1}")
            dst3 = wbuf[s3].ap().rearrange("p (k c) -> p k c", k=16)
            lt3 = p.dma("gpsimd", dst3,
                        w3ap.rearrange("(k p) c -> p k c", p=128),
                        after=[prev_use[s3]], stream=f"w{s3}")
            pts, ptg = prev_ts, prev_tg
            th = tu = None
            for k in range(16):
                for n0 in (0, 512):
                    hdeps = [lt1, y_toks[k]]
                    udeps = [lt3, y_toks[k]]
                    if k == 0 and n0 == 0:
                        hdeps += [pts]
                        udeps += [ptg]
                        if ex == 0 and m == 0:
                            hdeps.append(prev_tsg)
                            udeps.append(prev_ev)
                    th = p.op("tensor", lambda e, k=k, n0=n0, s1=s1: e.matmul(
                        PS[0].ap()[:, n0:n0 + 512],
                        wbuf[s1].ap()[:, 128 * k:128 * (k + 1)],
                        Y[k].ap()[:, n0:n0 + 512],
                        start=(k == 0), stop=(k == 15)),
                        after=hdeps, inc=(k == 15 and n0 == 512))
                    tu = p.op("tensor", lambda e, k=k, n0=n0, s3=s3: e.matmul(
                        PS[1].ap()[:, n0:n0 + 512],
                        wbuf[s3].ap()[:, 128 * k:128 * (k + 1)],
                        Y[k].ap()[:, n0:n0 + 512],
                        start=(k == 0), stop=(k == 15)),
                        after=udeps, inc=(k == 15 and n0 == 512))
            prev_use[s1] = th
            prev_use[s3] = tu
            ts = p.op("scalar", lambda e: e.activation(
                out=silu_t.ap(), in_=PS[0].ap()[:, :T], func=AF.Silu),
                after=[th], inc=True)
            prev_ts = ts
            tg = p.op("vector", lambda e, m=m: e.tensor_mul(
                out=g_e.ap()[:, m * T:(m + 1) * T], in0=silu_t.ap(),
                in1=PS[1].ap()[:, :T]), after=[ts, tu, prev_ydone], inc=True)
            prev_tg = tg
            if ex != "shared":
                tg = p.op("vector", lambda e, m=m, ex=ex: e.tensor_mul(
                    out=g_e.ap()[:, m * T:(m + 1) * T],
                    in0=g_e.ap()[:, m * T:(m + 1) * T],
                    in1=CMB[ex].ap()), after=[tg] + cmb_toks, inc=True)
            hu_done.append(tg)
        for f in range(NF):
            if ex == "shared":
                w2ap = W["sw2"][:, 128 * f:128 * (f + 1)]
            else:
                w2ap = W["w2"][ex, :, 128 * f:128 * (f + 1)]
            s2 = f % 2
            dst2 = w2buf[s2].ap().rearrange("p (k c) -> p k c", k=NH)
            lt2 = p.dma("gpsimd", dst2,
                        w2ap.rearrange("(k p) c -> p k c", p=128),
                        after=[prev_w2use[s2]], stream=f"w2{s2}")
            tmm = None
            for k in range(NH):
                for n0 in (0, 512):
                    deps = [lt2, hu_done[k]]
                    if k == 0 and n0 == 0:
                        deps.append(prev_racc)
                    tmm = p.op("tensor", lambda e, k=k, n0=n0, s2=s2:
                        e.matmul(PS[2].ap()[:, n0:n0 + 512],
                                 w2buf[s2].ap()[:, 128 * k:128 * (k + 1)],
                                 g_e.ap()[:, k * T + n0:k * T + n0 + 512],
                                 start=(k == 0), stop=(k == NH - 1)),
                        after=deps, inc=(k == NH - 1 and n0 == 512))
            prev_w2use[s2] = tmm
            racc_toks[f] = p.op("vector", lambda e, f=f: e.scalar_tensor_tensor(
                out=R[f].ap(), in0=PS[2].ap()[:, :T], scalar=0.5,
                in1=R[f].ap(), op0=ALU.mult, op1=ALU.add),
                after=[tmm], inc=True)
            prev_racc = racc_toks[f]
        prev_ydone = racc_toks[NF - 1]
    return racc_toks




def emit_f0(nc, p, C, Y, R, T, W, y_toks, name="f0"):
    PS = C.PS
    NF = 16

    xl = nc.alloc_sbuf_tensor(name + "_xl", [128, 64], F32)
    xn = nc.alloc_sbuf_tensor(name + "_xn", [128, 64], BF16)
    xmid = nc.alloc_sbuf_tensor(name + "_xmid", [128, 64], F32)
    selc = nc.alloc_sbuf_tensor(name + "_sel", [4, 1], F32)
    ing = nc.alloc_sbuf_tensor(name + "_ing", [128, 16], F32)
    inb = nc.alloc_sbuf_tensor(name + "_inb", [128, 16], F32)
    gb = nc.alloc_sbuf_tensor(name + "_gb", [128, 16], F32)
    outg = nc.alloc_sbuf_tensor(name + "_outg", [128, 16], F32)
    outb = nc.alloc_sbuf_tensor(name + "_outb", [128, 16], F32)
    b_agg = nc.alloc_sbuf_tensor(name + "_bagg", [4, 2048], BF16)
    b_d1q = nc.alloc_sbuf_tensor(name + "_bd1q", [4, 2048], BF16)
    b_d2 = nc.alloc_sbuf_tensor(name + "_bd2", [4, 2048], BF16)
    b_bc = nc.alloc_sbuf_tensor(name + "_bbc", [4, 2048], BF16)
    png = nc.alloc_sbuf_tensor(name + "_png", [4, 2048], BF16)
    pnb = nc.alloc_sbuf_tensor(name + "_pnb", [4, 2048], BF16)
    lds = []
    lds.append(p.dma("sync", xl.ap(), W["xlast"], stream="qsm"))
    for dst, key in ((selc, "sel"), (ing, "in_g"), (inb, "in_b"),
                     (gb, "gate_b"), (outg, "out_g"), (outb, "out_b")):
        lds.append(p.dma("sync", dst.ap(), W[key], stream="qsm"))
    lds2 = []
    for dst, key in ((b_agg, "agg_b4"), (b_d2, "d2b4"),
                     (b_bc, "bc_b4"), (png, "pn_g4"), (pnb, "pn_b4")):
        lds2.append(p.dma("gpsimd", dst.ap(), W[key], stream="qsmb"))
    sm_tok0 = lds[-1]
    sm_tok = lds2[-1]


    xchunks = [xl.ap()[:, 4 * i:4 * (i + 1)] for i in range(NF)]
    mchunks = [xmid.ap()[:, 4 * i:4 * (i + 1)] for i in range(NF)]
    nchunks = [xn.ap()[:, 4 * i:4 * (i + 1)] for i in range(NF)]
    after0 = [sm_tok0, p.now("vector"), p.now("scalar"), p.now("tensor")]
    t_ln1 = C.layernorm(xchunks, mchunks, T=4, eps=1e-5, after=after0)
    t_ln2 = C.layernorm(mchunks, nchunks, T=4, eps=1e-6,
                        gamma=[ing.ap()[:, i:i + 1] for i in range(NF)],
                        beta=[inb.ap()[:, i:i + 1] for i in range(NF)],
                        after=t_ln1)

    NSL = 4
    wsl = [nc.alloc_sbuf_tensor(f"{name}_wsl{i}", [128, 1024], BF16)
           for i in range(NSL)]
    prev_wuse = [None] * NSL

    th_tm = nc.alloc_sbuf_tensor(name + "_th", [4, 2048], F32)
    t_tm = nc.alloc_sbuf_tensor(name + "_ttm", [4, 2048], F32)
    g_tm = nc.alloc_sbuf_tensor(name + "_gtm", [4, 2048], F32)
    t_fm = nc.alloc_sbuf_tensor(name + "_tfm", [128, 64], BF16)
    g_fm = nc.alloc_sbuf_tensor(name + "_gfm", [128, 256], BF16)
    r4a = nc.alloc_sbuf_tensor(name + "_r4a", [4, 1], F32)
    r4b = nc.alloc_sbuf_tensor(name + "_r4b", [4, 1], F32)
    bcv = nc.alloc_sbuf_tensor(name + "_bcv", [128, 16], F32)

    def mm_tm(src_fm_chunks, w_dram, kin, nout, src_tok, stream, evac_fn,
              after_ps=()):
        guard = list(after_ps)
        all_toks = []
        nq = nout // 2048
        for q in range(nq):
            last = None
            for k in range(kin // 128):
                pr = 2 * (k % 2)
                lt = [None, None]
                for h in range(2):
                    s = pr + h
                    lt[h] = p.dma(
                        "gpsimd", wsl[s].ap(),
                        w_dram[128 * k:128 * (k + 1),
                               2048 * q + 1024 * h:
                               2048 * q + 1024 * (h + 1)],
                        after=[prev_wuse[s]], stream=stream + str(s))
                rd = [None, None]
                for n0 in range(0, 2048, 512):
                    reg = PS[0] if n0 < 1024 else PS[1]
                    c0 = n0 % 1024
                    h = n0 // 1024
                    deps = [lt[h], src_tok]
                    if k == 0 and n0 == 0:
                        deps += guard
                    last = p.op("tensor",
                        lambda e, k=k, reg=reg, c0=c0, s=pr + h, kin=kin:
                        e.matmul(reg.ap()[0:4, c0:c0 + 512],
                                 src_fm_chunks[k],
                                 wsl[s].ap()[:, c0:c0 + 512],
                                 start=(k == 0), stop=(k == kin // 128 - 1)),
                        after=deps, inc=True)
                    rd[h] = last
                for h in range(2):
                    prev_wuse[pr + h] = rd[h]
            guard = evac_fn(q, last)
            all_toks += guard
        return all_toks

    xn_chunks = [xn.ap()[:, 4 * i:4 * (i + 1)] for i in range(NF)]

    def agg_evac(q, mtok):
        toks = []
        for half in range(2):
            reg = PS[0] if half == 0 else PS[1]
            toks.append(p.op("vector",
                lambda e, reg=reg, half=half, q=q: e.tensor_add(
                    out=th_tm.ap()[:, 2048 * q + 1024 * half:
                                   2048 * q + 1024 * (half + 1)],
                    in0=reg.ap()[0:4, 0:1024],
                    in1=b_agg.ap()[:, 1024 * half:1024 * (half + 1)]),
                after=[mtok, sm_tok], inc=True))
        return toks

    th_toks = mm_tm(xn_chunks, W["agg_w"], 2048, 2048, t_ln2[-1], "wA",
                    agg_evac, after_ps=[p.now("vector"), p.now("scalar")])
    th_tok = th_toks[-1]
    bd1_use = [None]

    for step in range(4):
        t1 = p.op("vector", lambda e: e.reduce_sum(
            out=r4a.ap(), in_=th_tm.ap(), axis=AX.X), after=[th_tok],
            inc=True)
        t2 = p.op("vector", lambda e: e.tensor_scalar_mul(
            out=r4a.ap(), in0=r4a.ap(), scalar1=1.0 / 2048), after=[t1],
            inc=True)
        t3 = p.op("vector", lambda e: e.tensor_scalar(
            out=t_tm.ap(), in0=th_tm.ap(), scalar1=r4a.ap(),
            scalar2=None, op0=ALU.subtract), after=[t2], inc=True)
        t4 = p.op("vector", lambda e: e.scalar_tensor_tensor(
            out=g_tm.ap(), in0=t_tm.ap(), scalar=1.0, in1=t_tm.ap(),
            op0=ALU.mult, op1=ALU.mult, accum_out=r4b.ap()),
            after=[t3], inc=True)
        t5 = p.op("scalar", lambda e: e.activation(
            out=r4b.ap(), in_=r4b.ap(), func=AF.Ln,
            bias=C.constf(1e-6), scale=C.constf(1.0 / 2048)),
            after=[t4], inc=True)
        t6 = p.op("scalar", lambda e: e.activation(
            out=r4b.ap(), in_=r4b.ap(), func=AF.Exp, scale=-0.5),
            after=[t5], inc=True)
        t7 = p.op("vector", lambda e: e.tensor_scalar_mul(
            out=t_tm.ap(), in0=t_tm.ap(), scalar1=r4b.ap()),
            after=[t6], inc=True)
        t8 = p.op("vector", lambda e: e.tensor_mul(
            out=t_tm.ap(), in0=t_tm.ap(), in1=png.ap()), after=[t7, sm_tok],
            inc=True)
        t9 = p.op("vector", lambda e: e.tensor_add(
            out=t_tm.ap(), in0=t_tm.ap(), in1=pnb.ap()), after=[t8],
            inc=True)
        tf_toks = []
        prev_ev = None
        for i in range(NF):
            tt = p.op("tensor", lambda e, i=i: e.transpose(
                PS[2].ap()[:, 0:4], t_tm.ap()[0:4, 128 * i:128 * (i + 1)],
                C.ident_f.ap()[0:4, 0:4]),
                after=[t9, C.ident_tok, prev_ev], inc=True)
            ev = p.op("vector", lambda e, i=i: e.tensor_copy(
                out=t_fm.ap()[:, 4 * i:4 * (i + 1)], in_=PS[2].ap()[:, 0:4]),
                after=[tt], inc=True)
            prev_ev = ev
            tf_toks.append(ev)
        tf_tok = tf_toks[-1]
        tfc = [t_fm.ap()[:, 4 * i:4 * (i + 1)] for i in range(NF)]
        gf_toks = []

        def d1_evac(q, mtok):
            bt = p.dma("gpsimd", b_d1q.ap(),
                       W["d1b4"][:, 2048 * q:2048 * (q + 1)],
                       after=[bd1_use[0]], stream="qbd1")
            toks = []
            for half in range(2):
                reg = PS[0] if half == 0 else PS[1]
                c0 = 1024 * half
                ta = p.op("vector", lambda e, reg=reg, c0=c0:
                    e.tensor_add(
                        out=g_tm.ap()[:, c0:c0 + 1024],
                        in0=reg.ap()[0:4, 0:1024],
                        in1=b_d1q.ap()[:, c0:c0 + 1024]),
                    after=[mtok, bt], inc=True)
                bd1_use[0] = ta
                toks.append(p.op("scalar", lambda e, c0=c0: e.activation(
                    out=g_tm.ap()[:, c0:c0 + 1024],
                    in_=g_tm.ap()[:, c0:c0 + 1024], func=AF.Gelu),
                    after=[ta], inc=True))
            prev_ev = gf_toks[-1] if gf_toks else None
            for i in range(16):
                gi = 16 * q + i
                tt = p.op("tensor", lambda e, i=i: e.transpose(
                    PS[2].ap()[:, 0:4],
                    g_tm.ap()[0:4, 128 * i:128 * (i + 1)],
                    C.ident_f.ap()[0:4, 0:4]),
                    after=toks + [C.ident_tok, prev_ev], inc=True)
                ev = p.op("vector", lambda e, gi=gi: e.tensor_copy(
                    out=g_fm.ap()[:, 4 * gi:4 * (gi + 1)],
                    in_=PS[2].ap()[:, 0:4]), after=[tt], inc=True)
                prev_ev = ev
                gf_toks.append(ev)
            return [gf_toks[-1]]

        mm_tm(tfc, W["d1_w"], 2048, 8192, tf_tok, "wB", d1_evac,
              after_ps=[th_tok])
        gf_tok = gf_toks[-1]
        tb = p.op("vector", lambda e: e.tensor_add(
            out=th_tm.ap(), in0=th_tm.ap(), in1=b_d2.ap()),
            after=[th_tok, sm_tok], inc=True)
        gfc = [g_fm.ap()[:, 4 * i:4 * (i + 1)] for i in range(64)]

        def d2_evac(q, mtok):
            toks = []
            for half in range(2):
                reg = PS[0] if half == 0 else PS[1]
                c0 = 1024 * half
                toks.append(p.op("vector", lambda e, reg=reg, c0=c0:
                    e.tensor_add(
                        out=th_tm.ap()[:, c0:c0 + 1024],
                        in0=reg.ap()[0:4, 0:1024],
                        in1=th_tm.ap()[:, c0:c0 + 1024]),
                    after=[mtok, tb], inc=True))
            return toks

        th_toks = mm_tm(gfc, W["d2_w"], 8192, 2048, gf_tok, "wB", d2_evac,
                        after_ps=gf_toks[-1:])
        th_tok = th_toks[-1]

    thc = []
    tf_toks = []
    prev_ev = None
    for i in range(NF):
        tt = p.op("tensor", lambda e, i=i: e.transpose(
            PS[2].ap()[:, 0:4], th_tm.ap()[0:4, 128 * i:128 * (i + 1)],
            C.ident_f.ap()[0:4, 0:4]),
            after=[th_tok, C.ident_tok, prev_ev], inc=True)
        ev = p.op("vector", lambda e, i=i: e.tensor_copy(
            out=t_fm.ap()[:, 4 * i:4 * (i + 1)], in_=PS[2].ap()[:, 0:4]),
            after=[tt], inc=True)
        prev_ev = ev
        tf_toks.append(ev)
    tfc = [t_fm.ap()[:, 4 * i:4 * (i + 1)] for i in range(NF)]
    def bc_evac(q, mtok):
        toks = []
        for half in range(2):
            reg = PS[0] if half == 0 else PS[1]
            c0 = 1024 * half
            toks.append(p.op("vector", lambda e, reg=reg, c0=c0:
                e.tensor_add(
                    out=t_tm.ap()[:, c0:c0 + 1024],
                    in0=reg.ap()[0:4, 0:1024],
                    in1=b_bc.ap()[:, c0:c0 + 1024]), after=[mtok], inc=True))
        return toks

    bc_toks = mm_tm(tfc, W["bc_w"], 2048, 2048, tf_toks[-1], "wA",
                    bc_evac, after_ps=[th_tok])
    bcv_toks = []
    prev_ev = None
    for i in range(NF):
        tt = p.op("tensor", lambda e, i=i: e.matmul(
            PS[2].ap()[0:128, 0:1], t_tm.ap()[0:4, 128 * i:128 * (i + 1)],
            selc.ap(), start=True, stop=True),
            after=bc_toks + [sm_tok, prev_ev], inc=True)
        ev = p.op("vector", lambda e, i=i: e.tensor_copy(
            out=bcv.ap()[:, i:i + 1], in_=PS[2].ap()[0:128, 0:1]),
            after=[tt], inc=True)
        prev_ev = ev
        bcv_toks.append(ev)
    bcv_tok = bcv_toks[-1]

    OUTb = [nc.alloc_sbuf_tensor(f"{name}_o{i}", [128, T], BF16)
            for i in range(NF)]
    gsl = wsl
    gate_bf = nc.alloc_sbuf_tensor(name + "_gatebf", [128, T], BF16)
    prev_guse = prev_wuse
    prev_sig = None
    out_toks = []
    for m in range(NF):
        pr = 2 * (m % 2)
        lt = [None, None]
        for h in range(2):
            s = pr + h
            dstg = gsl[s].ap().rearrange("p (k c) -> p k c", k=8)
            lt[h] = p.dma("gpsimd", dstg,
                          W["gate_w"][1024 * h:1024 * (h + 1),
                                      128 * m:128 * (m + 1)]
                          .rearrange("(k p) c -> p k c", p=128),
                          after=[prev_guse[s]], stream=f"wg{s}")
        last = None
        for k in range(NF):
            h = k // 8
            for n0 in range(0, T, 512):
                deps = [lt[h], y_toks[k]]
                if k == 0 and n0 == 0:
                    deps += [prev_sig, bcv_tok]
                last = p.op("tensor",
                    lambda e, k=k, n0=n0, s=pr + h, kc=k % 8: e.matmul(
                        PS[3].ap()[:, n0:n0 + 512],
                        gsl[s].ap()[:, 128 * kc:128 * (kc + 1)],
                        Y[k].ap()[:, n0:n0 + 512],
                        start=(k == 0), stop=(k == NF - 1)),
                    after=deps, inc=(k == NF - 1 and n0 + 512 >= T))
        for h in range(2):
            prev_guse[pr + h] = last
        tsig = p.op("scalar", lambda e, m=m: e.activation(
            out=gate_bf.ap()[:, :T], in_=PS[3].ap()[:, :T], func=AF.Sigmoid,
            bias=gb.ap()[:, m:m + 1]),
            after=[last, sm_tok0,
                   out_toks[-1] if out_toks else None], inc=True)
        prev_sig = tsig
        out_toks.append(p.op("vector", lambda e, m=m: e.scalar_tensor_tensor(
            out=OUTb[m].ap(), in0=gate_bf.ap()[:, :T],
            scalar=bcv.ap()[:, m:m + 1], in1=Y[m].ap(),
            op0=ALU.mult, op1=ALU.add), after=[tsig, bcv_tok], inc=True))

    ln_after = out_toks + [p.now("vector"), p.now("scalar")]
    lno = C.layernorm([OUTb[i].ap() for i in range(NF)],
                      [Y[i].ap() for i in range(NF)], eps=1e-6,
                      gamma=[outg.ap()[:, i:i + 1] for i in range(NF)],
                      beta=[outb.ap()[:, i:i + 1] for i in range(NF)],
                      after=ln_after)
    r_toks = []
    for i in range(NF):
        r_toks.append(p.op("vector", lambda e, i=i: e.scalar_tensor_tensor(
            out=R[i].ap(), in0=Y[i].ap(), scalar=0.5, in1=R[i].ap(),
            op0=ALU.mult, op1=ALU.add), after=[lno[i]], inc=True))
    return r_toks


def build_f0_launch(T=1024):
    nc = bass.Bass("TRN2", target_bir_lowering=False, debug=False)
    x_d = nc.dram_tensor("x", [2048, T], F32, kind="ExternalInput").ap()
    names = {
        "xlast": [128, 64], "sel": [4, 1],
        "agg_w": [2048, 2048], "d1_w": [2048, 8192], "d2_w": [8192, 2048],
        "bc_w": [2048, 2048], "gate_w": [2048, 2048],
        "agg_b4": [4, 2048], "d1b4": [4, 8192], "d2b4": [4, 2048],
        "bc_b4": [4, 2048], "pn_g4": [4, 2048], "pn_b4": [4, 2048],
        "in_g": [128, 16], "in_b": [128, 16], "gate_b": [128, 16],
        "out_g": [128, 16], "out_b": [128, 16],
    }
    W = {k: nc.dram_tensor(
            k, sh,
            BF16 if k in ("agg_w", "d1_w", "d2_w", "bc_w", "gate_w") else F32,
            kind="ExternalInput").ap()
         for k, sh in names.items()}
    xo_d = nc.dram_tensor("xo", [2048, T], F32, kind="ExternalOutput").ap()
    p = Plan(nc)
    C = Common(nc, p, T)
    NF = 16
    R = [nc.alloc_sbuf_tensor(f"R{i}", [128, T], F32) for i in range(NF)]
    Y = [nc.alloc_sbuf_tensor(f"Yb{i}", [128, T], BF16) for i in range(NF)]
    xt = [p.dma("sync", R[i].ap(), x_d[128 * i:128 * (i + 1), :],
                stream="qx") for i in range(NF)]
    ln_toks = C.layernorm([R[i].ap() for i in range(NF)],
                          [Y[i].ap() for i in range(NF)],
                          eps=1e-5, after=xt)
    emit_f0(nc, p, C, Y, R, T, W, y_toks=ln_toks)
    for i in range(NF):
        p.dma("sync", xo_d[128 * i:128 * (i + 1), :], R[i].ap(),
              after=[p.now("vector")], stream="qout")
    p.build()
    return nc


def f0_inputs(kw, prefix, core):
    b = core // 2
    sel = np.zeros((4, 1), np.float32)
    sel[b, 0] = 1.0
    t4 = lambda v: np.tile(np.asarray(v, np.float32)[None, :], (4, 1))
    fmc = lambda v: np.ascontiguousarray(
        np.asarray(v, np.float32).reshape(16, 128).T)
    return {
        "sel": sel,
        "agg_b4": t4(kw[prefix + "agg_b"]),
        "d1b4": t4(kw[prefix + "d1_b"]),
        "d2b4": t4(kw[prefix + "d2_b"]),
        "bc_b4": t4(kw[prefix + "bc_b"]),
        "pn_g4": t4(kw[prefix + "pn_gamma"]),
        "pn_b4": t4(kw[prefix + "pn_beta"]),
        "in_g": fmc(kw[prefix + "in_gamma"]),
        "in_b": fmc(kw[prefix + "in_beta"]),
        "gate_b": fmc(kw[prefix + "gate_b"]),
        "out_g": fmc(kw[prefix + "out_gamma"]),
        "out_b": fmc(kw[prefix + "out_beta"]),
    }


def run_f0(x_shards, xlast, kw, prefix="L0_ffn_", cast=None):
    if cast is None:
        cast = run_cast(kw)
    nc = build_f0_launch()
    wmap = {"agg_w": cast["f0_agg"], "d1_w": cast["f0_d1"],
            "d2_w": cast["f0_d2"], "bc_w": cast["f0_bc"],
            "gate_w": cast["f0_gate"]}
    in_maps = []
    for c in range(8):
        m = f0_inputs(kw, prefix, c)
        m.update(wmap)
        m["x"] = x_shards[c]
        m["xlast"] = xlast
        in_maps.append(m)
    res = run_bass_kernel_spmd(nc, in_maps, core_ids=list(range(8)))
    return [res.results[c]["xo"] for c in range(8)]




def build_a0_launch():
    nc = bass.Bass("TRN2", target_bir_lowering=False, debug=False)
    x_d = nc.dram_tensor("x", [2048, 1027], F32, kind="ExternalInput").ap()
    names = {
        "w_inT": [2048, 8512], "w_outT": [4096, 2048],
        "conv_w": [128, 136], "conv_b": [128, 34],
        "dt_bias": [64, 1], "A_log": [64, 1],
        "D_fm": [128, 32], "norm_w": [128, 32],
    }
    W = {k: nc.dram_tensor(
            k, sh, BF16 if k in ("w_inT", "w_outT") else F32,
            kind="ExternalInput").ap()
         for k, sh in names.items()}
    xo_d = nc.dram_tensor("xo", [2048, 1024], F32, kind="ExternalOutput").ap()
    lt_np = (np.arange(256)[:, None] <= np.arange(256)[None, :]
             ).astype(np.float32)
    ltc_d = [nc.inline_tensor(np.ascontiguousarray(
        lt_np[128 * s:128 * (s + 1)]), name=f"ltc{s}") for s in range(2)]
    mk = []
    for sc in range(2):
        m = (np.arange(256)[None, :] >= (np.arange(128)[:, None] + 128 * sc)
             ).astype(BF)
        mk.append(nc.inline_tensor(np.ascontiguousarray(m), name=f"mkc{sc}"))

    p = Plan(nc)
    C = Common(nc, p, 512, ps_shapes=(1024, 512, 512))
    psA, psB, psC = C.PS[:3]
    C.ln_ps = (psA, psA, 512, psB, psC)
    PSD2 = nc.alloc_psum_tensor("a_psd2", [128, 512], F32)
    PSY = nc.alloc_psum_tensor("a_psy", [128, 1024], F32)
    PSTB = nc.alloc_psum_tensor("a_pstb", [128, 512], BF16)

    NF = 16
    R = [nc.alloc_sbuf_tensor(f"R{i}", [128, 1027], F32) for i in range(NF)]
    Yt = [nc.alloc_sbuf_tensor(f"Yt{i}", [128, 259], BF16) for i in range(NF)]
    hid = [nc.alloc_sbuf_tensor(f"a_hid{i}", [128, 259], BF16)
           for i in range(32)]
    gate = [nc.alloc_sbuf_tensor(f"a_gate{i}", [128, 256], BF16)
            for i in range(32)]
    Bt = nc.alloc_sbuf_tensor("a_Bt", [128, 259], BF16)
    Ct = nc.alloc_sbuf_tensor("a_Ct", [128, 259], BF16)
    dt_raw = nc.alloc_sbuf_tensor("a_dtraw", [64, 256], F32)
    dt_t1 = nc.alloc_sbuf_tensor("a_dtt1", [64, 256], F32)
    dt_t2 = nc.alloc_sbuf_tensor("a_dtt2", [64, 256], F32)
    dtv = nc.alloc_sbuf_tensor("a_dtv", [64, 256], F32)
    At = nc.alloc_sbuf_tensor("a_At", [64, 256], F32)
    acc = nc.alloc_sbuf_tensor("a_acc", [128, 256], F32)
    cw = nc.alloc_sbuf_tensor("a_cw", [128, 136], F32)
    cb = nc.alloc_sbuf_tensor("a_cb", [128, 34], F32)
    dtb = nc.alloc_sbuf_tensor("a_dtb", [64, 1], F32)
    acol = nc.alloc_sbuf_tensor("a_acol", [64, 1], F32)
    dfm = nc.alloc_sbuf_tensor("a_dfm", [128, 32], F32)
    nw = nc.alloc_sbuf_tensor("a_nw", [128, 32], F32)
    ltc = [nc.alloc_sbuf_tensor(f"a_ltc{s}", [128, 256], F32)
           for s in range(2)]
    mkc = [nc.alloc_sbuf_tensor(f"a_mkc{s}", [128, 256], BF16)
           for s in range(2)]
    atm = nc.alloc_sbuf_tensor("a_atm", [128, 128], F32)
    acum = nc.alloc_sbuf_tensor("a_acum", [64, 256], F32)
    nacumT = nc.alloc_sbuf_tensor("a_nacumT", [128, 128], F32)
    dkf = nc.alloc_sbuf_tensor("a_dkf", [64, 256], F32)
    sdf = nc.alloc_sbuf_tensor("a_sdf", [64, 256], F32)
    dkT = nc.alloc_sbuf_tensor("a_dkT", [128, 128], F32)
    sdT = nc.alloc_sbuf_tensor("a_sdT", [128, 128], F32)
    dtvT = nc.alloc_sbuf_tensor("a_dtvT", [128, 128], F32)
    hdtT = [nc.alloc_sbuf_tensor(f"a_hdtT{s}", [128, 4096], BF16)
            for s in range(2)]
    btm = nc.alloc_sbuf_tensor("a_btm", [128, 256], BF16)
    bdtm = nc.alloc_sbuf_tensor("a_bdtm", [128, 256], BF16)
    gmm = [nc.alloc_sbuf_tensor(f"a_gmm{s}", [128, 256], BF16)
           for s in range(2)]
    d2sb = nc.alloc_sbuf_tensor("a_d2sb", [128, 512], F32)
    mtsb = nc.alloc_sbuf_tensor("a_mtsb", [128, 512], BF16)
    pmt = [nc.alloc_sbuf_tensor(f"a_pmt{s}", [128, 512], BF16)
           for s in range(2)]
    stsb = nc.alloc_sbuf_tensor("a_stsb", [128, 64], BF16)
    yosb = nc.alloc_sbuf_tensor("a_yosb", [128, 128], F32)
    ysb = nc.alloc_sbuf_tensor("a_ysb", [128, 32 * 256], BF16)
    wsl = [nc.alloc_sbuf_tensor(f"a_wsl{i}", [128, 2048], BF16)
           for i in range(2)]
    wo = [nc.alloc_sbuf_tensor(f"a_wo{i}", [128, 4096], BF16)
          for i in range(2)]

    xt = [p.dma("sync", R[i].ap(), x_d[128 * i:128 * (i + 1), :],
                stream="qx") for i in range(NF)]
    sm = []
    for dst, key in ((cw, "conv_w"), (cb, "conv_b"), (dtb, "dt_bias"),
                     (acol, "A_log"), (dfm, "D_fm"), (nw, "norm_w")):
        sm.append(p.dma("sync", dst.ap(), W[key], stream="qsm"))
    for s in range(2):
        sm.append(p.dma("sync", ltc[s].ap(), ltc_d[s].ap(), stream="qsm"))
        sm.append(p.dma("sync", mkc[s].ap(), mk[s].ap(), stream="qsm"))
    sm_tok = sm[-1]
    ta = p.op("scalar", lambda e: e.activation(
        out=acol.ap(), in_=acol.ap(), func=AF.Exp), after=[sm_tok], inc=True)
    acol_tok = p.op("vector", lambda e: e.tensor_scalar_mul(
        out=acol.ap(), in0=acol.ap(), scalar1=-1.0), after=[ta], inc=True)

    prev_wuse = [None, None]
    prev_wouse = [None, None]
    r_upd = [Tok("qx", p.qcnt["qx"])] * NF
    YH = nc.alloc_sbuf_tensor("a_YH", [128, 48], BF16)
    yh_toks = C.layernorm(
        [R[i].ap()[:, 0:3] for i in range(NF)],
        [YH.ap()[:, 3 * i:3 * (i + 1)] for i in range(NF)], T=3,
        eps=1e-5, after=xt + [acol_tok])

    for cix in range(4):
        col0 = 3 + 256 * cix
        ln_after = r_upd + [p.now("tensor"), p.now("scalar")]
        if cix == 0:
            ln_after += yh_toks
        yt_toks = C.layernorm(
            [R[i].ap()[:, col0:col0 + 256] for i in range(NF)],
            [Yt[i].ap()[:, 3:259] for i in range(NF)], T=256, eps=1e-5,
            after=ln_after)
        nyt = []
        for i in range(NF):
            c1 = p.op("vector", lambda e, i=i: e.tensor_copy(
                out=Yt[i].ap()[:, 0:3], in_=YH.ap()[:, 3 * i:3 * (i + 1)]),
                after=[yt_toks[i]], inc=True)
            c2 = p.op("vector", lambda e, i=i: e.tensor_copy(
                out=YH.ap()[:, 3 * i:3 * (i + 1)],
                in_=Yt[i].ap()[:, 256:259]), after=[c1], inc=True)
            nyt.append(c2)
        yt_toks = nyt
        ev_toks = {}
        prev_ev = None
        for m in range(67):
            mw = 128 if m < 66 else 64
            s = m % 2
            dstw = wsl[s].ap()[:, 0:16 * mw].rearrange(
                "p (k c) -> p k c", k=16)
            lt = p.dma("gpsimd", dstw,
                       W["w_inT"][:, 128 * m:128 * m + mw]
                       .rearrange("(k p) c -> p k c", p=128),
                       after=[prev_wuse[s]], stream=f"wi{s}")
            last = None
            for k in range(16):
                deps = [lt, yt_toks[k]]
                if k == 0:
                    deps.append(prev_ev)
                    if m == 0:
                        deps.append(yt_toks[-1])
                last = p.op("tensor", lambda e, k=k, s=s, mw=mw: e.matmul(
                    psC.ap()[0:mw, 0:259],
                    wsl[s].ap()[:, mw * k:mw * (k + 1)],
                    Yt[k].ap()[:, 0:259],
                    start=(k == 0), stop=(k == 15)), after=deps,
                    inc=(k == 15))
            prev_wuse[s] = last
            if m < 32:
                ev = p.op("vector", lambda e, m=m: e.tensor_copy(
                    out=gate[m].ap(), in_=psC.ap()[:, 3:259]),
                    after=[last], inc=True)
            elif m < 64:
                ev = p.op("vector", lambda e, m=m: e.tensor_copy(
                    out=hid[m - 32].ap(), in_=psC.ap()[:, 0:259]),
                    after=[last], inc=True)
            elif m == 64:
                ev = p.op("vector", lambda e: e.tensor_copy(
                    out=Bt.ap(), in_=psC.ap()[:, 0:259]), after=[last],
                    inc=True)
            elif m == 65:
                ev = p.op("vector", lambda e: e.tensor_copy(
                    out=Ct.ap(), in_=psC.ap()[:, 0:259]), after=[last],
                    inc=True)
            else:
                ev = p.op("vector", lambda e: e.tensor_copy(
                    out=dt_raw.ap(), in_=psC.ap()[0:64, 3:259]),
                    after=[last], inc=True)
            ev_toks[m] = ev
            prev_ev = ev
        conv_toks = {}
        prev_silu = None
        for f in range(34):
            srcap = hid[f] if f < 32 else (Bt if f == 32 else Ct)
            t0 = p.op("vector", lambda e, f=f, srcap=srcap: e.tensor_scalar(
                out=acc.ap(), in0=srcap.ap()[:, 0:256],
                scalar1=cw.ap()[:, 4 * f:4 * f + 1],
                scalar2=cb.ap()[:, f:f + 1], op0=ALU.mult, op1=ALU.add),
                after=[ev_toks[f + 32], sm_tok, prev_silu], inc=True)
            for i in (1, 2, 3):
                t0 = p.op("vector", lambda e, f=f, i=i, srcap=srcap:
                    e.scalar_tensor_tensor(
                        out=acc.ap(), in0=srcap.ap()[:, i:256 + i],
                        scalar=cw.ap()[:, 4 * f + i:4 * f + i + 1],
                        in1=acc.ap(), op0=ALU.mult, op1=ALU.add),
                    after=[t0], inc=True)
            conv_toks[f] = p.op("scalar", lambda e, srcap=srcap: e.activation(
                out=srcap.ap()[:, 0:256], in_=acc.ap(), func=AF.Silu),
                after=[t0], inc=True)
            prev_silu = conv_toks[f]
        d1 = p.op("vector", lambda e: e.tensor_scalar_add(
            out=dt_t1.ap(), in0=dt_raw.ap(), scalar1=dtb.ap()),
            after=[ev_toks[66], sm_tok], inc=True)
        d2_ = p.op("vector", lambda e: e.tensor_scalar_max(
            out=dt_t2.ap(), in0=dt_t1.ap(), scalar1=0.0), after=[d1],
            inc=True)
        d3 = p.op("scalar", lambda e: e.activation(
            out=dt_t1.ap(), in_=dt_t1.ap(), func=AF.Abs), after=[d1, d2_],
            inc=True)
        d4 = p.op("scalar", lambda e: e.activation(
            out=dt_t1.ap(), in_=dt_t1.ap(), func=AF.Exp, scale=-1.0),
            after=[d3], inc=True)
        d5 = p.op("scalar", lambda e: e.activation(
            out=dt_t1.ap(), in_=dt_t1.ap(), func=AF.Ln, bias=1.0),
            after=[d4], inc=True)
        d6 = p.op("vector", lambda e: e.tensor_add(
            out=dtv.ap(), in0=dt_t2.ap(), in1=dt_t1.ap()),
            after=[d2_, d5], inc=True)
        at_tok = p.op("vector", lambda e: e.tensor_scalar_mul(
            out=At.ap(), in0=dtv.ap(), scalar1=acol.ap()),
            after=[d6, acol_tok], inc=True)
        atm_toks = []
        prev_t = Tok("evector", p.ecnt["vector"])
        for sc in range(2):
            tt = p.op("tensor", lambda e, sc=sc: e.transpose(
                psB.ap()[:, 0:64], At.ap()[0:64, 128 * sc:128 * (sc + 1)],
                C.ident_f.ap()[0:64, 0:64]),
                after=[at_tok, C.ident_tok, prev_t], inc=True)
            ev = p.op("vector", lambda e, sc=sc: e.tensor_copy(
                out=atm.ap()[:, 64 * sc:64 * (sc + 1)],
                in_=psB.ap()[:, 0:64]), after=[tt], inc=True)
            prev_t = ev
            atm_toks.append(ev)
        mm = None
        for sc in range(2):
            mm = p.op("tensor", lambda e, sc=sc: e.matmul(
                psA.ap()[0:64, 0:256], atm.ap()[:, 64 * sc:64 * (sc + 1)],
                ltc[sc].ap(), start=(sc == 0), stop=(sc == 1)),
                after=[atm_toks[sc], sm_tok, p.now("vector")], inc=(sc == 1))
        acum_tok = p.op("vector", lambda e: e.tensor_copy(
            out=acum.ap(), in_=psA.ap()[0:64, 0:256]), after=[mm], inc=True)
        nac_toks = []
        for lc in range(2):
            mm2 = None
            for sc in range(2):
                mm2 = p.op("tensor", lambda e, sc=sc, lc=lc: e.matmul(
                    psB.ap()[:, 0:64],
                    ltc[sc].ap()[:, 128 * lc:128 * (lc + 1)],
                    atm.ap()[:, 64 * sc:64 * (sc + 1)],
                    start=(sc == 0), stop=(sc == 1)),
                    after=[atm_toks[sc], prev_t], inc=(sc == 1))
            ev = p.op("vector", lambda e, lc=lc: e.tensor_scalar_mul(
                out=nacumT.ap()[:, 64 * lc:64 * (lc + 1)],
                in0=psB.ap()[:, 0:64], scalar1=-1.0), after=[mm2], inc=True)
            prev_t = ev
            nac_toks.append(ev)
        dk = p.op("scalar", lambda e: e.activation(
            out=dkf.ap(), in_=acum.ap(), func=AF.Exp, scale=-1.0,
            bias=acum.ap()[:, 255:256]), after=[acum_tok], inc=True)
        sd = p.op("scalar", lambda e: e.activation(
            out=sdf.ap(), in_=acum.ap(), func=AF.Exp), after=[acum_tok],
            inc=True)
        dkT_toks, sdT_toks, dtvT_toks = [], [], []
        for sc in range(2):
            tt = p.op("tensor", lambda e, sc=sc: e.transpose(
                psB.ap()[:, 0:64], dkf.ap()[0:64, 128 * sc:128 * (sc + 1)],
                C.ident_f.ap()[0:64, 0:64]), after=[dk, prev_t], inc=True)
            ev = p.op("vector", lambda e, sc=sc: e.tensor_copy(
                out=dkT.ap()[:, 64 * sc:64 * (sc + 1)],
                in_=psB.ap()[:, 0:64]), after=[tt], inc=True)
            prev_t = ev
            dkT_toks.append(ev)
            tt = p.op("tensor", lambda e, sc=sc: e.transpose(
                psB.ap()[:, 0:64], sdf.ap()[0:64, 128 * sc:128 * (sc + 1)],
                C.ident_f.ap()[0:64, 0:64]), after=[sd, prev_t], inc=True)
            ev = p.op("vector", lambda e, sc=sc: e.tensor_copy(
                out=sdT.ap()[:, 64 * sc:64 * (sc + 1)],
                in_=psB.ap()[:, 0:64]), after=[tt], inc=True)
            prev_t = ev
            sdT_toks.append(ev)
            tt = p.op("tensor", lambda e, sc=sc: e.transpose(
                psB.ap()[:, 0:64], dtv.ap()[0:64, 128 * sc:128 * (sc + 1)],
                C.ident_f.ap()[0:64, 0:64]), after=[d6, prev_t], inc=True)
            ev = p.op("vector", lambda e, sc=sc: e.tensor_copy(
                out=dtvT.ap()[:, 64 * sc:64 * (sc + 1)],
                in_=psB.ap()[:, 0:64]), after=[tt], inc=True)
            prev_t = ev
            dtvT_toks.append(ev)
        dtvT_tok = dtvT_toks[-1]
        hdt_toks = {}
        for f in range(32):
            for sc in range(2):
                tt = p.op("tensor", lambda e, f=f, sc=sc: e.transpose(
                    PSTB.ap()[:, 0:128],
                    hid[f].ap()[:, 128 * sc:128 * (sc + 1)],
                    C.ident_b.ap()),
                    after=[conv_toks[f], prev_t, C.ident_tok], inc=True)
                evs = []
                for hh in range(2):
                    h = 2 * f + hh
                    evs.append(p.op("vector",
                        lambda e, f=f, sc=sc, hh=hh, h=h:
                        e.tensor_scalar_mul(
                            out=hdtT[sc].ap()[:, 128 * f + 64 * hh:
                                              128 * f + 64 * (hh + 1)],
                            in0=PSTB.ap()[:, 64 * hh:64 * (hh + 1)],
                            scalar1=dtvT.ap()[:, 64 * sc + h:
                                              64 * sc + h + 1]
                            if h < 64 else None),
                        after=[tt, dtvT_tok], inc=True))
                prev_t = evs[-1]
                hdt_toks[(f, sc)] = evs
        btm_toks = []
        for sc in range(2):
            tt = p.op("tensor", lambda e, sc=sc: e.transpose(
                PSTB.ap()[:, 128:256], Bt.ap()[:, 128 * sc:128 * (sc + 1)],
                C.ident_b.ap()), after=[conv_toks[32], prev_t, C.ident_tok],
                inc=True)
            ev = p.op("vector", lambda e, sc=sc: e.tensor_copy(
                out=btm.ap()[:, 128 * sc:128 * (sc + 1)],
                in_=PSTB.ap()[:, 128:256]), after=[tt], inc=True)
            prev_t = ev
            btm_toks.append(ev)
        gm_toks = []
        prev_gm = p.now("vector")
        for sc in range(2):
            mmg = p.op("tensor", lambda e, sc=sc: e.matmul(
                psB.ap()[0:128, 0:256], Bt.ap()[:, 128 * sc:128 * (sc + 1)],
                Ct.ap()[:, 0:256], start=True, stop=True),
                after=[conv_toks[32], conv_toks[33], prev_gm], inc=True)
            ev = p.op("vector", lambda e, sc=sc: e.tensor_mul(
                out=gmm[sc].ap(), in0=psB.ap()[0:128, 0:256],
                in1=mkc[sc].ap()), after=[mmg, sm_tok], inc=True)
            prev_gm = ev
            gm_toks.append(ev)
        y_toks = {}
        prev_grp_readers = [p.now("vector"), p.now("scalar")]
        prev_yd = None
        prev_st = None
        prev_yo = None
        for grp in range(32):
            h0 = 2 * grp
            mmd = None
            for j in range(2):
                h = h0 + j
                mmd = p.op("tensor", lambda e, h=h, j=j: e.matmul(
                    PSD2.ap()[:, 256 * j:256 * (j + 1)],
                    C.ident_f.ap()[0:64, h:h + 1].broadcast_to([64, 128]),
                    acum.ap(), start=True, stop=True),
                    after=[acum_tok, C.ident_tok] + prev_grp_readers,
                    inc=True)
            pmts = []
            for sc in range(2):
                bm_toks = []
                for j in range(2):
                    h = h0 + j
                    bm_toks.append(p.op("vector",
                        lambda e, sc=sc, j=j, h=h: e.tensor_scalar(
                            out=d2sb.ap()[:, 256 * j:256 * (j + 1)],
                            in0=PSD2.ap()[:, 256 * j:256 * (j + 1)],
                            scalar1=nacumT.ap()[:, 64 * sc + h:
                                                64 * sc + h + 1]
                            if h < 64 else None,
                            scalar2=0.0, op0=ALU.add, op1=ALU.min),
                        after=[mmd, nac_toks[sc]], inc=True))
                ex = p.op("scalar", lambda e: e.activation(
                    out=mtsb.ap(), in_=d2sb.ap(), func=AF.Exp),
                    after=bm_toks, inc=True)
                pm = p.op("vector", lambda e, sc=sc: e.tensor_mul(
                    out=pmt[sc].ap(), in0=mtsb.ap(),
                    in1=gmm[sc].ap()[:, None, :].broadcast_to([128, 2, 256])),
                    after=[ex, gm_toks[sc]], inc=True)
                pmts.append(pm)
            for j in range(2):
                h = h0 + j
                f, hh = h // 2, h % 2
                for sc in range(2):
                    p.op("tensor", lambda e, sc=sc, f=f, hh=hh, j=j:
                        e.matmul(
                            PSY.ap()[0:64, 0:256],
                            hdtT[sc].ap()[:, 128 * f + 64 * hh:
                                          128 * f + 64 * (hh + 1)],
                            pmt[sc].ap()[:, 256 * j:256 * (j + 1)],
                            start=(sc == 0), stop=False),
                        after=[pmts[sc], hdt_toks[(f, sc)][hh], prev_yd],
                        inc=False)
                bd_toks = []
                for sc in range(2):
                    bd_toks.append(p.op("vector",
                        lambda e, sc=sc, h=h: e.tensor_scalar_mul(
                            out=bdtm.ap()[:, 128 * sc:128 * (sc + 1)],
                            in0=btm.ap()[:, 128 * sc:128 * (sc + 1)],
                            scalar1=dkT.ap()[:, 64 * sc + h:64 * sc + h + 1]
                            if h < 64 else None),
                        after=[btm_toks[sc], dkT_toks[sc], prev_st,
                               prev_yd], inc=True))
                mms = None
                for sc in range(2):
                    mms = p.op("tensor", lambda e, sc=sc, f=f, hh=hh:
                        e.matmul(
                            PSY.ap()[0:128, 512:576],
                            bdtm.ap()[:, 128 * sc:128 * (sc + 1)],
                            hdtT[sc].ap()[:, 128 * f + 64 * hh:
                                          128 * f + 64 * (hh + 1)],
                            start=(sc == 0), stop=(sc == 1)),
                        after=[bd_toks[sc], hdt_toks[(f, sc)][hh]],
                        inc=(sc == 1))
                st_ev = p.op("vector", lambda e: e.tensor_copy(
                    out=stsb.ap(), in_=PSY.ap()[0:128, 512:576]),
                    after=[mms], inc=True)
                prev_st = st_ev
                yo_evs = []
                for lc in range(2):
                    mmo = p.op("tensor", lambda e, lc=lc: e.matmul(
                        PSY.ap()[0:128, 576:640],
                        Ct.ap()[:, 128 * lc:128 * (lc + 1)],
                        stsb.ap(), start=True, stop=True),
                        after=[st_ev, conv_toks[33], prev_yo], inc=True)
                    yo_evs.append(p.op("vector",
                        lambda e, lc=lc, h=h: e.tensor_scalar_mul(
                            out=yosb.ap()[:, 64 * lc:64 * (lc + 1)],
                            in0=PSY.ap()[0:128, 576:640],
                            scalar1=sdT.ap()[:, 64 * lc + h:64 * lc + h + 1]
                            if h < 64 else None),
                        after=[mmo, sdT_toks[lc]], inc=True))
                    prev_yo = yo_evs[-1]
                for lc in range(2):
                    p.op("tensor", lambda e, lc=lc: e.matmul(
                        PSY.ap()[0:64, 128 * lc:128 * (lc + 1)],
                        yosb.ap()[:, 64 * lc:64 * (lc + 1)],
                        C.ident_f.ap(), start=False, stop=(lc == 1),
                        is_transpose=True),
                        after=[yo_evs[lc], C.ident_tok], inc=(lc == 1))
                yev = p.op("vector", lambda e, f=f, hh=hh:
                    e.tensor_copy(
                        out=ysb.ap()[64 * hh:64 * (hh + 1),
                                     256 * f:256 * (f + 1)],
                        in_=PSY.ap()[0:64, 0:256]),
                    after=[p.now("tensor")], inc=True)
                prev_yd = yev
                y_toks[h] = yev
            prev_grp_readers = [p.now("vector"), p.now("scalar")]
        yf_toks = []
        for f in range(32):
            sil_tok = p.op("scalar", lambda e, f=f: e.activation(
                out=gate[f].ap(), in_=gate[f].ap(), func=AF.Silu),
                after=[ev_toks[f], y_toks[2 * f + 1]], inc=True)
            t1_ = p.op("vector", lambda e, f=f: e.scalar_tensor_tensor(
                out=ysb.ap()[:, 256 * f:256 * (f + 1)],
                in0=hid[f].ap()[:, 0:256], scalar=dfm.ap()[:, f:f + 1],
                in1=ysb.ap()[:, 256 * f:256 * (f + 1)],
                op0=ALU.mult, op1=ALU.add),
                after=[conv_toks[f], y_toks[2 * f], y_toks[2 * f + 1],
                       sm_tok], inc=True)
            yf_toks.append(p.op("vector", lambda e, f=f: e.tensor_mul(
                out=ysb.ap()[:, 256 * f:256 * (f + 1)],
                in0=ysb.ap()[:, 256 * f:256 * (f + 1)],
                in1=gate[f].ap()), after=[t1_, sil_tok], inc=True))
        sqs = None
        for f in range(32):
            tq = p.op("scalar", lambda e, f=f: e.activation(
                out=C.sq.ap()[:, 0:256],
                in_=ysb.ap()[:, 256 * f:256 * (f + 1)],
                func=AF.Square), after=[yf_toks[f], sqs], inc=True)
            sqs = p.op("tensor", lambda e, f=f: e.matmul(
                psA.ap()[0:1, 512:768], C.onescol.ap(), C.sq.ap()[:, 0:256],
                start=(f == 0), stop=(f == 31)),
                after=[tq, C.const_tok,
                       acum_tok if f == 0 else None], inc=True)
        v1 = p.op("scalar", lambda e: e.activation(
            out=C.r_v.ap()[0:1, 0:256], in_=psA.ap()[0:1, 512:768],
            func=AF.Ln, bias=C.constf(1e-5 * 4096), scale=1.0),
            after=[sqs], inc=True)
        v2 = p.op("scalar", lambda e: e.activation(
            out=C.r_v.ap()[0:1, 0:256], in_=C.r_v.ap()[0:1, 0:256],
            func=AF.Exp, scale=-0.5), after=[v1], inc=True)
        v3 = p.op("vector", lambda e: e.tensor_scalar_mul(
            out=C.r_v.ap()[0:1, 0:256], in0=C.r_v.ap()[0:1, 0:256],
            scalar1=64.0), after=[v2], inc=True)
        bmv = p.op("tensor", lambda e: e.matmul(
            psB.ap()[:, 0:256], C.onesrow_f.ap(), C.r_v.ap()[0:1, 0:256],
            start=True, stop=True), after=[v3, p.now("vector")], inc=True)
        yn_toks = []
        for f in range(32):
            yn_toks.append(p.op("vector", lambda e, f=f:
                e.scalar_tensor_tensor(
                    out=ysb.ap()[:, 256 * f:256 * (f + 1)],
                    in0=ysb.ap()[:, 256 * f:256 * (f + 1)],
                    scalar=nw.ap()[:, f:f + 1], in1=psB.ap()[:, 0:256],
                    op0=ALU.mult, op1=ALU.mult),
                after=[bmv, yf_toks[f], sm_tok], inc=True))
        prev_rupd = None
        for m in range(16):
            s = m % 2
            dstw = wo[s].ap().rearrange("p (k c) -> p k c", k=32)
            lt = p.dma("gpsimd", dstw,
                       W["w_outT"][:, 128 * m:128 * (m + 1)]
                       .rearrange("(k p) c -> p k c", p=128),
                       after=[prev_wouse[s]], stream=f"wo{s}")
            last = None
            for k in range(32):
                deps = [lt, yn_toks[k]]
                if k == 0:
                    deps.append(r_upd[m])
                    deps.append(prev_rupd)
                    if m < 2:
                        deps.append(ev_toks[66])
                last = p.op("tensor", lambda e, k=k, s=s: e.matmul(
                    psC.ap()[:, 0:256], wo[s].ap()[:, 128 * k:128 * (k + 1)],
                    ysb.ap()[:, 256 * k:256 * (k + 1)],
                    start=(k == 0), stop=(k == 31)), after=deps,
                    inc=(k == 31))
            prev_wouse[s] = last
            r_upd[m] = p.op("vector", lambda e, m=m, cix=cix: e.tensor_add(
                out=R[m].ap()[:, 3 + 256 * cix:3 + 256 * (cix + 1)],
                in0=psC.ap()[:, 0:256],
                in1=R[m].ap()[:, 3 + 256 * cix:3 + 256 * (cix + 1)]),
                after=[last], inc=True)
            prev_rupd = r_upd[m]

    for i in range(NF):
        p.dma("sync", xo_d[128 * i:128 * (i + 1), :], R[i].ap()[:, 3:1027],
              after=[r_upd[i]], stream="qout")
    p.build()
    return nc




def build_a1a_launch():
    nc = bass.Bass("TRN2", target_bir_lowering=False, debug=False)
    x_d = nc.dram_tensor("x", [2048, 2048], F32, kind="ExternalInput").ap()
    wq_d = nc.dram_tensor("wqT", [2048, 1024], BF16,
                          kind="ExternalInput").ap()
    wk_d = nc.dram_tensor("wkT", [2048, 1024], BF16,
                          kind="ExternalInput").ap()
    wv_d = nc.dram_tensor("wvT", [2048, 1024], BF16,
                          kind="ExternalInput").ap()
    sl_d = nc.dram_tensor("slopes8", [1, 8], F32, kind="ExternalInput").ap()
    oraw_d = nc.dram_tensor("oraw", [1024, 2048], BF16,
                            kind="ExternalOutput").ap()
    sqp_d = nc.dram_tensor("sqp", [1, 2048], F32, kind="ExternalOutput").ap()
    ar0 = nc.inline_tensor(np.arange(256, dtype=np.float32)[None, :],
                           name="ar0")
    ar1 = nc.inline_tensor((np.arange(256, dtype=np.float32) + 1)[None, :],
                           name="ar1")
    onesrow256 = nc.inline_tensor(np.ones((1, 256), np.float32), name="o256")
    mknp = []
    for sc in range(2):
        m = (np.arange(256)[None, :] >= (np.arange(128)[:, None] + 128 * sc)
             ).astype(BF)
        mknp.append(nc.inline_tensor(np.ascontiguousarray(m),
                                     name=f"a1mk{sc}"))

    p = Plan(nc)
    C = Common(nc, p, 256, ps_shapes=(512, 512, 512, 512))
    PS = C.PS
    PSTB = nc.alloc_psum_tensor("b_pstb", [128, 512], BF16)

    NF = 16
    Y = [nc.alloc_sbuf_tensor(f"Y{i}", [128, 2048], BF16) for i in range(NF)]
    xblk = [nc.alloc_sbuf_tensor(f"b_x{i}", [128, 256], BF16)
            for i in range(NF)]
    qf = [nc.alloc_sbuf_tensor(f"b_q{i}", [128, 2048], BF16)
          for i in range(8)]
    kf = [nc.alloc_sbuf_tensor(f"b_k{i}", [128, 2048], BF16)
          for i in range(8)]
    vt = [nc.alloc_sbuf_tensor(f"b_v{i}", [128, 1024], BF16)
          for i in range(16)]
    OUT = Y[0:8]
    wsl = [nc.alloc_sbuf_tensor(f"b_wsl{i}", [128, 2048], BF16)
           for i in range(2)]
    dd = [nc.alloc_sbuf_tensor(f"b_dd{i}", [128, 8 * 256], BF16)
          for i in range(2)]
    qd = nc.alloc_sbuf_tensor("b_qd", [128, 8 * 256], BF16)
    kdT = nc.alloc_sbuf_tensor("b_kdT", [128, 16], F32)
    bdc = nc.alloc_sbuf_tensor("b_bdc", [128, 8], F32)
    s8 = nc.alloc_sbuf_tensor("b_s8", [1, 8], F32)
    srow = nc.alloc_sbuf_tensor("b_srow", [1, 256], F32)
    nsrow = nc.alloc_sbuf_tensor("b_nsrow", [1, 256], F32)
    ar0s = nc.alloc_sbuf_tensor("b_ar0", [1, 256], F32)
    ar1s = nc.alloc_sbuf_tensor("b_ar1", [1, 256], F32)
    ones256 = nc.alloc_sbuf_tensor("b_o256", [1, 256], F32)
    l2 = nc.alloc_sbuf_tensor("b_l2", [2, 256], F32)
    r2 = nc.alloc_sbuf_tensor("b_r2", [2, 256], F32)
    mks = [nc.alloc_sbuf_tensor(f"b_mk{i}", [128, 256], BF16)
           for i in range(2)]
    ddscr = nc.alloc_sbuf_tensor("b_ddscr", [128, 256], F32)
    kv = nc.alloc_sbuf_tensor("b_kv", [128, 128], F32)
    kvb = nc.alloc_sbuf_tensor("b_kvb", [128, 128], BF16)
    qs_sb = nc.alloc_sbuf_tensor("b_qs", [128, 256], BF16)
    p_sb = nc.alloc_sbuf_tensor("b_p", [128, 512], BF16)
    kst = nc.alloc_sbuf_tensor("b_kst", [128, 256], BF16)
    sqp = nc.alloc_sbuf_tensor("b_sqp", [1, 512], F32)

    sm = [p.dma("sync", s8.ap(), sl_d, stream="qsm"),
          p.dma("sync", ar0s.ap(), ar0.ap(), stream="qsm"),
          p.dma("sync", ar1s.ap(), ar1.ap(), stream="qsm"),
          p.dma("sync", ones256.ap(), onesrow256.ap(), stream="qsm"),
          p.dma("sync", mks[0].ap(), mknp[0].ap(), stream="qsm"),
          p.dma("sync", mks[1].ap(), mknp[1].ap(), stream="qsm"),
          p.dma("sync", l2.ap()[1:2, :], onesrow256.ap(), stream="qsm"),
          p.dma("sync", r2.ap()[0:1, :], onesrow256.ap(), stream="qsm")]
    sm_tok = sm[-1]

    y_toks = []
    prev_blk = [None] * NF
    for blk in range(8):
        xt = [p.dma("gpsimd", xblk[i].ap(),
                    x_d[128 * i:128 * (i + 1), 256 * blk:256 * (blk + 1)],
                    after=[prev_blk[i]], stream=f"qx{i % 4}")
              for i in range(NF)]
        toks = C.layernorm(
            [xblk[i].ap() for i in range(NF)],
            [Y[i].ap()[:, 256 * blk:256 * (blk + 1)] for i in range(NF)],
            T=256, eps=1e-5, after=xt)
        for i in range(NF):
            prev_blk[i] = toks[i]
        y_toks.append(toks)
    yall = [y_toks[7][i] for i in range(NF)]

    stoks = []
    prev_ps3 = Tok("evector", p.ecnt["vector"])
    prev_mmr = None
    for h in range(8):
        t1 = p.op("vector", lambda e, h=h: e.tensor_scalar_mul(
            out=srow.ap(), in0=ar1s.ap(), scalar1=s8.ap()[0:1, h:h + 1]),
            after=[sm_tok], inc=True)
        t2 = p.op("scalar", lambda e: e.activation(
            out=srow.ap(), in_=srow.ap(), func=AF.Exp, scale=-1.0),
            after=[t1], inc=True)
        bm = p.op("tensor", lambda e: e.matmul(
            PS[3].ap()[:, 0:256], C.onesrow_f.ap(), srow.ap(),
            start=True, stop=True), after=[t2, prev_ps3, C.const_tok],
            inc=True)
        ev = p.op("vector", lambda e, h=h: e.tensor_copy(
            out=qd.ap()[:, 256 * h:256 * (h + 1)], in_=PS[3].ap()[:, 0:256]),
            after=[bm], inc=True)
        prev_ps3 = ev
        t3 = p.op("vector", lambda e, h=h: e.tensor_scalar(
            out=nsrow.ap(), in0=ar0s.ap(), scalar1=-1.0, scalar2=255.0,
            op0=ALU.mult, op1=ALU.add), after=[sm_tok], inc=True)
        t4 = p.op("vector", lambda e, h=h: e.tensor_scalar_mul(
            out=nsrow.ap(), in0=nsrow.ap(), scalar1=s8.ap()[0:1, h:h + 1]),
            after=[t3], inc=True)
        t5 = p.op("scalar", lambda e: e.activation(
            out=nsrow.ap(), in_=nsrow.ap(), func=AF.Exp, scale=-1.0),
            after=[t4], inc=True)
        for sc2 in range(2):
            tt = p.op("tensor", lambda e, sc2=sc2: e.transpose(
                PS[3].ap()[:, 256:257],
                nsrow.ap()[0:1, 128 * sc2:128 * (sc2 + 1)],
                C.ident_f.ap()[0:1, 0:1]),
                after=[t5, prev_ps3, C.ident_tok], inc=True)
            ev = p.op("vector", lambda e, h=h, sc2=sc2: e.tensor_copy(
                out=kdT.ap()[:, 2 * h + sc2:2 * h + sc2 + 1],
                in_=PS[3].ap()[:, 256:257]), after=[tt], inc=True)
            prev_ps3 = ev
        tn = p.op("vector", lambda e, h=h: e.tensor_scalar_mul(
            out=nsrow.ap(), in0=ar0s.ap(), scalar1=s8.ap()[0:1, h:h + 1]),
            after=[prev_ps3, prev_mmr], inc=True)
        tneg = p.op("vector", lambda e: e.tensor_scalar_mul(
            out=srow.ap(), in0=nsrow.ap(), scalar1=-1.0), after=[tn],
            inc=True)
        c1 = p.op("vector", lambda e: e.tensor_copy(
            out=l2.ap()[0:1, :], in_=nsrow.ap()), after=[tneg], inc=True)
        cdma = p.dma("sync", r2.ap()[1:2, :], srow.ap(),
                     after=[tneg, prev_mmr], stream="qr2")
        for sc2 in range(2):
            mmr = p.op("tensor", lambda e, sc2=sc2: e.matmul(
                PS[3].ap()[:, 0:256],
                l2.ap()[:, 128 * sc2:128 * (sc2 + 1)], r2.ap(),
                start=True, stop=True), after=[c1, cdma, sm_tok, prev_ps3],
                inc=True)
            prev_mmr = mmr
            tb = p.op("vector", lambda e: e.tensor_scalar_min(
                out=ddscr.ap(), in0=PS[3].ap()[:, 0:256], scalar1=0.0),
                after=[mmr], inc=True)
            prev_ps3 = tb
            te = p.op("scalar", lambda e: e.activation(
                out=ddscr.ap(), in_=ddscr.ap(), func=AF.Exp), after=[tb],
                inc=True)
            stoks.append(p.op("vector", lambda e, h=h, sc2=sc2: e.tensor_mul(
                out=dd[sc2].ap()[:, 256 * h:256 * (h + 1)],
                in0=ddscr.ap(), in1=mks[sc2].ap()), after=[te, sm_tok],
                inc=True))
    t1 = p.op("vector", lambda e: e.tensor_scalar_mul(
        out=srow.ap()[0:1, 0:8], in0=s8.ap(), scalar1=-256.0),
        after=[sm_tok, prev_mmr], inc=True)
    t2 = p.op("scalar", lambda e: e.activation(
        out=srow.ap()[0:1, 0:8], in_=srow.ap()[0:1, 0:8], func=AF.Exp),
        after=[t1], inc=True)
    bmv = p.op("tensor", lambda e: e.matmul(
        PS[3].ap()[:, 0:8], C.onesrow_f.ap(), srow.ap()[0:1, 0:8],
        start=True, stop=True), after=[t2, prev_ps3, C.const_tok], inc=True)
    bd_tok = p.op("vector", lambda e: e.tensor_copy(
        out=bdc.ap(), in_=PS[3].ap()[:, 0:8]), after=[bmv], inc=True)
    prev_ps3 = bd_tok
    dd_tok = stoks[-1]

    prev_w = [None, None]
    v_toks = [None] * 16
    wv_tok = []
    for k in range(16):
        qb, off = qf[k % 8], 1024 * (k // 8)
        wv_tok.append(p.dma("gpsimd",
                      qb.ap()[:, off:off + 1024],
                      wv_d[128 * k:128 * (k + 1), :], stream="qwv"))
    wv_all = wv_tok[-1]
    for t in range(16):
        last = None
        for k in range(16):
            qb, off = qf[k % 8], 1024 * (k // 8)
            for n0 in (0, 512):
                reg = PS[0] if n0 == 0 else PS[1]
                deps = [wv_all, yall[k]]
                if k == 0 and t > 0:
                    deps.append(v_toks[t - 1])
                last = p.op("tensor",
                    lambda e, k=k, n0=n0, t=t, qb=qb, off=off, reg=reg:
                    e.matmul(
                        reg.ap()[:, 0:512],
                        Y[k].ap()[:, 128 * t:128 * (t + 1)],
                        qb.ap()[:, off + n0:off + n0 + 512],
                        start=(k == 0), stop=(k == 15)),
                    after=deps, inc=(k == 15))
        e1 = p.op("scalar", lambda e, t=t: e.activation(
            out=vt[t].ap()[:, 0:512], in_=PS[0].ap()[:, 0:512],
            func=AF.Silu), after=[last], inc=True)
        v_toks[t] = p.op("scalar", lambda e, t=t: e.activation(
            out=vt[t].ap()[:, 512:1024], in_=PS[1].ap()[:, 0:512],
            func=AF.Silu), after=[e1], inc=True)
    v_all = v_toks[-1]

    q_toks, k_toks = [None] * 8, [None] * 8
    prev_ev = v_toks[15]
    for which, dst, wd, toks in (("q", qf, wq_d, q_toks),
                                 ("k", kf, wk_d, k_toks)):
        for m in range(8):
            s = m % 2
            dstw = wsl[s].ap().rearrange("p (k c) -> p k c", k=16)
            wdeps = [prev_w[s]]
            if which == "q":
                wdeps.append(v_all)
            lt = p.dma("gpsimd", dstw,
                       wd[:, 128 * m:128 * (m + 1)]
                       .rearrange("(k p) c -> p k c", p=128),
                       after=wdeps, stream=f"w{s}")
            for tg in range(4):
                last = None
                for k in range(16):
                    deps = [lt, yall[k]]
                    if k == 0:
                        deps.append(prev_ev)
                    last = p.op("tensor", lambda e, k=k, tg=tg, s=s:
                        e.matmul(
                            PS[0].ap()[:, 0:512],
                            wsl[s].ap()[:, 128 * k:128 * (k + 1)],
                            Y[k].ap()[:, 512 * tg:512 * (tg + 1)],
                            start=(k == 0), stop=(k == 15)),
                        after=deps, inc=(k == 15))
                prev_ev = p.op("scalar", lambda e, m=m, tg=tg, dst=dst:
                    e.activation(
                        out=dst[m].ap()[:, 512 * tg:512 * (tg + 1)],
                        in_=PS[0].ap()[:, 0:512], func=AF.Silu),
                    after=[last], inc=True)
            prev_w[s] = last
            toks[m] = prev_ev

    out_toks = {}
    prev_sc = k_toks[7]
    prev_o = v_toks[15]
    prev_kvps = None
    for h in range(8):
        kv_tok = None
        kvb_tok = None
        for blk in range(8):
            m0 = 256 * blk
            p_toks = []
            for nch in range(2):
                mm = p.op("tensor", lambda e, h=h, nch=nch, m0=m0: e.matmul(
                    PS[0].ap()[:, 0:256],
                    kf[h].ap()[:, m0 + 128 * nch:m0 + 128 * (nch + 1)],
                    qf[h].ap()[:, m0:m0 + 256],
                    start=True, stop=True),
                    after=[q_toks[h], k_toks[h], prev_sc], inc=True)
                pt = p.op("vector", lambda e, h=h, nch=nch: e.tensor_mul(
                    out=p_sb.ap()[:, 256 * nch:256 * (nch + 1)],
                    in0=PS[0].ap()[:, 0:256],
                    in1=dd[nch].ap()[:, 256 * h:256 * (h + 1)]),
                    after=[mm, dd_tok], inc=True)
                prev_sc = pt
                p_toks.append(pt)
            qs_tok = p.op("vector", lambda e, h=h, m0=m0: e.tensor_mul(
                out=qs_sb.ap(), in0=qf[h].ap()[:, m0:m0 + 256],
                in1=qd.ap()[:, 256 * h:256 * (h + 1)]),
                after=[q_toks[h], dd_tok, prev_o], inc=True)
            last = None
            for nch in range(2):
                tch = 2 * blk + nch
                last = p.op("tensor", lambda e, h=h, nch=nch, tch=tch,
                            blk=blk: e.matmul(
                        PS[1].ap()[:, 0:256],
                        vt[tch].ap()[:, 128 * h:128 * (h + 1)],
                        p_sb.ap()[:, 256 * nch:256 * (nch + 1)],
                        start=(nch == 0),
                        stop=(nch == 1 and blk == 0)),
                    after=[p_toks[nch], v_toks[tch], prev_o],
                    inc=(nch == 1 and blk == 0))
            if blk > 0:
                last = p.op("tensor", lambda e, h=h: e.matmul(
                    PS[1].ap()[:, 0:256], kvb.ap(), qs_sb.ap(),
                    start=False, stop=True),
                    after=[qs_tok, kvb_tok], inc=True)
            oev_deps = [last]
            if blk == 0:
                oev_deps += [q_toks[7], k_toks[7]]
            oev = p.op("vector", lambda e, h=h, m0=m0: e.tensor_copy(
                out=OUT[h].ap()[:, m0:m0 + 256], in_=PS[1].ap()[:, 0:256]),
                after=oev_deps, inc=True)
            prev_o = oev
            out_toks[(h, blk)] = oev
            if blk < 7:
                kst_toks = []
                for nch in range(2):
                    tt = p.op("tensor", lambda e, h=h, nch=nch, m0=m0:
                        e.transpose(
                            PSTB.ap()[:, 0:128],
                            kf[h].ap()[:, m0 + 128 * nch:
                                       m0 + 128 * (nch + 1)],
                            C.ident_b.ap()),
                        after=[k_toks[h], C.ident_tok,
                               kst_toks[-1] if kst_toks else prev_kvps],
                        inc=True)
                    kst_toks.append(p.op("vector",
                        lambda e, h=h, nch=nch: e.tensor_scalar_mul(
                            out=kst.ap()[:, 128 * nch:128 * (nch + 1)],
                            in0=PSTB.ap()[:, 0:128],
                            scalar1=kdT.ap()[:, 2 * h + nch:
                                             2 * h + nch + 1]),
                        after=[tt], inc=True))
                mmk = None
                for nch in range(2):
                    tch = 2 * blk + nch
                    mmk = p.op("tensor", lambda e, h=h, nch=nch, tch=tch:
                        e.matmul(
                            PS[2].ap()[:, 0:128],
                            kst.ap()[:, 128 * nch:128 * (nch + 1)],
                            vt[tch].ap()[:, 128 * h:128 * (h + 1)],
                            start=(nch == 0), stop=(nch == 1)),
                        after=[kst_toks[nch], v_toks[tch], prev_kvps],
                        inc=(nch == 1))
                if blk == 0:
                    kv_tok = p.op("vector", lambda e: e.tensor_copy(
                        out=kv.ap(), in_=PS[2].ap()[:, 0:128]),
                        after=[mmk], inc=True)
                else:
                    kv_tok = p.op("vector", lambda e, h=h:
                        e.scalar_tensor_tensor(
                            out=kv.ap(), in0=kv.ap(),
                            scalar=bdc.ap()[:, h:h + 1],
                            in1=PS[2].ap()[:, 0:128],
                            op0=ALU.mult, op1=ALU.add),
                        after=[mmk, bd_tok, kv_tok], inc=True)
                prev_kvps = kv_tok
                kvb_tok = p.op("vector", lambda e: e.tensor_copy(
                    out=kvb.ap(), in_=kv.ap()), after=[kv_tok], inc=True)

    sq_toks = []
    prev_mq = None
    for tg in range(4):
        last = None
        for h in range(8):
            tq = p.op("scalar", lambda e, h=h, tg=tg: e.activation(
                out=C.xb.ap()[:, 0:256],
                in_=OUT[h].ap()[:, 512 * tg:512 * tg + 256],
                func=AF.Square),
                after=[out_toks[(h, 7)], last, prev_mq], inc=True)
            tq2 = p.op("scalar", lambda e, h=h, tg=tg: e.activation(
                out=C.sq.ap()[:, 0:256],
                in_=OUT[h].ap()[:, 512 * tg + 256:512 * (tg + 1)],
                func=AF.Square), after=[tq], inc=True)
            mq = p.op("tensor", lambda e, h=h, tg=tg: e.matmul(
                PS[3].ap()[0:1, 0:256], C.onescol.ap(),
                C.xb.ap()[:, 0:256],
                start=(h == 0), stop=(h == 7)),
                after=[tq, C.const_tok], inc=True)
            prev_mq = mq
            mq2 = p.op("tensor", lambda e, h=h, tg=tg: e.matmul(
                PS[2].ap()[0:1, 0:256], C.onescol.ap(),
                C.sq.ap()[:, 0:256],
                start=(h == 0), stop=(h == 7)),
                after=[tq2, C.const_tok], inc=True)
            last = mq2
        ev0 = p.op("vector", lambda e, tg=tg: e.tensor_copy(
            out=sqp.ap()[0:1, 0:256],
            in_=PS[3].ap()[0:1, 0:256]),
            after=[last, sq_toks[-1] if sq_toks else None], inc=True)
        ev = p.op("vector", lambda e, tg=tg: e.tensor_copy(
            out=sqp.ap()[0:1, 256:512],
            in_=PS[2].ap()[0:1, 0:256]), after=[ev0], inc=True)
        prev_mq = ev
        sq_toks.append(p.dma("sync", sqp_d[0:1, 512 * tg:512 * (tg + 1)],
                             sqp.ap(), after=[ev], stream="qsq"))

    for h in range(8):
        p.dma("sync", oraw_d[128 * h:128 * (h + 1), :], OUT[h].ap(),
              after=[out_toks[(h, 7)], p.now("vector"), p.now("scalar")],
              stream="qout")
    p.build()
    return nc


def build_a1b_launch():
    nc = bass.Bass("TRN2", target_bir_lowering=False, debug=False)
    T = 1024
    x_d = nc.dram_tensor("x", [2048, T], F32, kind="ExternalInput").ap()
    or_d = nc.dram_tensor("oraw", [2048, T], BF16, kind="ExternalInput").ap()
    sqa_d = nc.dram_tensor("sqa", [1, T], F32, kind="ExternalInput").ap()
    sqb_d = nc.dram_tensor("sqb", [1, T], F32, kind="ExternalInput").ap()
    wg_d = nc.dram_tensor("wgT", [2048, 2048], BF16,
                          kind="ExternalInput").ap()
    wo_d = nc.dram_tensor("woT", [2048, 2048], BF16,
                          kind="ExternalInput").ap()
    nw_d = nc.dram_tensor("norm_w", [128, 16], F32, kind="ExternalInput").ap()
    xo_d = nc.dram_tensor("xo", [2048, T], F32, kind="ExternalOutput").ap()
    p = Plan(nc)
    C = Common(nc, p, T)
    PS = C.PS
    NF = 16
    R = [nc.alloc_sbuf_tensor(f"R{i}", [128, T], F32) for i in range(NF)]
    Y = [nc.alloc_sbuf_tensor(f"Yb{i}", [128, T], BF16) for i in range(NF)]
    ON = [nc.alloc_sbuf_tensor(f"c_on{i}", [128, T], BF16)
          for i in range(NF)]
    g_sb = nc.alloc_sbuf_tensor("c_gsb", [128, T], BF16)
    nw = nc.alloc_sbuf_tensor("c_nw", [128, 16], F32)
    sqr = nc.alloc_sbuf_tensor("c_sqr", [1, T], F32)
    sqr2 = nc.alloc_sbuf_tensor("c_sqr2", [1, T], F32)
    wsl = [nc.alloc_sbuf_tensor(f"c_wsl{i}", [128, 2048], BF16)
           for i in range(2)]

    xt = [p.dma("sync", R[i].ap(), x_d[128 * i:128 * (i + 1), :],
                stream="qx") for i in range(NF)]
    ot = [p.dma("sync", ON[i].ap(), or_d[128 * i:128 * (i + 1), :],
                stream="qor") for i in range(NF)]
    s1 = p.dma("sync", sqr.ap(), sqa_d, stream="qsm")
    s2 = p.dma("sync", sqr2.ap(), sqb_d, stream="qsm")
    s3 = p.dma("sync", nw.ap(), nw_d, stream="qsm")

    ln_toks = C.layernorm([R[i].ap() for i in range(NF)],
                          [Y[i].ap() for i in range(NF)],
                          eps=1e-5, after=xt)
    t1 = p.op("vector", lambda e: e.tensor_add(
        out=sqr.ap(), in0=sqr.ap(), in1=sqr2.ap()),
        after=[s1, s2, s3], inc=True)
    t2 = p.op("scalar", lambda e: e.activation(
        out=sqr.ap(), in_=sqr.ap(), func=AF.Ln,
        bias=C.constf(1e-6 * 2048)), after=[t1], inc=True)
    t3 = p.op("scalar", lambda e: e.activation(
        out=sqr.ap(), in_=sqr.ap(), func=AF.Exp, scale=-0.5),
        after=[t2], inc=True)
    t4 = p.op("vector", lambda e: e.tensor_scalar_mul(
        out=sqr.ap(), in0=sqr.ap(), scalar1=float(np.sqrt(2048.0))),
        after=[t3], inc=True)
    bts = []
    for n0 in range(0, T, 512):
        bts.append(p.op("tensor", lambda e, n0=n0: e.matmul(
            PS[2].ap()[:, n0:n0 + 512], C.onesrow_f.ap(),
            sqr.ap()[0:1, n0:n0 + 512], start=True, stop=True),
            after=[t4, ln_toks[-1], C.const_tok], inc=True))
    on_toks = []
    for i in range(NF):
        on_toks.append(p.op("vector", lambda e, i=i:
            e.scalar_tensor_tensor(
                out=ON[i].ap(), in0=ON[i].ap(), scalar=nw.ap()[:, i:i + 1],
                in1=PS[2].ap()[:, :T], op0=ALU.mult, op1=ALU.mult),
            after=bts + [ot[-1], s3], inc=True))
    prev_w = [None, None]
    prev_sig = None
    go_toks = []
    for m in range(NF):
        s = m % 2
        dstw = wsl[s].ap().rearrange("p (k c) -> p k c", k=16)
        lt = p.dma("gpsimd", dstw,
                   wg_d[:, 128 * m:128 * (m + 1)]
                   .rearrange("(k p) c -> p k c", p=128),
                   after=[prev_w[s]], stream=f"wg{s}")
        last = None
        for k in range(16):
            for n0 in (0, 512):
                deps = [lt, ln_toks[k]]
                if k == 0 and n0 == 0:
                    deps.append(prev_sig)
                last = p.op("tensor", lambda e, k=k, n0=n0, s=s: e.matmul(
                    PS[3].ap()[:, n0:n0 + 512],
                    wsl[s].ap()[:, 128 * k:128 * (k + 1)],
                    Y[k].ap()[:, n0:n0 + 512],
                    start=(k == 0), stop=(k == 15)),
                    after=deps, inc=(k == 15 and n0 == 512))
        prev_w[s] = last
        tsig = p.op("scalar", lambda e: e.activation(
            out=g_sb.ap(), in_=PS[3].ap()[:, :T], func=AF.Sigmoid),
            after=[last], inc=True)
        prev_sig = tsig
        go_toks.append(p.op("vector", lambda e, m=m: e.tensor_mul(
            out=ON[m].ap(), in0=ON[m].ap(), in1=g_sb.ap()),
            after=[tsig, on_toks[m]], inc=True))
        prev_sig = go_toks[-1]
    r_toks = []
    prev_ev = None
    for m in range(NF):
        s = m % 2
        dstw = wsl[s].ap().rearrange("p (k c) -> p k c", k=16)
        lt = p.dma("gpsimd", dstw,
                   wo_d[:, 128 * m:128 * (m + 1)]
                   .rearrange("(k p) c -> p k c", p=128),
                   after=[prev_w[s]], stream=f"wg{s}")
        last = None
        for k in range(16):
            for n0 in (0, 512):
                deps = [lt, go_toks[k]]
                if k == 0 and n0 == 0:
                    deps.append(prev_ev)
                last = p.op("tensor", lambda e, k=k, n0=n0, s=s: e.matmul(
                    PS[0].ap()[:, n0:n0 + 512],
                    wsl[s].ap()[:, 128 * k:128 * (k + 1)],
                    ON[k].ap()[:, n0:n0 + 512],
                    start=(k == 0), stop=(k == 15)),
                    after=deps, inc=(k == 15 and n0 == 512))
        prev_w[s] = last
        prev_ev = p.op("vector", lambda e, m=m: e.tensor_add(
            out=R[m].ap(), in0=PS[0].ap()[:, :T], in1=R[m].ap()),
            after=[last], inc=True)
        r_toks.append(prev_ev)
    for i in range(NF):
        p.dma("sync", xo_d[128 * i:128 * (i + 1), :], R[i].ap(),
              after=[r_toks[i]], stream="qout")
    p.build()
    return nc


def run_a1(x_shards, kw, prefix="L1_attn_", cast=None):
    if cast is None:
        cast = run_cast(kw)
    g = lambda k: np.asarray(kw[prefix + k], np.float32)
    wqkvT = cast["a1_wqkv"]
    slopes = np.asarray(kw[prefix + "slope_rate"], np.float32).reshape(-1)
    nc_a = build_a1a_launch()
    in_a = []
    for c in range(8):
        b, hh = c // 2, c % 2
        heads = range(8 * hh, 8 * hh + 8)
        xfull = np.ascontiguousarray(
            np.concatenate([x_shards[2 * b], x_shards[2 * b + 1]], axis=1))
        in_a.append({
            "x": xfull,
            "wqT": np.ascontiguousarray(np.concatenate(
                [wqkvT[:, 384 * h:384 * h + 128] for h in heads], axis=1)),
            "wkT": np.ascontiguousarray(np.concatenate(
                [wqkvT[:, 384 * h + 128:384 * h + 256] for h in heads],
                axis=1)),
            "wvT": np.ascontiguousarray(np.concatenate(
                [wqkvT[:, 384 * h + 256:384 * h + 384] for h in heads],
                axis=1)),
            "slopes8": np.ascontiguousarray(
                slopes[8 * hh:8 * hh + 8][None, :]),
        })
    res = run_bass_kernel_spmd(nc_a, in_a, core_ids=list(range(8)))
    oraw = [res.results[c]["oraw"] for c in range(8)]
    sqp = [res.results[c]["sqp"] for c in range(8)]

    nc_b = build_a1b_launch()
    wgT = cast["a1_wg"]
    woT = cast["a1_wo"]
    nwfm = np.ascontiguousarray(g("norm_w").reshape(16, 128).T)
    in_b = []
    for c in range(8):
        b, hh = c // 2, c % 2
        sl = slice(1024 * hh, 1024 * (hh + 1))
        in_b.append({
            "x": x_shards[c],
            "oraw": np.ascontiguousarray(np.concatenate(
                [oraw[2 * b][:, sl], oraw[2 * b + 1][:, sl]], axis=0)),
            "sqa": np.ascontiguousarray(sqp[2 * b][:, sl]),
            "sqb": np.ascontiguousarray(sqp[2 * b + 1][:, sl]),
            "wgT": wgT, "woT": woT, "norm_w": nwfm,
        })
    res = run_bass_kernel_spmd(nc_b, in_b, core_ids=list(range(8)))
    return [res.results[c]["xo"] for c in range(8)]


def build_f1_launch(T=1024):
    nc = bass.Bass("TRN2", target_bir_lowering=False, debug=False)
    x_d = nc.dram_tensor("x", [2048, T], F32, kind="ExternalInput").ap()
    W = {
        "gate_wT": nc.dram_tensor("gate_wT", [2048, 8], F32,
                                  kind="ExternalInput").ap(),
        "w1": nc.dram_tensor("w1", [8, 2048, 1408], BF16,
                             kind="ExternalInput").ap(),
        "w3": nc.dram_tensor("w3", [8, 2048, 1408], BF16,
                             kind="ExternalInput").ap(),
        "w2": nc.dram_tensor("w2", [8, 1408, 2048], BF16,
                             kind="ExternalInput").ap(),
        "sw1": nc.dram_tensor("sw1", [2048, 1408], BF16,
                              kind="ExternalInput").ap(),
        "sw3": nc.dram_tensor("sw3", [2048, 1408], BF16,
                              kind="ExternalInput").ap(),
        "sw2": nc.dram_tensor("sw2", [1408, 2048], BF16,
                              kind="ExternalInput").ap(),
    }
    xo_d = nc.dram_tensor("xo", [2048, T], F32, kind="ExternalOutput").ap()

    p = Plan(nc)
    C = Common(nc, p, T)
    NF = 16
    R = [nc.alloc_sbuf_tensor(f"R{i}", [128, T], F32) for i in range(NF)]
    Y = [nc.alloc_sbuf_tensor(f"Yb{i}", [128, T], BF16) for i in range(NF)]

    xt = [p.dma("sync", R[i].ap(), x_d[128 * i:128 * (i + 1), :],
                stream="qx") for i in range(NF)]
    ln_toks = C.layernorm([R[i].ap() for i in range(NF)],
                          [Y[i].ap() for i in range(NF)],
                          eps=1e-5, after=xt)
    emit_f1(nc, p, C, Y, R, T, W, y_toks=ln_toks)
    for i in range(NF):
        p.dma("sync", xo_d[128 * i:128 * (i + 1), :], R[i].ap(),
              after=[p.now("vector")], stream="qout")
    p.build()
    return nc



CAST_SPECS = [
    ("f1_w1", 16384, 1408),
    ("f1_w3", 16384, 1408),
    ("f1_w2", 11264, 2048),
    ("f1_sw1", 2048, 1408),
    ("f1_sw3", 2048, 1408),
    ("f1_sw2", 1408, 2048),
    ("f0_agg", 2048, 2048),
    ("f0_d1", 2048, 8192),
    ("f0_d2", 8192, 2048),
    ("f0_bc", 2048, 2048),
    ("f0_gate", 2048, 2048),
    ("a0_win", 2048, 8512),
    ("a0_wout", 4096, 2048),
    ("a1_wqkv", 2048, 6144),
    ("a1_wg", 2048, 2048),
    ("a1_wo", 2048, 2048),
]


def build_cast_launch():
    nc = bass.Bass("TRN2", target_bir_lowering=False, debug=False)
    p = Plan(nc)
    NB = 6
    bufs = [nc.alloc_sbuf_tensor(f"cb{i}", [128, 8512], BF16)
            for i in range(NB)]
    prev_out = [None] * NB
    b = 0
    for nm, rows, cols in CAST_SPECS:
        r8 = rows // 8
        in_d = nc.dram_tensor(nm + "_f", [r8, cols], F32,
                              kind="ExternalInput").ap()
        out_d = nc.dram_tensor(nm + "_b", [r8, cols], BF16,
                               kind="ExternalOutput").ap()
        for r0 in range(0, r8, 128):
            rp = min(128, r8 - r0)
            lt = p.dma("gpsimd", bufs[b].ap()[0:rp, 0:cols],
                       in_d[r0:r0 + rp, :], after=[prev_out[b]],
                       stream=f"ci{b}")
            prev_out[b] = p.dma("sync", out_d[r0:r0 + rp, :],
                                bufs[b].ap()[0:rp, 0:cols], after=[lt],
                                stream=f"co{b}")
            b = (b + 1) % NB
    p.build()
    return nc


def cast_arrays(kw):
    return {
        "f1_w1": _c(kw["L1_ffn_w1"]).reshape(16384, 1408),
        "f1_w3": _c(kw["L1_ffn_w3"]).reshape(16384, 1408),
        "f1_w2": _c(kw["L1_ffn_w2"]).reshape(11264, 2048),
        "f1_sw1": _c(kw["L1_ffn_sw1"]),
        "f1_sw3": _c(kw["L1_ffn_sw3"]),
        "f1_sw2": _c(kw["L1_ffn_sw2"]),
        "f0_agg": _c(kw["L0_ffn_agg_w"]),
        "f0_d1": _c(kw["L0_ffn_d1_w"]),
        "f0_d2": _c(kw["L0_ffn_d2_w"]),
        "f0_bc": _c(kw["L0_ffn_bc_w"]),
        "f0_gate": _c(kw["L0_ffn_gate_w"]),
        "a0_win": _t(kw["L0_attn_W_in"]),
        "a0_wout": _t(kw["L0_attn_W_out"]),
        "a1_wqkv": _t(kw["L1_attn_w_qkv"]),
        "a1_wg": _t(kw["L1_attn_w_gate"]),
        "a1_wo": _t(kw["L1_attn_w_out"]),
    }


def run_cast(kw):
    nc = build_cast_launch()
    arrs = cast_arrays(kw)
    in_maps = []
    for c in range(8):
        m = {}
        for nm, rows, cols in CAST_SPECS:
            r8 = rows // 8
            m[nm + "_f"] = np.ascontiguousarray(
                arrs[nm][c * r8:(c + 1) * r8])
        in_maps.append(m)
    res = run_bass_kernel_spmd(nc, in_maps, core_ids=list(range(8)))
    return {nm: np.concatenate(
                [res.results[c][nm + "_b"] for c in range(8)], axis=0)
            for nm, rows, cols in CAST_SPECS}



def _t(a):
    return np.ascontiguousarray(np.asarray(a, dtype=np.float32).T)


def _c(a):
    return np.ascontiguousarray(np.asarray(a, dtype=np.float32))


def shard_tokens_fm(x):
    out = []
    for c in range(8):
        b, h = c // 2, c % 2
        out.append(_t(x[b, 1024 * h:1024 * h + 1024, :]))
    return out


def unshard_tokens_fm(shards):
    x = np.empty((4, 2048, 2048), dtype=np.float32)
    for c in range(8):
        b, h = c // 2, c % 2
        x[b, 1024 * h:1024 * h + 1024, :] = shards[c].T
    return x


def a0_inputs(kw, prefix="L0_attn_"):
    g = lambda k: np.asarray(kw[prefix + k], np.float32)
    conv_w = g("conv_w").reshape(34, 128, 4).transpose(1, 0, 2).reshape(128, 136)
    return {
        "conv_w": np.ascontiguousarray(conv_w),
        "conv_b": np.ascontiguousarray(g("conv_b").reshape(34, 128).T),
        "dt_bias": np.ascontiguousarray(g("dt_bias")[:, None]),
        "A_log": np.ascontiguousarray(g("A_log")[:, None]),
        "D_fm": np.ascontiguousarray(np.repeat(g("D"), 64).reshape(32, 128).T),
        "norm_w": np.ascontiguousarray(g("norm_w").reshape(32, 128).T),
    }


def run_a0(x_shards, kw, prefix="L0_attn_", cast=None):
    if cast is None:
        cast = run_cast(kw)
    nc = build_a0_launch()
    wmap = a0_inputs(kw, prefix)
    wmap["w_inT"] = cast["a0_win"]
    wmap["w_outT"] = cast["a0_wout"]
    in_maps = []
    for c in range(8):
        halo = (np.zeros((2048, 3), np.float32) if c % 2 == 0
                else np.ascontiguousarray(x_shards[c - 1][:, 1021:1024]))
        m = dict(wmap)
        m["x"] = np.ascontiguousarray(
            np.concatenate([halo, x_shards[c]], axis=1))
        in_maps.append(m)
    res = run_bass_kernel_spmd(nc, in_maps, core_ids=list(range(8)))
    return [res.results[c]["xo"] for c in range(8)]


def f1_inputs(weights_prefix, kw, cast):
    return {
        "gate_wT": _t(kw[weights_prefix + "gate_w"]),
        "w1": cast["f1_w1"].reshape(8, 2048, 1408),
        "w3": cast["f1_w3"].reshape(8, 2048, 1408),
        "w2": cast["f1_w2"].reshape(8, 1408, 2048),
        "sw1": cast["f1_sw1"],
        "sw3": cast["f1_sw3"],
        "sw2": cast["f1_sw2"],
    }


def run_f1(x_shards, kw, prefix="L1_ffn_", cast=None):
    if cast is None:
        cast = run_cast(kw)
    nc = build_f1_launch()
    wmap = f1_inputs(prefix, kw, cast)
    in_maps = [dict(wmap, x=x_shards[c]) for c in range(8)]
    res = run_bass_kernel_spmd(nc, in_maps, core_ids=list(range(8)))
    return [res.results[c]["xo"] for c in range(8)]


def _xlast64(shards):
    xl = np.ascontiguousarray(
        np.stack([shards[2 * b + 1][:, 1023] for b in range(4)], axis=1))
    return np.ascontiguousarray(
        xl.reshape(16, 128, 4).transpose(1, 0, 2).reshape(128, 64))


def kernel(**kw):
    x = np.asarray(kw["L0_attn_input_states"], dtype=np.float32)
    cast = run_cast(kw)
    shards = shard_tokens_fm(x)
    shards = run_f0(shards, _xlast64(shards), kw, "L0_ffn_", cast)
    shards = run_a0(shards, kw, "L0_attn_", cast)
    shards = run_f0(shards, _xlast64(shards), kw, "L0_ffn_", cast)
    shards = run_f1(shards, kw, "L1_ffn_", cast)
    shards = run_a1(shards, kw, "L1_attn_", cast)
    shards = run_f1(shards, kw, "L1_ffn_", cast)
    return unshard_tokens_fm(shards)


# version 03 (final)
# speedup vs baseline: 1.1260x; 1.0328x over previous
import numpy as np
import ml_dtypes
import concourse.bass as bass
import concourse.mybir as mybir
from concourse.bass_utils import run_bass_kernel_spmd

F32 = mybir.dt.float32
BF16 = mybir.dt.bfloat16
I32 = mybir.dt.int32
AF = mybir.ActivationFunctionType
ALU = mybir.AluOpType
AX = mybir.AxisListType
BF = ml_dtypes.bfloat16

ENGINES = ("sync", "scalar", "vector", "tensor", "gpsimd")
DMAQ = {"sync": "qsync", "gpsimd": "qpool", "scalar": "qact"}


class Tok:
    __slots__ = ("sem", "val")

    def __init__(self, sem, val):
        self.sem = sem
        self.val = val


class Plan:

    def __init__(self, nc):
        self.nc = nc
        self.ops = []
        self.ecnt = {e: 0 for e in ENGINES}
        self.ecnt["sync"] = 1
        self.qcnt = {}

    SERIAL = ("vector", "scalar", "gpsimd")

    def op(self, eng, fn, after=(), inc=False):
        waits = [t for t in after if t is not None]
        if eng in self.SERIAL:
            inc = True
        tok = None
        if inc:
            self.ecnt[eng] += 1
            tok = Tok("e" + eng, self.ecnt[eng])
        self.ops.append(
            (eng, fn, waits,
             ("e" + eng, 1, self.ecnt[eng]) if inc else None))
        return tok

    def dma(self, eng, out, in_, after=(), stream=None):
        q = stream or ("q" + eng)
        self.qcnt[q] = self.qcnt.get(q, 0) + 16
        tok = Tok(q, self.qcnt[q])
        waits = [t for t in after if t is not None]
        self.ops.append(
            (eng, lambda e, out=out, in_=in_: e.dma_start(out=out, in_=in_),
             waits, (q, 16, self.qcnt[q])))
        return tok

    def now(self, eng):
        if eng in ENGINES:
            return Tok("e" + eng, self.ecnt[eng])
        return Tok(eng, self.qcnt[eng])

    def build(self):
        nc = self.nc
        sems = {}
        names = ["e" + e for e in ENGINES] + sorted(self.qcnt)
        for n in names:
            sems[n] = nc.alloc_semaphore(name="s_" + n)
        for e in ENGINES:
            if e != "sync":
                self.ecnt[e] += 1

        with nc.Block() as block:
            def mk_body(name):
                def body(eng):
                    if name == "sync":
                        for s in sems.values():
                            eng.sem_clear(s)
                        eng.nop().then_inc(sems["esync"], 1)
                    else:
                        eng.wait_op(sems["esync"], 1, "sem-eq")
                    hwm = {}
                    for (oe, fn, waits, inc) in self.ops:
                        if oe != name:
                            continue
                        need = {}
                        for t in waits:
                            need[t.sem] = max(need.get(t.sem, 0), t.val)
                        if (name in self.SERIAL and inc is not None
                                and inc[0] == "e" + name):
                            pv = inc[2] - 1
                            if pv > 0:
                                need["e" + name] = max(
                                    need.get("e" + name, 0), pv)
                        for s, v in need.items():
                            if hwm.get(s, 0) >= v:
                                continue
                            hwm[s] = v
                            eng.wait_ge(sems[s], v)
                        inst = fn(eng)
                        if inc is not None:
                            inst.then_inc(sems[inc[0]], inc[1])
                    if name == "sync":
                        for e in ENGINES:
                            if e != "sync":
                                eng.wait_ge(sems["e" + e], self.ecnt[e])
                        for q in sorted(self.qcnt):
                            if self.qcnt[q]:
                                eng.wait_ge(sems[q], self.qcnt[q])
                        eng.nop().then_inc(sems["esync"], 1)
                    else:
                        eng.wait_ge(sems["e" + name], self.ecnt[name] - 1)
                        eng.nop().then_inc(sems["e" + name], 1)
                return body
            for name in ENGINES:
                getattr(block, name)(mk_body(name))


def cdiv(a, b):
    return -(-a // b)



class Common:
    def __init__(self, nc, p, T, ps_shapes=(1024, 1024, 1024, 1024)):
        self.nc, self.p, self.T = nc, p, T
        self.xb = nc.alloc_sbuf_tensor("c_xb", [128, T], BF16)
        self.sq = nc.alloc_sbuf_tensor("c_sq", [128, T], BF16)
        self.r_mu = nc.alloc_sbuf_tensor("c_rmu", [1, T], F32)
        self.r_v = nc.alloc_sbuf_tensor("c_rv", [1, T], F32)
        self.lntmp = nc.alloc_sbuf_tensor("c_lntmp", [128, T], F32)
        self.r_b = self.lntmp
        self.onescol = nc.alloc_sbuf_tensor("c_onescol", [128, 1], BF16)
        self.onesrow_f = nc.alloc_sbuf_tensor("c_onesrow_f", [1, 128], F32)
        self.ident_f = nc.alloc_sbuf_tensor("c_ident_f", [128, 128], F32)
        self.ident_b = nc.alloc_sbuf_tensor("c_ident_b", [128, 128], BF16)
        ident_d = nc.inline_tensor(np.eye(128, dtype=np.float32),
                                   name="c_ident_src")
        identb_d = nc.inline_tensor(np.eye(128, dtype=BF),
                                    name="c_identb_src")
        self.PS = [nc.alloc_psum_tensor(f"c_ps{i}", [128, s], F32)
                   for i, s in enumerate(ps_shapes)]
        self.cbuf = nc.alloc_sbuf_tensor("c_consts", [128, 32], F32)
        self.cmap = {}
        p.op("vector", lambda e: e.memset(self.onescol.ap(), 1.0))
        p.op("vector", lambda e: e.memset(self.onesrow_f.ap(), 1.0))
        for v in (1e-5, 1e-6, -0.5, 0.5, 1.0 / 2048, 1.0 / 4096,
                  1e-5 * 4096, 1e-6 * 2048):
            self.constf(v)
        p.op("vector", lambda e: e.nop(), inc=True)
        t1 = p.dma("sync", self.ident_f.ap(), ident_d.ap(),
                   stream="qconst")
        t2 = p.dma("sync", self.ident_b.ap(), identb_d.ap(),
                   stream="qconst")
        self.const_tok = Tok("evector", p.ecnt["vector"])
        self.ident_tok = t2

    def constf(self, val):
        val = float(val)
        if val not in self.cmap:
            assert not hasattr(self, "const_tok"), \
                f"constf({val}) after init: pre-register it in Common"
            i = len(self.cmap)
            assert i < 32
            self.cmap[val] = i
            ap = self.cbuf.ap()[:, i:i + 1]
            self.p.op("vector", lambda e, ap=ap, val=val: e.memset(ap, val))
            self.nc.const_aps.register(F32, val, ap)
        return val

    def layernorm(self, x_chunks, out_chunks, T=None, eps=1e-5,
                  gamma=None, beta=None, after=()):
        p = self.p
        T = T or self.T
        nf = len(x_chunks)
        D = nf * 128
        if hasattr(self, "ln_ps"):
            PSsum, PSsq, self.sq_off, PSb1, PSb2 = self.ln_ps
        else:
            PSsum, PSsq, PSb1, PSb2 = self.PS
            self.sq_off = 0
        after = list(after) + [self.const_tok]
        mm = []
        for i, xc in enumerate(x_chunks):
            tc = p.op("vector", lambda e, xc=xc: e.tensor_copy(
                out=self.xb.ap()[:, :T], in_=xc), after=after, inc=True)
            ts = p.op("scalar", lambda e, xc=xc: e.activation(
                out=self.sq.ap()[:, :T], in_=xc, func=AF.Square),
                after=after, inc=True)
            mm = []
            for n0 in range(0, T, 512):
                n1 = min(T, n0 + 512)
                mm.append(p.op("tensor", lambda e, i=i, n0=n0, n1=n1:
                    e.matmul(PSsum.ap()[0:1, n0:n1], self.onescol.ap(),
                             self.xb.ap()[:, n0:n1],
                             start=(i == 0), stop=(i == nf - 1)),
                    after=[tc], inc=True))
                mm.append(p.op("tensor", lambda e, i=i, n0=n0, n1=n1:
                    e.matmul(PSsq.ap()[0:1, self.sq_off + n0:
                                       self.sq_off + n1],
                             self.onescol.ap(), self.sq.ap()[:, n0:n1],
                             start=(i == 0), stop=(i == nf - 1)),
                    after=[ts], inc=True))
            if i < nf - 1:
                p.op("vector", lambda e: e.nop(), after=[mm[-2]])
                p.op("scalar", lambda e: e.nop(), after=[mm[-1]])
            after = []
        r_mu, r_v, r_b = self.r_mu, self.r_v, self.r_b
        tmu = p.op("vector", lambda e: e.tensor_scalar_mul(
            out=r_mu.ap()[:, :T], in0=PSsum.ap()[0:1, :T], scalar1=1.0 / D),
            after=mm, inc=True)
        t1 = p.op("vector", lambda e: e.tensor_scalar_mul(
            out=r_v.ap()[:, :T],
            in0=PSsq.ap()[0:1, self.sq_off:self.sq_off + T],
            scalar1=1.0 / D), after=mm, inc=True)
        t2 = p.op("scalar", lambda e: e.activation(
            out=r_b.ap()[0:1, :T], in_=r_mu.ap()[:, :T], func=AF.Square),
            after=[tmu], inc=True)
        t3 = p.op("vector", lambda e: e.tensor_sub(
            out=r_v.ap()[:, :T], in0=r_v.ap()[:, :T],
            in1=r_b.ap()[0:1, :T]), after=[t1, t2], inc=True)
        t4 = p.op("scalar", lambda e: e.activation(
            out=r_v.ap()[:, :T], in_=r_v.ap()[:, :T], func=AF.Ln,
            bias=self.constf(eps)), after=[t3], inc=True)
        t5 = p.op("scalar", lambda e: e.activation(
            out=r_v.ap()[:, :T], in_=r_v.ap()[:, :T], func=AF.Exp,
            scale=-0.5), after=[t4], inc=True)
        t6 = p.op("vector", lambda e: e.tensor_mul(
            out=r_b.ap()[0:1, :T], in0=r_mu.ap()[:, :T],
            in1=r_v.ap()[:, :T]), after=[t5], inc=True)
        t7 = p.op("vector", lambda e: e.tensor_scalar_mul(
            out=r_b.ap()[0:1, :T], in0=r_b.ap()[0:1, :T], scalar1=-1.0),
            after=[t6], inc=True)
        bt = []
        for n0 in range(0, T, 512):
            n1 = min(T, n0 + 512)
            bt.append(p.op("tensor", lambda e, n0=n0, n1=n1: e.matmul(
                PSb1.ap()[:, n0:n1], self.onesrow_f.ap(),
                r_v.ap()[0:1, n0:n1], start=True, stop=True),
                after=[t5, t7], inc=True))
            bt.append(p.op("tensor", lambda e, n0=n0, n1=n1: e.matmul(
                PSb2.ap()[:, n0:n1], self.onesrow_f.ap(),
                r_b.ap()[0:1, n0:n1], start=True, stop=True),
                after=[t7], inc=True))
        toks = []
        for i, (xc, oc) in enumerate(zip(x_chunks, out_chunks)):
            ta = p.op("vector", lambda e, xc=xc: e.tensor_mul(
                out=self.lntmp.ap()[:, :T], in0=xc, in1=PSb1.ap()[:, :T]),
                after=bt, inc=True)
            if gamma is None:
                tl = p.op("vector", lambda e, oc=oc: e.tensor_add(
                    out=oc, in0=self.lntmp.ap()[:, :T],
                    in1=PSb2.ap()[:, :T]), after=[ta], inc=True)
            else:
                p.op("vector", lambda e: e.tensor_add(
                    out=self.lntmp.ap()[:, :T], in0=self.lntmp.ap()[:, :T],
                    in1=PSb2.ap()[:, :T]), after=[ta])
                tl = p.op("vector", lambda e, oc=oc, i=i: e.tensor_scalar(
                    out=oc, in0=self.lntmp.ap()[:, :T], scalar1=gamma[i],
                    scalar2=beta[i], op0=ALU.mult, op1=ALU.add), inc=True)
            toks.append(tl)
        return toks



def emit_f1(nc, p, C, Y, R, T, W, y_toks, name="f1"):
    NF, NH = 16, 11
    PS = C.PS

    gw = nc.alloc_sbuf_tensor(name + "_gw", [128, 16 * 8], BF16)
    gwt_toks = [p.dma("gpsimd", gw.ap()[:, 8 * i:8 * (i + 1)],
                      W["gate_wT"][128 * i:128 * (i + 1), :], stream="qgw")
                for i in range(16)]
    gwt = [gwt_toks[-1]] * 16
    ncht = T // 128
    CMB = [nc.alloc_sbuf_tensor(f"{name}_cmb{e}", [128, T], BF16)
           for e in range(8)]
    sc_sb = nc.alloc_sbuf_tensor(name + "_scsb", [128, 8], F32)
    sc2 = nc.alloc_sbuf_tensor(name + "_sc2", [128, 8], F32)
    mrow = nc.alloc_sbuf_tensor(name + "_mrow", [128, 4], F32)
    cmb_toks = []
    prev_tsg = None
    prev_bm = None
    prev_ev = None
    for tchunk in range(ncht):
        t0 = 128 * tchunk
        tm = None
        for i in range(16):
            deps = [y_toks[i], gwt[i]]
            if i == 0 and prev_tsg is not None:
                deps.append(prev_tsg)
            tm = p.op("tensor", lambda e, i=i, t0=t0: e.matmul(
                PS[0].ap()[:, 0:8], Y[i].ap()[:, t0:t0 + 128],
                gw.ap()[:, 8 * i:8 * (i + 1)],
                start=(i == 0), stop=(i == 15)),
                after=deps, inc=(i == 15))
        tsg = p.op("scalar", lambda e: e.activation(
            out=sc_sb.ap(), in_=PS[0].ap()[:, 0:8], func=AF.Sigmoid),
            after=[tm], inc=True)
        prev_tsg = tsg
        t1 = p.op("vector", lambda e: e.reduce_max(
            out=mrow.ap()[:, 0:1], in_=sc_sb.ap(), axis=AX.X),
            after=[tsg, prev_bm], inc=True)
        t2 = p.op("vector", lambda e: e.tensor_scalar(
            out=sc2.ap(), in0=sc_sb.ap(), scalar1=mrow.ap()[:, 0:1],
            scalar2=-1e30, op0=ALU.is_ge, op1=ALU.mult), after=[t1], inc=True)
        t3 = p.op("vector", lambda e: e.tensor_add(
            out=sc2.ap(), in0=sc2.ap(), in1=sc_sb.ap()), after=[t2], inc=True)
        t4 = p.op("vector", lambda e: e.reduce_max(
            out=mrow.ap()[:, 1:2], in_=sc2.ap(), axis=AX.X),
            after=[t3], inc=True)
        t5 = p.op("vector", lambda e: e.tensor_add(
            out=mrow.ap()[:, 2:3], in0=mrow.ap()[:, 0:1],
            in1=mrow.ap()[:, 1:2]), after=[t4], inc=True)
        p.op("vector", lambda e: e.tensor_scalar_add(
            out=mrow.ap()[:, 2:3], in0=mrow.ap()[:, 2:3], scalar1=1e-20))
        t6 = p.op("vector", lambda e: e.reciprocal(
            out=mrow.ap()[:, 2:3], in_=mrow.ap()[:, 2:3]), inc=True)
        p.op("vector", lambda e: e.tensor_scalar_mul(
            out=mrow.ap()[:, 2:3], in0=mrow.ap()[:, 2:3], scalar1=2.5))
        t7 = p.op("vector", lambda e: e.tensor_scalar(
            out=sc2.ap(), in0=sc_sb.ap(), scalar1=mrow.ap()[:, 1:2],
            scalar2=None, op0=ALU.is_ge), after=[t6], inc=True)
        t8 = p.op("vector", lambda e: e.tensor_mul(
            out=sc2.ap(), in0=sc2.ap(), in1=sc_sb.ap()), after=[t7], inc=True)
        t9 = p.op("vector", lambda e: e.tensor_scalar_mul(
            out=sc2.ap(), in0=sc2.ap(), scalar1=mrow.ap()[:, 2:3]),
            after=[t8], inc=True)
        evs = []
        for e8 in range(8):
            bm = p.op("tensor", lambda e, e8=e8: e.matmul(
                PS[1].ap()[:, 128 * e8:128 * (e8 + 1)],
                sc2.ap()[:, e8:e8 + 1].broadcast_to([128, 128]),
                C.ident_f.ap(), start=True, stop=True),
                after=[t9, C.ident_tok, prev_ev], inc=True)
            prev_bm = bm
            ev = p.op("vector", lambda e, e8=e8, t0=t0: e.tensor_copy(
                out=CMB[e8].ap()[:, t0:t0 + 128],
                in_=PS[1].ap()[:, 128 * e8:128 * (e8 + 1)]),
                after=[bm], inc=True)
            prev_ev = ev
            evs.append(ev)
        cmb_toks.append(evs[-1])

    wbuf = [nc.alloc_sbuf_tensor(f"{name}_wb{i}", [128, 2048], BF16)
            for i in range(4)]
    w2buf = [nc.alloc_sbuf_tensor(f"{name}_w2b{i}", [128, 1408], BF16)
             for i in range(2)]
    g_e = nc.alloc_sbuf_tensor(name + "_g", [128, NH * T], BF16)
    silu_t = nc.alloc_sbuf_tensor(name + "_silu", [128, T], BF16)

    racc_toks = [None] * NF
    prev_racc = None
    experts = list(range(8)) + ["shared"]
    prev_use = {0: None, 1: None, 2: None, 3: None}
    prev_w2use = {0: None, 1: None}
    prev_ydone = None
    prev_ts = None
    prev_tg = None
    for ex in experts:
        hu_done = []
        for m in range(NH):
            if ex == "shared":
                w1ap = W["sw1"][:, 128 * m:128 * (m + 1)]
                w3ap = W["sw3"][:, 128 * m:128 * (m + 1)]
            else:
                w1ap = W["w1"][ex, :, 128 * m:128 * (m + 1)]
                w3ap = W["w3"][ex, :, 128 * m:128 * (m + 1)]
            s1, s3 = (2 * (m % 2)), (2 * (m % 2) + 1)
            dst1 = wbuf[s1].ap().rearrange("p (k c) -> p k c", k=16)
            lt1 = p.dma("gpsimd", dst1,
                        w1ap.rearrange("(k p) c -> p k c", p=128),
                        after=[prev_use[s1]], stream=f"w{s1}")
            dst3 = wbuf[s3].ap().rearrange("p (k c) -> p k c", k=16)
            lt3 = p.dma("gpsimd", dst3,
                        w3ap.rearrange("(k p) c -> p k c", p=128),
                        after=[prev_use[s3]], stream=f"w{s3}")
            pts, ptg = prev_ts, prev_tg
            th = tu = None
            for k in range(16):
                for n0 in (0, 512):
                    hdeps = [lt1, y_toks[k]]
                    udeps = [lt3, y_toks[k]]
                    if k == 0 and n0 == 0:
                        hdeps += [pts]
                        udeps += [ptg]
                        if ex == 0 and m == 0:
                            hdeps.append(prev_tsg)
                            udeps.append(prev_ev)
                    th = p.op("tensor", lambda e, k=k, n0=n0, s1=s1: e.matmul(
                        PS[0].ap()[:, n0:n0 + 512],
                        wbuf[s1].ap()[:, 128 * k:128 * (k + 1)],
                        Y[k].ap()[:, n0:n0 + 512],
                        start=(k == 0), stop=(k == 15)),
                        after=hdeps, inc=(k == 15 and n0 == 512))
                    tu = p.op("tensor", lambda e, k=k, n0=n0, s3=s3: e.matmul(
                        PS[1].ap()[:, n0:n0 + 512],
                        wbuf[s3].ap()[:, 128 * k:128 * (k + 1)],
                        Y[k].ap()[:, n0:n0 + 512],
                        start=(k == 0), stop=(k == 15)),
                        after=udeps, inc=(k == 15 and n0 == 512))
            prev_use[s1] = th
            prev_use[s3] = tu
            ts = p.op("scalar", lambda e: e.activation(
                out=silu_t.ap(), in_=PS[0].ap()[:, :T], func=AF.Silu),
                after=[th], inc=True)
            prev_ts = ts
            tg = p.op("vector", lambda e, m=m: e.tensor_mul(
                out=g_e.ap()[:, m * T:(m + 1) * T], in0=silu_t.ap(),
                in1=PS[1].ap()[:, :T]), after=[ts, tu, prev_ydone], inc=True)
            prev_tg = tg
            if ex != "shared":
                tg = p.op("vector", lambda e, m=m, ex=ex: e.tensor_mul(
                    out=g_e.ap()[:, m * T:(m + 1) * T],
                    in0=g_e.ap()[:, m * T:(m + 1) * T],
                    in1=CMB[ex].ap()), after=[tg] + cmb_toks, inc=True)
            hu_done.append(tg)
        for f in range(NF):
            if ex == "shared":
                w2ap = W["sw2"][:, 128 * f:128 * (f + 1)]
            else:
                w2ap = W["w2"][ex, :, 128 * f:128 * (f + 1)]
            s2 = f % 2
            dst2 = w2buf[s2].ap().rearrange("p (k c) -> p k c", k=NH)
            lt2 = p.dma("gpsimd", dst2,
                        w2ap.rearrange("(k p) c -> p k c", p=128),
                        after=[prev_w2use[s2]], stream=f"w2{s2}")
            tmm = None
            for k in range(NH):
                for n0 in (0, 512):
                    deps = [lt2, hu_done[k]]
                    if k == 0 and n0 == 0:
                        deps.append(prev_racc)
                    tmm = p.op("tensor", lambda e, k=k, n0=n0, s2=s2:
                        e.matmul(PS[2].ap()[:, n0:n0 + 512],
                                 w2buf[s2].ap()[:, 128 * k:128 * (k + 1)],
                                 g_e.ap()[:, k * T + n0:k * T + n0 + 512],
                                 start=(k == 0), stop=(k == NH - 1)),
                        after=deps, inc=(k == NH - 1 and n0 == 512))
            prev_w2use[s2] = tmm
            racc_toks[f] = p.op("vector", lambda e, f=f: e.scalar_tensor_tensor(
                out=R[f].ap(), in0=PS[2].ap()[:, :T], scalar=0.5,
                in1=R[f].ap(), op0=ALU.mult, op1=ALU.add),
                after=[tmm], inc=True)
            prev_racc = racc_toks[f]
        prev_ydone = racc_toks[NF - 1]
    return racc_toks




def emit_f0(nc, p, C, Y, R, T, W, y_toks, name="f0"):
    PS = C.PS
    NF = 16

    xl = nc.alloc_sbuf_tensor(name + "_xl", [128, 64], F32)
    xn = nc.alloc_sbuf_tensor(name + "_xn", [128, 64], BF16)
    xmid = nc.alloc_sbuf_tensor(name + "_xmid", [128, 64], F32)
    selc = nc.alloc_sbuf_tensor(name + "_sel", [4, 1], F32)
    ing = nc.alloc_sbuf_tensor(name + "_ing", [128, 16], F32)
    inb = nc.alloc_sbuf_tensor(name + "_inb", [128, 16], F32)
    gb = nc.alloc_sbuf_tensor(name + "_gb", [128, 16], F32)
    outg = nc.alloc_sbuf_tensor(name + "_outg", [128, 16], F32)
    outb = nc.alloc_sbuf_tensor(name + "_outb", [128, 16], F32)
    b_agg = nc.alloc_sbuf_tensor(name + "_bagg", [4, 2048], BF16)
    b_d1q = nc.alloc_sbuf_tensor(name + "_bd1q", [4, 2048], BF16)
    b_d2 = nc.alloc_sbuf_tensor(name + "_bd2", [4, 2048], BF16)
    b_bc = nc.alloc_sbuf_tensor(name + "_bbc", [4, 2048], BF16)
    png = nc.alloc_sbuf_tensor(name + "_png", [4, 2048], BF16)
    pnb = nc.alloc_sbuf_tensor(name + "_pnb", [4, 2048], BF16)
    lds = []
    lds.append(p.dma("sync", xl.ap(), W["xlast"], stream="qsm"))
    for dst, key in ((selc, "sel"), (ing, "in_g"), (inb, "in_b"),
                     (gb, "gate_b"), (outg, "out_g"), (outb, "out_b")):
        lds.append(p.dma("sync", dst.ap(), W[key], stream="qsm"))
    lds2 = []
    for dst, key in ((b_agg, "agg_b4"), (b_d2, "d2b4"),
                     (b_bc, "bc_b4"), (png, "pn_g4"), (pnb, "pn_b4")):
        lds2.append(p.dma("gpsimd", dst.ap(), W[key], stream="qsmb"))
    sm_tok0 = lds[-1]
    sm_tok = lds2[-1]


    xchunks = [xl.ap()[:, 4 * i:4 * (i + 1)] for i in range(NF)]
    mchunks = [xmid.ap()[:, 4 * i:4 * (i + 1)] for i in range(NF)]
    nchunks = [xn.ap()[:, 4 * i:4 * (i + 1)] for i in range(NF)]
    after0 = [sm_tok0, p.now("vector"), p.now("scalar"), p.now("tensor")]
    t_ln1 = C.layernorm(xchunks, mchunks, T=4, eps=1e-5, after=after0)
    t_ln2 = C.layernorm(mchunks, nchunks, T=4, eps=1e-6,
                        gamma=[ing.ap()[:, i:i + 1] for i in range(NF)],
                        beta=[inb.ap()[:, i:i + 1] for i in range(NF)],
                        after=t_ln1)

    NSL = 4
    DMAE = ("gpsimd", "sync", "scalar")
    rr = [0]

    def wdma(s, dst, src, after, stream):
        eng = DMAE[rr[0] % 3]
        rr[0] += 1
        return p.dma(eng, dst, src, after=after,
                     stream=f"{stream}{s}_{eng}")

    wsl = [nc.alloc_sbuf_tensor(f"{name}_wsl{i}", [128, 1024], BF16)
           for i in range(NSL)]
    prev_wuse = [None] * NSL

    th_tm = nc.alloc_sbuf_tensor(name + "_th", [4, 2048], F32)
    t_tm = nc.alloc_sbuf_tensor(name + "_ttm", [4, 2048], F32)
    g_tm = nc.alloc_sbuf_tensor(name + "_gtm", [4, 2048], F32)
    t_fm = nc.alloc_sbuf_tensor(name + "_tfm", [128, 64], BF16)
    g_fm = nc.alloc_sbuf_tensor(name + "_gfm", [128, 256], BF16)
    r4a = nc.alloc_sbuf_tensor(name + "_r4a", [4, 1], F32)
    r4b = nc.alloc_sbuf_tensor(name + "_r4b", [4, 1], F32)
    bcv = nc.alloc_sbuf_tensor(name + "_bcv", [128, 16], F32)

    tslots = [(PS[2], 0), (PS[2], 512), (PS[3], 0), (PS[3], 512)]
    tsl_last = [None] * 4
    tsl_i = [0]

    def transpose_seq(srcs, dsts, after_first=()):
        toks = []
        for i, (sap, dap) in enumerate(zip(srcs, dsts)):
            j = tsl_i[0] % 4
            tsl_i[0] += 1
            reg, off = tslots[j]
            deps = [C.ident_tok, tsl_last[j]] + (
                [t for t in after_first if t is not None] if i == 0 else [])
            tt = p.op("tensor", lambda e, sap=sap, reg=reg, off=off:
                e.transpose(reg.ap()[:, off:off + 4], sap,
                            C.ident_f.ap()[0:4, 0:4]),
                after=deps, inc=True)
            ev = p.op("vector", lambda e, dap=dap, reg=reg, off=off:
                e.tensor_copy(out=dap, in_=reg.ap()[:, off:off + 4]),
                after=[tt], inc=True)
            tsl_last[j] = ev
            toks.append(ev)
        return toks

    def mm_tm(src_fm_chunks, w_dram, kin, nout, src_tok, stream, evac_fn,
              after_ps=()):
        guard = list(after_ps)
        all_toks = []
        nq = nout // 2048
        for q in range(nq):
            last = None
            for k in range(kin // 128):
                pr = 2 * (k % 2)
                lt = [None, None]
                for h in range(2):
                    s = pr + h
                    r0 = ((q * (kin // 128) + k) * 2 + h) * 128
                    lt[h] = wdma(
                        s, wsl[s].ap(), w_dram[r0:r0 + 128, :],
                        after=[prev_wuse[s]], stream=stream)
                rd = [None, None]
                for n0 in range(0, 2048, 512):
                    reg = PS[0] if n0 < 1024 else PS[1]
                    c0 = n0 % 1024
                    h = n0 // 1024
                    deps = [lt[h], src_tok]
                    if k == 0 and n0 == 0:
                        deps += guard
                    last = p.op("tensor",
                        lambda e, k=k, reg=reg, c0=c0, s=pr + h, kin=kin:
                        e.matmul(reg.ap()[0:4, c0:c0 + 512],
                                 src_fm_chunks[k],
                                 wsl[s].ap()[:, c0:c0 + 512],
                                 start=(k == 0), stop=(k == kin // 128 - 1)),
                        after=deps, inc=True)
                    rd[h] = last
                for h in range(2):
                    prev_wuse[pr + h] = rd[h]
            guard = evac_fn(q, last)
            all_toks += guard
        return all_toks

    xn_chunks = [xn.ap()[:, 4 * i:4 * (i + 1)] for i in range(NF)]

    def agg_evac(q, mtok):
        toks = []
        for half in range(2):
            reg = PS[0] if half == 0 else PS[1]
            toks.append(p.op("vector",
                lambda e, reg=reg, half=half, q=q: e.tensor_add(
                    out=th_tm.ap()[:, 2048 * q + 1024 * half:
                                   2048 * q + 1024 * (half + 1)],
                    in0=reg.ap()[0:4, 0:1024],
                    in1=b_agg.ap()[:, 1024 * half:1024 * (half + 1)]),
                after=[mtok, sm_tok], inc=True))
        return toks

    th_toks = mm_tm(xn_chunks, W["agg_w"], 2048, 2048, t_ln2[-1], "wA",
                    agg_evac, after_ps=[p.now("vector"), p.now("scalar")])
    th_tok = th_toks[-1]
    bd1_use = [None]

    for step in range(4):
        t1 = p.op("vector", lambda e: e.reduce_sum(
            out=r4a.ap(), in_=th_tm.ap(), axis=AX.X), after=[th_tok],
            inc=True)
        t2 = p.op("vector", lambda e: e.tensor_scalar_mul(
            out=r4a.ap(), in0=r4a.ap(), scalar1=1.0 / 2048), after=[t1],
            inc=True)
        t3 = p.op("vector", lambda e: e.tensor_scalar(
            out=t_tm.ap(), in0=th_tm.ap(), scalar1=r4a.ap(),
            scalar2=None, op0=ALU.subtract), after=[t2], inc=True)
        t4 = p.op("vector", lambda e: e.scalar_tensor_tensor(
            out=g_tm.ap(), in0=t_tm.ap(), scalar=1.0, in1=t_tm.ap(),
            op0=ALU.mult, op1=ALU.mult, accum_out=r4b.ap()),
            after=[t3], inc=True)
        t5 = p.op("scalar", lambda e: e.activation(
            out=r4b.ap(), in_=r4b.ap(), func=AF.Ln,
            bias=C.constf(1e-6), scale=C.constf(1.0 / 2048)),
            after=[t4], inc=True)
        t6 = p.op("scalar", lambda e: e.activation(
            out=r4b.ap(), in_=r4b.ap(), func=AF.Exp, scale=-0.5),
            after=[t5], inc=True)
        t7 = p.op("vector", lambda e: e.tensor_scalar_mul(
            out=t_tm.ap(), in0=t_tm.ap(), scalar1=r4b.ap()),
            after=[t6], inc=True)
        t8 = p.op("vector", lambda e: e.tensor_mul(
            out=t_tm.ap(), in0=t_tm.ap(), in1=png.ap()), after=[t7, sm_tok],
            inc=True)
        t9 = p.op("vector", lambda e: e.tensor_add(
            out=t_tm.ap(), in0=t_tm.ap(), in1=pnb.ap()), after=[t8],
            inc=True)
        tf_toks = transpose_seq(
            [t_tm.ap()[0:4, 128 * i:128 * (i + 1)] for i in range(NF)],
            [t_fm.ap()[:, 4 * i:4 * (i + 1)] for i in range(NF)],
            after_first=[t9])
        tf_tok = tf_toks[-1]
        tfc = [t_fm.ap()[:, 4 * i:4 * (i + 1)] for i in range(NF)]
        gf_toks = []

        def d1_evac(q, mtok):
            bt = p.dma("gpsimd", b_d1q.ap(),
                       W["d1b4"][:, 2048 * q:2048 * (q + 1)],
                       after=[bd1_use[0]], stream="qbd1")
            toks = []
            for half in range(2):
                reg = PS[0] if half == 0 else PS[1]
                c0 = 1024 * half
                ta = p.op("vector", lambda e, reg=reg, c0=c0:
                    e.tensor_add(
                        out=g_tm.ap()[:, c0:c0 + 1024],
                        in0=reg.ap()[0:4, 0:1024],
                        in1=b_d1q.ap()[:, c0:c0 + 1024]),
                    after=[mtok, bt], inc=True)
                bd1_use[0] = ta
                toks.append(p.op("scalar", lambda e, c0=c0: e.activation(
                    out=g_tm.ap()[:, c0:c0 + 1024],
                    in_=g_tm.ap()[:, c0:c0 + 1024], func=AF.Gelu),
                    after=[ta], inc=True))
            gf_toks.extend(transpose_seq(
                [g_tm.ap()[0:4, 128 * i:128 * (i + 1)] for i in range(16)],
                [g_fm.ap()[:, 4 * (16 * q + i):4 * (16 * q + i + 1)]
                 for i in range(16)],
                after_first=toks))
            return [gf_toks[-1]]

        mm_tm(tfc, W["d1_w"], 2048, 8192, tf_tok, "wB", d1_evac,
              after_ps=[th_tok])
        gf_tok = gf_toks[-1]
        tb = p.op("vector", lambda e: e.tensor_add(
            out=th_tm.ap(), in0=th_tm.ap(), in1=b_d2.ap()),
            after=[th_tok, sm_tok], inc=True)
        gfc = [g_fm.ap()[:, 4 * i:4 * (i + 1)] for i in range(64)]

        def d2_evac(q, mtok):
            toks = []
            for half in range(2):
                reg = PS[0] if half == 0 else PS[1]
                c0 = 1024 * half
                toks.append(p.op("vector", lambda e, reg=reg, c0=c0:
                    e.tensor_add(
                        out=th_tm.ap()[:, c0:c0 + 1024],
                        in0=reg.ap()[0:4, 0:1024],
                        in1=th_tm.ap()[:, c0:c0 + 1024]),
                    after=[mtok, tb], inc=True))
            return toks

        th_toks = mm_tm(gfc, W["d2_w"], 8192, 2048, gf_tok, "wB", d2_evac,
                        after_ps=gf_toks[-1:])
        th_tok = th_toks[-1]

    tf_toks = transpose_seq(
        [th_tm.ap()[0:4, 128 * i:128 * (i + 1)] for i in range(NF)],
        [t_fm.ap()[:, 4 * i:4 * (i + 1)] for i in range(NF)],
        after_first=[th_tok])
    tfc = [t_fm.ap()[:, 4 * i:4 * (i + 1)] for i in range(NF)]
    def bc_evac(q, mtok):
        toks = []
        for half in range(2):
            reg = PS[0] if half == 0 else PS[1]
            c0 = 1024 * half
            toks.append(p.op("vector", lambda e, reg=reg, c0=c0:
                e.tensor_add(
                    out=t_tm.ap()[:, c0:c0 + 1024],
                    in0=reg.ap()[0:4, 0:1024],
                    in1=b_bc.ap()[:, c0:c0 + 1024]), after=[mtok], inc=True))
        return toks

    bc_toks = mm_tm(tfc, W["bc_w"], 2048, 2048, tf_toks[-1], "wA",
                    bc_evac, after_ps=[th_tok])
    bcv_toks = []
    prev_ev = None
    for i in range(NF):
        tt = p.op("tensor", lambda e, i=i: e.matmul(
            PS[2].ap()[0:128, 0:1], t_tm.ap()[0:4, 128 * i:128 * (i + 1)],
            selc.ap(), start=True, stop=True),
            after=bc_toks + [sm_tok, prev_ev], inc=True)
        ev = p.op("vector", lambda e, i=i: e.tensor_copy(
            out=bcv.ap()[:, i:i + 1], in_=PS[2].ap()[0:128, 0:1]),
            after=[tt], inc=True)
        prev_ev = ev
        bcv_toks.append(ev)
    bcv_tok = bcv_toks[-1]

    OUTb = [nc.alloc_sbuf_tensor(f"{name}_o{i}", [128, T], BF16)
            for i in range(NF)]
    gsl = wsl
    gate_bf = nc.alloc_sbuf_tensor(name + "_gatebf", [128, T], BF16)
    prev_guse = prev_wuse
    prev_sig = None
    out_toks = []
    for m in range(NF):
        pr = 2 * (m % 2)
        lt = [None, None]
        for h in range(2):
            s = pr + h
            r0 = (2 * m + h) * 128
            lt[h] = wdma(s, gsl[s].ap(), W["gate_w"][r0:r0 + 128, :],
                         after=[prev_guse[s]], stream="wg")
        last = None
        for k in range(NF):
            h = k // 8
            for n0 in range(0, T, 512):
                deps = [lt[h], y_toks[k]]
                if k == 0 and n0 == 0:
                    deps += [prev_sig, bcv_tok]
                last = p.op("tensor",
                    lambda e, k=k, n0=n0, s=pr + h, kc=k % 8: e.matmul(
                        PS[3].ap()[:, n0:n0 + 512],
                        gsl[s].ap()[:, 128 * kc:128 * (kc + 1)],
                        Y[k].ap()[:, n0:n0 + 512],
                        start=(k == 0), stop=(k == NF - 1)),
                    after=deps, inc=(k == NF - 1 and n0 + 512 >= T))
        for h in range(2):
            prev_guse[pr + h] = last
        tsig = p.op("scalar", lambda e, m=m: e.activation(
            out=gate_bf.ap()[:, :T], in_=PS[3].ap()[:, :T], func=AF.Sigmoid,
            bias=gb.ap()[:, m:m + 1]),
            after=[last, sm_tok0,
                   out_toks[-1] if out_toks else None], inc=True)
        prev_sig = tsig
        out_toks.append(p.op("vector", lambda e, m=m: e.scalar_tensor_tensor(
            out=OUTb[m].ap(), in0=gate_bf.ap()[:, :T],
            scalar=bcv.ap()[:, m:m + 1], in1=Y[m].ap(),
            op0=ALU.mult, op1=ALU.add), after=[tsig, bcv_tok], inc=True))

    ln_after = out_toks + [p.now("vector"), p.now("scalar")]
    lno = C.layernorm([OUTb[i].ap() for i in range(NF)],
                      [Y[i].ap() for i in range(NF)], eps=1e-6,
                      gamma=[outg.ap()[:, i:i + 1] for i in range(NF)],
                      beta=[outb.ap()[:, i:i + 1] for i in range(NF)],
                      after=ln_after)
    r_toks = []
    for i in range(NF):
        r_toks.append(p.op("vector", lambda e, i=i: e.scalar_tensor_tensor(
            out=R[i].ap(), in0=Y[i].ap(), scalar=0.5, in1=R[i].ap(),
            op0=ALU.mult, op1=ALU.add), after=[lno[i]], inc=True))
    return r_toks


def build_f0_launch(T=1024):
    nc = bass.Bass("TRN2", target_bir_lowering=False, debug=False)
    x_d = nc.dram_tensor("x", [2048, T], F32, kind="ExternalInput").ap()
    names = {
        "xlast": [128, 64], "sel": [4, 1],
        "agg_w": [4096, 1024], "d1_w": [16384, 1024],
        "d2_w": [16384, 1024], "bc_w": [4096, 1024],
        "gate_w": [4096, 1024],
        "agg_b4": [4, 2048], "d1b4": [4, 8192], "d2b4": [4, 2048],
        "bc_b4": [4, 2048], "pn_g4": [4, 2048], "pn_b4": [4, 2048],
        "in_g": [128, 16], "in_b": [128, 16], "gate_b": [128, 16],
        "out_g": [128, 16], "out_b": [128, 16],
    }
    W = {k: nc.dram_tensor(
            k, sh,
            BF16 if k in ("agg_w", "d1_w", "d2_w", "bc_w", "gate_w") else F32,
            kind="ExternalInput").ap()
         for k, sh in names.items()}
    xo_d = nc.dram_tensor("xo", [2048, T], F32, kind="ExternalOutput").ap()
    p = Plan(nc)
    C = Common(nc, p, T)
    NF = 16
    R = [nc.alloc_sbuf_tensor(f"R{i}", [128, T], F32) for i in range(NF)]
    Y = [nc.alloc_sbuf_tensor(f"Yb{i}", [128, T], BF16) for i in range(NF)]
    xt = [p.dma("sync", R[i].ap(), x_d[128 * i:128 * (i + 1), :],
                stream="qx") for i in range(NF)]
    ln_toks = C.layernorm([R[i].ap() for i in range(NF)],
                          [Y[i].ap() for i in range(NF)],
                          eps=1e-5, after=xt)
    emit_f0(nc, p, C, Y, R, T, W, y_toks=ln_toks)
    for i in range(NF):
        p.dma("sync", xo_d[128 * i:128 * (i + 1), :], R[i].ap(),
              after=[p.now("vector")], stream="qout")
    p.build()
    return nc


def f0_inputs(kw, prefix, core):
    b = core // 2
    sel = np.zeros((4, 1), np.float32)
    sel[b, 0] = 1.0
    t4 = lambda v: np.tile(np.asarray(v, np.float32)[None, :], (4, 1))
    fmc = lambda v: np.ascontiguousarray(
        np.asarray(v, np.float32).reshape(16, 128).T)
    return {
        "sel": sel,
        "agg_b4": t4(kw[prefix + "agg_b"]),
        "d1b4": t4(kw[prefix + "d1_b"]),
        "d2b4": t4(kw[prefix + "d2_b"]),
        "bc_b4": t4(kw[prefix + "bc_b"]),
        "pn_g4": t4(kw[prefix + "pn_gamma"]),
        "pn_b4": t4(kw[prefix + "pn_beta"]),
        "in_g": fmc(kw[prefix + "in_gamma"]),
        "in_b": fmc(kw[prefix + "in_beta"]),
        "gate_b": fmc(kw[prefix + "gate_b"]),
        "out_g": fmc(kw[prefix + "out_gamma"]),
        "out_b": fmc(kw[prefix + "out_beta"]),
    }


def run_f0(x_shards, xlast, kw, prefix="L0_ffn_", cast=None):
    if cast is None:
        cast = run_cast(kw)
    nc = build_f0_launch()
    wmap = {"agg_w": cast["f0_agg"], "d1_w": cast["f0_d1"],
            "d2_w": cast["f0_d2"], "bc_w": cast["f0_bc"],
            "gate_w": cast["f0_gate"]}
    in_maps = []
    for c in range(8):
        m = f0_inputs(kw, prefix, c)
        m.update(wmap)
        m["x"] = x_shards[c]
        m["xlast"] = xlast
        in_maps.append(m)
    res = run_bass_kernel_spmd(nc, in_maps, core_ids=list(range(8)))
    return [res.results[c]["xo"] for c in range(8)]




def build_a0_launch():
    nc = bass.Bass("TRN2", target_bir_lowering=False, debug=False)
    x_d = nc.dram_tensor("x", [2048, 1027], F32, kind="ExternalInput").ap()
    names = {
        "w_inT": [2048, 8512], "w_outT": [4096, 2048],
        "conv_w": [128, 136], "conv_b": [128, 34],
        "dt_bias": [64, 1], "A_log": [64, 1],
        "D_fm": [128, 32], "norm_w": [128, 32],
    }
    W = {k: nc.dram_tensor(
            k, sh, BF16 if k in ("w_inT", "w_outT") else F32,
            kind="ExternalInput").ap()
         for k, sh in names.items()}
    xo_d = nc.dram_tensor("xo", [2048, 1024], F32, kind="ExternalOutput").ap()
    lt_np = (np.arange(256)[:, None] <= np.arange(256)[None, :]
             ).astype(np.float32)
    ltc_d = [nc.inline_tensor(np.ascontiguousarray(
        lt_np[128 * s:128 * (s + 1)]), name=f"ltc{s}") for s in range(2)]
    mk = []
    for sc in range(2):
        m = (np.arange(256)[None, :] >= (np.arange(128)[:, None] + 128 * sc)
             ).astype(BF)
        mk.append(nc.inline_tensor(np.ascontiguousarray(m), name=f"mkc{sc}"))

    p = Plan(nc)
    C = Common(nc, p, 512, ps_shapes=(1024, 512, 512))
    psA, psB, psC = C.PS[:3]
    C.ln_ps = (psA, psA, 512, psB, psC)
    PSD2 = nc.alloc_psum_tensor("a_psd2", [128, 512], F32)
    PSY = nc.alloc_psum_tensor("a_psy", [128, 1024], F32)
    PSTB = nc.alloc_psum_tensor("a_pstb", [128, 512], BF16)

    NF = 16
    R = [nc.alloc_sbuf_tensor(f"R{i}", [128, 1027], F32) for i in range(NF)]
    Yt = [nc.alloc_sbuf_tensor(f"Yt{i}", [128, 259], BF16) for i in range(NF)]
    hid = [nc.alloc_sbuf_tensor(f"a_hid{i}", [128, 259], BF16)
           for i in range(32)]
    gate = [nc.alloc_sbuf_tensor(f"a_gate{i}", [128, 256], BF16)
            for i in range(32)]
    Bt = nc.alloc_sbuf_tensor("a_Bt", [128, 259], BF16)
    Ct = nc.alloc_sbuf_tensor("a_Ct", [128, 259], BF16)
    dt_raw = nc.alloc_sbuf_tensor("a_dtraw", [64, 256], F32)
    dt_t1 = nc.alloc_sbuf_tensor("a_dtt1", [64, 256], F32)
    dt_t2 = nc.alloc_sbuf_tensor("a_dtt2", [64, 256], F32)
    dtv = nc.alloc_sbuf_tensor("a_dtv", [64, 256], F32)
    At = nc.alloc_sbuf_tensor("a_At", [64, 256], F32)
    acc = nc.alloc_sbuf_tensor("a_acc", [128, 256], F32)
    cw = nc.alloc_sbuf_tensor("a_cw", [128, 136], F32)
    cb = nc.alloc_sbuf_tensor("a_cb", [128, 34], F32)
    dtb = nc.alloc_sbuf_tensor("a_dtb", [64, 1], F32)
    acol = nc.alloc_sbuf_tensor("a_acol", [64, 1], F32)
    dfm = nc.alloc_sbuf_tensor("a_dfm", [128, 32], F32)
    nw = nc.alloc_sbuf_tensor("a_nw", [128, 32], F32)
    ltc = [nc.alloc_sbuf_tensor(f"a_ltc{s}", [128, 256], F32)
           for s in range(2)]
    mkc = [nc.alloc_sbuf_tensor(f"a_mkc{s}", [128, 256], BF16)
           for s in range(2)]
    atm = nc.alloc_sbuf_tensor("a_atm", [128, 128], F32)
    acum = nc.alloc_sbuf_tensor("a_acum", [64, 256], F32)
    nacumT = nc.alloc_sbuf_tensor("a_nacumT", [128, 128], F32)
    dkf = nc.alloc_sbuf_tensor("a_dkf", [64, 256], F32)
    sdf = nc.alloc_sbuf_tensor("a_sdf", [64, 256], F32)
    dkT = nc.alloc_sbuf_tensor("a_dkT", [128, 128], F32)
    sdT = nc.alloc_sbuf_tensor("a_sdT", [128, 128], F32)
    dtvT = nc.alloc_sbuf_tensor("a_dtvT", [128, 128], F32)
    hdtT = [nc.alloc_sbuf_tensor(f"a_hdtT{s}", [128, 4096], BF16)
            for s in range(2)]
    btm = nc.alloc_sbuf_tensor("a_btm", [128, 256], BF16)
    bdtm = nc.alloc_sbuf_tensor("a_bdtm", [128, 256], BF16)
    gmm = [nc.alloc_sbuf_tensor(f"a_gmm{s}", [128, 256], BF16)
           for s in range(2)]
    d2sb = nc.alloc_sbuf_tensor("a_d2sb", [128, 512], F32)
    mtsb = nc.alloc_sbuf_tensor("a_mtsb", [128, 512], BF16)
    pmt = [nc.alloc_sbuf_tensor(f"a_pmt{s}", [128, 512], BF16)
           for s in range(2)]
    stsb = nc.alloc_sbuf_tensor("a_stsb", [128, 64], BF16)
    yosb = nc.alloc_sbuf_tensor("a_yosb", [128, 128], F32)
    ysb = nc.alloc_sbuf_tensor("a_ysb", [128, 32 * 256], BF16)
    wsl = [nc.alloc_sbuf_tensor(f"a_wsl{i}", [128, 2048], BF16)
           for i in range(2)]
    wo = [nc.alloc_sbuf_tensor(f"a_wo{i}", [128, 4096], BF16)
          for i in range(2)]

    xt = [p.dma("sync", R[i].ap(), x_d[128 * i:128 * (i + 1), :],
                stream="qx") for i in range(NF)]
    sm = []
    for dst, key in ((cw, "conv_w"), (cb, "conv_b"), (dtb, "dt_bias"),
                     (acol, "A_log"), (dfm, "D_fm"), (nw, "norm_w")):
        sm.append(p.dma("sync", dst.ap(), W[key], stream="qsm"))
    for s in range(2):
        sm.append(p.dma("sync", ltc[s].ap(), ltc_d[s].ap(), stream="qsm"))
        sm.append(p.dma("sync", mkc[s].ap(), mk[s].ap(), stream="qsm"))
    sm_tok = sm[-1]
    ta = p.op("scalar", lambda e: e.activation(
        out=acol.ap(), in_=acol.ap(), func=AF.Exp), after=[sm_tok], inc=True)
    acol_tok = p.op("vector", lambda e: e.tensor_scalar_mul(
        out=acol.ap(), in0=acol.ap(), scalar1=-1.0), after=[ta], inc=True)

    prev_wuse = [None, None]
    prev_wouse = [None, None]
    r_upd = [Tok("qx", p.qcnt["qx"])] * NF
    YH = nc.alloc_sbuf_tensor("a_YH", [128, 48], BF16)
    yh_toks = C.layernorm(
        [R[i].ap()[:, 0:3] for i in range(NF)],
        [YH.ap()[:, 3 * i:3 * (i + 1)] for i in range(NF)], T=3,
        eps=1e-5, after=xt + [acol_tok])

    for cix in range(4):
        col0 = 3 + 256 * cix
        ln_after = r_upd + [p.now("tensor"), p.now("scalar")]
        if cix == 0:
            ln_after += yh_toks
        yt_toks = C.layernorm(
            [R[i].ap()[:, col0:col0 + 256] for i in range(NF)],
            [Yt[i].ap()[:, 3:259] for i in range(NF)], T=256, eps=1e-5,
            after=ln_after)
        nyt = []
        for i in range(NF):
            c1 = p.op("vector", lambda e, i=i: e.tensor_copy(
                out=Yt[i].ap()[:, 0:3], in_=YH.ap()[:, 3 * i:3 * (i + 1)]),
                after=[yt_toks[i]], inc=True)
            c2 = p.op("vector", lambda e, i=i: e.tensor_copy(
                out=YH.ap()[:, 3 * i:3 * (i + 1)],
                in_=Yt[i].ap()[:, 256:259]), after=[c1], inc=True)
            nyt.append(c2)
        yt_toks = nyt
        ev_toks = {}
        prev_ev = None
        for m in range(67):
            mw = 128 if m < 66 else 64
            s = m % 2
            dstw = wsl[s].ap()[:, 0:16 * mw].rearrange(
                "p (k c) -> p k c", k=16)
            lt = p.dma("gpsimd", dstw,
                       W["w_inT"][:, 128 * m:128 * m + mw]
                       .rearrange("(k p) c -> p k c", p=128),
                       after=[prev_wuse[s]], stream=f"wi{s}")
            last = None
            for k in range(16):
                deps = [lt, yt_toks[k]]
                if k == 0:
                    deps.append(prev_ev)
                    if m == 0:
                        deps.append(yt_toks[-1])
                last = p.op("tensor", lambda e, k=k, s=s, mw=mw: e.matmul(
                    psC.ap()[0:mw, 0:259],
                    wsl[s].ap()[:, mw * k:mw * (k + 1)],
                    Yt[k].ap()[:, 0:259],
                    start=(k == 0), stop=(k == 15)), after=deps,
                    inc=(k == 15))
            prev_wuse[s] = last
            if m < 32:
                ev = p.op("vector", lambda e, m=m: e.tensor_copy(
                    out=gate[m].ap(), in_=psC.ap()[:, 3:259]),
                    after=[last], inc=True)
            elif m < 64:
                ev = p.op("vector", lambda e, m=m: e.tensor_copy(
                    out=hid[m - 32].ap(), in_=psC.ap()[:, 0:259]),
                    after=[last], inc=True)
            elif m == 64:
                ev = p.op("vector", lambda e: e.tensor_copy(
                    out=Bt.ap(), in_=psC.ap()[:, 0:259]), after=[last],
                    inc=True)
            elif m == 65:
                ev = p.op("vector", lambda e: e.tensor_copy(
                    out=Ct.ap(), in_=psC.ap()[:, 0:259]), after=[last],
                    inc=True)
            else:
                ev = p.op("vector", lambda e: e.tensor_copy(
                    out=dt_raw.ap(), in_=psC.ap()[0:64, 3:259]),
                    after=[last], inc=True)
            ev_toks[m] = ev
            prev_ev = ev
        conv_toks = {}
        prev_silu = None
        for f in range(34):
            srcap = hid[f] if f < 32 else (Bt if f == 32 else Ct)
            t0 = p.op("vector", lambda e, f=f, srcap=srcap: e.tensor_scalar(
                out=acc.ap(), in0=srcap.ap()[:, 0:256],
                scalar1=cw.ap()[:, 4 * f:4 * f + 1],
                scalar2=cb.ap()[:, f:f + 1], op0=ALU.mult, op1=ALU.add),
                after=[ev_toks[f + 32], sm_tok, prev_silu], inc=True)
            for i in (1, 2, 3):
                t0 = p.op("vector", lambda e, f=f, i=i, srcap=srcap:
                    e.scalar_tensor_tensor(
                        out=acc.ap(), in0=srcap.ap()[:, i:256 + i],
                        scalar=cw.ap()[:, 4 * f + i:4 * f + i + 1],
                        in1=acc.ap(), op0=ALU.mult, op1=ALU.add),
                    after=[t0], inc=True)
            conv_toks[f] = p.op("scalar", lambda e, srcap=srcap: e.activation(
                out=srcap.ap()[:, 0:256], in_=acc.ap(), func=AF.Silu),
                after=[t0], inc=True)
            prev_silu = conv_toks[f]
        d1 = p.op("vector", lambda e: e.tensor_scalar_add(
            out=dt_t1.ap(), in0=dt_raw.ap(), scalar1=dtb.ap()),
            after=[ev_toks[66], sm_tok], inc=True)
        d2_ = p.op("vector", lambda e: e.tensor_scalar_max(
            out=dt_t2.ap(), in0=dt_t1.ap(), scalar1=0.0), after=[d1],
            inc=True)
        d3 = p.op("scalar", lambda e: e.activation(
            out=dt_t1.ap(), in_=dt_t1.ap(), func=AF.Abs), after=[d1, d2_],
            inc=True)
        d4 = p.op("scalar", lambda e: e.activation(
            out=dt_t1.ap(), in_=dt_t1.ap(), func=AF.Exp, scale=-1.0),
            after=[d3], inc=True)
        d5 = p.op("scalar", lambda e: e.activation(
            out=dt_t1.ap(), in_=dt_t1.ap(), func=AF.Ln, bias=1.0),
            after=[d4], inc=True)
        d6 = p.op("vector", lambda e: e.tensor_add(
            out=dtv.ap(), in0=dt_t2.ap(), in1=dt_t1.ap()),
            after=[d2_, d5], inc=True)
        at_tok = p.op("vector", lambda e: e.tensor_scalar_mul(
            out=At.ap(), in0=dtv.ap(), scalar1=acol.ap()),
            after=[d6, acol_tok], inc=True)
        atm_toks = []
        prev_t = Tok("evector", p.ecnt["vector"])
        for sc in range(2):
            tt = p.op("tensor", lambda e, sc=sc: e.transpose(
                psB.ap()[:, 0:64], At.ap()[0:64, 128 * sc:128 * (sc + 1)],
                C.ident_f.ap()[0:64, 0:64]),
                after=[at_tok, C.ident_tok, prev_t], inc=True)
            ev = p.op("vector", lambda e, sc=sc: e.tensor_copy(
                out=atm.ap()[:, 64 * sc:64 * (sc + 1)],
                in_=psB.ap()[:, 0:64]), after=[tt], inc=True)
            prev_t = ev
            atm_toks.append(ev)
        mm = None
        for sc in range(2):
            mm = p.op("tensor", lambda e, sc=sc: e.matmul(
                psA.ap()[0:64, 0:256], atm.ap()[:, 64 * sc:64 * (sc + 1)],
                ltc[sc].ap(), start=(sc == 0), stop=(sc == 1)),
                after=[atm_toks[sc], sm_tok, p.now("vector")], inc=(sc == 1))
        acum_tok = p.op("vector", lambda e: e.tensor_copy(
            out=acum.ap(), in_=psA.ap()[0:64, 0:256]), after=[mm], inc=True)
        nac_toks = []
        for lc in range(2):
            mm2 = None
            for sc in range(2):
                mm2 = p.op("tensor", lambda e, sc=sc, lc=lc: e.matmul(
                    psB.ap()[:, 0:64],
                    ltc[sc].ap()[:, 128 * lc:128 * (lc + 1)],
                    atm.ap()[:, 64 * sc:64 * (sc + 1)],
                    start=(sc == 0), stop=(sc == 1)),
                    after=[atm_toks[sc], prev_t], inc=(sc == 1))
            ev = p.op("vector", lambda e, lc=lc: e.tensor_scalar_mul(
                out=nacumT.ap()[:, 64 * lc:64 * (lc + 1)],
                in0=psB.ap()[:, 0:64], scalar1=-1.0), after=[mm2], inc=True)
            prev_t = ev
            nac_toks.append(ev)
        dk = p.op("scalar", lambda e: e.activation(
            out=dkf.ap(), in_=acum.ap(), func=AF.Exp, scale=-1.0,
            bias=acum.ap()[:, 255:256]), after=[acum_tok], inc=True)
        sd = p.op("scalar", lambda e: e.activation(
            out=sdf.ap(), in_=acum.ap(), func=AF.Exp), after=[acum_tok],
            inc=True)
        dkT_toks, sdT_toks, dtvT_toks = [], [], []
        for sc in range(2):
            tt = p.op("tensor", lambda e, sc=sc: e.transpose(
                psB.ap()[:, 0:64], dkf.ap()[0:64, 128 * sc:128 * (sc + 1)],
                C.ident_f.ap()[0:64, 0:64]), after=[dk, prev_t], inc=True)
            ev = p.op("vector", lambda e, sc=sc: e.tensor_copy(
                out=dkT.ap()[:, 64 * sc:64 * (sc + 1)],
                in_=psB.ap()[:, 0:64]), after=[tt], inc=True)
            prev_t = ev
            dkT_toks.append(ev)
            tt = p.op("tensor", lambda e, sc=sc: e.transpose(
                psB.ap()[:, 0:64], sdf.ap()[0:64, 128 * sc:128 * (sc + 1)],
                C.ident_f.ap()[0:64, 0:64]), after=[sd, prev_t], inc=True)
            ev = p.op("vector", lambda e, sc=sc: e.tensor_copy(
                out=sdT.ap()[:, 64 * sc:64 * (sc + 1)],
                in_=psB.ap()[:, 0:64]), after=[tt], inc=True)
            prev_t = ev
            sdT_toks.append(ev)
            tt = p.op("tensor", lambda e, sc=sc: e.transpose(
                psB.ap()[:, 0:64], dtv.ap()[0:64, 128 * sc:128 * (sc + 1)],
                C.ident_f.ap()[0:64, 0:64]), after=[d6, prev_t], inc=True)
            ev = p.op("vector", lambda e, sc=sc: e.tensor_copy(
                out=dtvT.ap()[:, 64 * sc:64 * (sc + 1)],
                in_=psB.ap()[:, 0:64]), after=[tt], inc=True)
            prev_t = ev
            dtvT_toks.append(ev)
        dtvT_tok = dtvT_toks[-1]
        hdt_toks = {}
        for f in range(32):
            for sc in range(2):
                tt = p.op("tensor", lambda e, f=f, sc=sc: e.transpose(
                    PSTB.ap()[:, 0:128],
                    hid[f].ap()[:, 128 * sc:128 * (sc + 1)],
                    C.ident_b.ap()),
                    after=[conv_toks[f], prev_t, C.ident_tok], inc=True)
                evs = []
                for hh in range(2):
                    h = 2 * f + hh
                    evs.append(p.op("vector",
                        lambda e, f=f, sc=sc, hh=hh, h=h:
                        e.tensor_scalar_mul(
                            out=hdtT[sc].ap()[:, 128 * f + 64 * hh:
                                              128 * f + 64 * (hh + 1)],
                            in0=PSTB.ap()[:, 64 * hh:64 * (hh + 1)],
                            scalar1=dtvT.ap()[:, 64 * sc + h:
                                              64 * sc + h + 1]
                            if h < 64 else None),
                        after=[tt, dtvT_tok], inc=True))
                prev_t = evs[-1]
                hdt_toks[(f, sc)] = evs
        btm_toks = []
        for sc in range(2):
            tt = p.op("tensor", lambda e, sc=sc: e.transpose(
                PSTB.ap()[:, 128:256], Bt.ap()[:, 128 * sc:128 * (sc + 1)],
                C.ident_b.ap()), after=[conv_toks[32], prev_t, C.ident_tok],
                inc=True)
            ev = p.op("vector", lambda e, sc=sc: e.tensor_copy(
                out=btm.ap()[:, 128 * sc:128 * (sc + 1)],
                in_=PSTB.ap()[:, 128:256]), after=[tt], inc=True)
            prev_t = ev
            btm_toks.append(ev)
        gm_toks = []
        prev_gm = p.now("vector")
        for sc in range(2):
            mmg = p.op("tensor", lambda e, sc=sc: e.matmul(
                psB.ap()[0:128, 0:256], Bt.ap()[:, 128 * sc:128 * (sc + 1)],
                Ct.ap()[:, 0:256], start=True, stop=True),
                after=[conv_toks[32], conv_toks[33], prev_gm], inc=True)
            ev = p.op("vector", lambda e, sc=sc: e.tensor_mul(
                out=gmm[sc].ap(), in0=psB.ap()[0:128, 0:256],
                in1=mkc[sc].ap()), after=[mmg, sm_tok], inc=True)
            prev_gm = ev
            gm_toks.append(ev)
        y_toks = {}
        prev_grp_readers = [p.now("vector"), p.now("scalar")]
        prev_yd = None
        prev_st = None
        prev_yo = None
        for grp in range(32):
            h0 = 2 * grp
            mmd = None
            for j in range(2):
                h = h0 + j
                mmd = p.op("tensor", lambda e, h=h, j=j: e.matmul(
                    PSD2.ap()[:, 256 * j:256 * (j + 1)],
                    C.ident_f.ap()[0:64, h:h + 1].broadcast_to([64, 128]),
                    acum.ap(), start=True, stop=True),
                    after=[acum_tok, C.ident_tok] + prev_grp_readers,
                    inc=True)
            pmts = []
            for sc in range(2):
                bm_toks = []
                for j in range(2):
                    h = h0 + j
                    bm_toks.append(p.op("vector",
                        lambda e, sc=sc, j=j, h=h: e.tensor_scalar(
                            out=d2sb.ap()[:, 256 * j:256 * (j + 1)],
                            in0=PSD2.ap()[:, 256 * j:256 * (j + 1)],
                            scalar1=nacumT.ap()[:, 64 * sc + h:
                                                64 * sc + h + 1]
                            if h < 64 else None,
                            scalar2=0.0, op0=ALU.add, op1=ALU.min),
                        after=[mmd, nac_toks[sc]], inc=True))
                ex = p.op("scalar", lambda e: e.activation(
                    out=mtsb.ap(), in_=d2sb.ap(), func=AF.Exp),
                    after=bm_toks, inc=True)
                pm = p.op("vector", lambda e, sc=sc: e.tensor_mul(
                    out=pmt[sc].ap(), in0=mtsb.ap(),
                    in1=gmm[sc].ap()[:, None, :].broadcast_to([128, 2, 256])),
                    after=[ex, gm_toks[sc]], inc=True)
                pmts.append(pm)
            for j in range(2):
                h = h0 + j
                f, hh = h // 2, h % 2
                for sc in range(2):
                    p.op("tensor", lambda e, sc=sc, f=f, hh=hh, j=j:
                        e.matmul(
                            PSY.ap()[0:64, 0:256],
                            hdtT[sc].ap()[:, 128 * f + 64 * hh:
                                          128 * f + 64 * (hh + 1)],
                            pmt[sc].ap()[:, 256 * j:256 * (j + 1)],
                            start=(sc == 0), stop=False),
                        after=[pmts[sc], hdt_toks[(f, sc)][hh], prev_yd],
                        inc=False)
                bd_toks = []
                for sc in range(2):
                    bd_toks.append(p.op("vector",
                        lambda e, sc=sc, h=h: e.tensor_scalar_mul(
                            out=bdtm.ap()[:, 128 * sc:128 * (sc + 1)],
                            in0=btm.ap()[:, 128 * sc:128 * (sc + 1)],
                            scalar1=dkT.ap()[:, 64 * sc + h:64 * sc + h + 1]
                            if h < 64 else None),
                        after=[btm_toks[sc], dkT_toks[sc], prev_st,
                               prev_yd], inc=True))
                mms = None
                for sc in range(2):
                    mms = p.op("tensor", lambda e, sc=sc, f=f, hh=hh:
                        e.matmul(
                            PSY.ap()[0:128, 512:576],
                            bdtm.ap()[:, 128 * sc:128 * (sc + 1)],
                            hdtT[sc].ap()[:, 128 * f + 64 * hh:
                                          128 * f + 64 * (hh + 1)],
                            start=(sc == 0), stop=(sc == 1)),
                        after=[bd_toks[sc], hdt_toks[(f, sc)][hh]],
                        inc=(sc == 1))
                st_ev = p.op("vector", lambda e: e.tensor_copy(
                    out=stsb.ap(), in_=PSY.ap()[0:128, 512:576]),
                    after=[mms], inc=True)
                prev_st = st_ev
                yo_evs = []
                for lc in range(2):
                    mmo = p.op("tensor", lambda e, lc=lc: e.matmul(
                        PSY.ap()[0:128, 576:640],
                        Ct.ap()[:, 128 * lc:128 * (lc + 1)],
                        stsb.ap(), start=True, stop=True),
                        after=[st_ev, conv_toks[33], prev_yo], inc=True)
                    yo_evs.append(p.op("vector",
                        lambda e, lc=lc, h=h: e.tensor_scalar_mul(
                            out=yosb.ap()[:, 64 * lc:64 * (lc + 1)],
                            in0=PSY.ap()[0:128, 576:640],
                            scalar1=sdT.ap()[:, 64 * lc + h:64 * lc + h + 1]
                            if h < 64 else None),
                        after=[mmo, sdT_toks[lc]], inc=True))
                    prev_yo = yo_evs[-1]
                for lc in range(2):
                    p.op("tensor", lambda e, lc=lc: e.matmul(
                        PSY.ap()[0:64, 128 * lc:128 * (lc + 1)],
                        yosb.ap()[:, 64 * lc:64 * (lc + 1)],
                        C.ident_f.ap(), start=False, stop=(lc == 1),
                        is_transpose=True),
                        after=[yo_evs[lc], C.ident_tok], inc=(lc == 1))
                yev = p.op("vector", lambda e, f=f, hh=hh:
                    e.tensor_copy(
                        out=ysb.ap()[64 * hh:64 * (hh + 1),
                                     256 * f:256 * (f + 1)],
                        in_=PSY.ap()[0:64, 0:256]),
                    after=[p.now("tensor")], inc=True)
                prev_yd = yev
                y_toks[h] = yev
            prev_grp_readers = [p.now("vector"), p.now("scalar")]
        yf_toks = []
        for f in range(32):
            sil_tok = p.op("scalar", lambda e, f=f: e.activation(
                out=gate[f].ap(), in_=gate[f].ap(), func=AF.Silu),
                after=[ev_toks[f], y_toks[2 * f + 1]], inc=True)
            t1_ = p.op("vector", lambda e, f=f: e.scalar_tensor_tensor(
                out=ysb.ap()[:, 256 * f:256 * (f + 1)],
                in0=hid[f].ap()[:, 0:256], scalar=dfm.ap()[:, f:f + 1],
                in1=ysb.ap()[:, 256 * f:256 * (f + 1)],
                op0=ALU.mult, op1=ALU.add),
                after=[conv_toks[f], y_toks[2 * f], y_toks[2 * f + 1],
                       sm_tok], inc=True)
            yf_toks.append(p.op("vector", lambda e, f=f: e.tensor_mul(
                out=ysb.ap()[:, 256 * f:256 * (f + 1)],
                in0=ysb.ap()[:, 256 * f:256 * (f + 1)],
                in1=gate[f].ap()), after=[t1_, sil_tok], inc=True))
        sqs = None
        for f in range(32):
            tq = p.op("scalar", lambda e, f=f: e.activation(
                out=C.sq.ap()[:, 0:256],
                in_=ysb.ap()[:, 256 * f:256 * (f + 1)],
                func=AF.Square), after=[yf_toks[f], sqs], inc=True)
            sqs = p.op("tensor", lambda e, f=f: e.matmul(
                psA.ap()[0:1, 512:768], C.onescol.ap(), C.sq.ap()[:, 0:256],
                start=(f == 0), stop=(f == 31)),
                after=[tq, C.const_tok,
                       acum_tok if f == 0 else None], inc=True)
        v1 = p.op("scalar", lambda e: e.activation(
            out=C.r_v.ap()[0:1, 0:256], in_=psA.ap()[0:1, 512:768],
            func=AF.Ln, bias=C.constf(1e-5 * 4096), scale=1.0),
            after=[sqs], inc=True)
        v2 = p.op("scalar", lambda e: e.activation(
            out=C.r_v.ap()[0:1, 0:256], in_=C.r_v.ap()[0:1, 0:256],
            func=AF.Exp, scale=-0.5), after=[v1], inc=True)
        v3 = p.op("vector", lambda e: e.tensor_scalar_mul(
            out=C.r_v.ap()[0:1, 0:256], in0=C.r_v.ap()[0:1, 0:256],
            scalar1=64.0), after=[v2], inc=True)
        bmv = p.op("tensor", lambda e: e.matmul(
            psB.ap()[:, 0:256], C.onesrow_f.ap(), C.r_v.ap()[0:1, 0:256],
            start=True, stop=True), after=[v3, p.now("vector")], inc=True)
        yn_toks = []
        for f in range(32):
            yn_toks.append(p.op("vector", lambda e, f=f:
                e.scalar_tensor_tensor(
                    out=ysb.ap()[:, 256 * f:256 * (f + 1)],
                    in0=ysb.ap()[:, 256 * f:256 * (f + 1)],
                    scalar=nw.ap()[:, f:f + 1], in1=psB.ap()[:, 0:256],
                    op0=ALU.mult, op1=ALU.mult),
                after=[bmv, yf_toks[f], sm_tok], inc=True))
        prev_rupd = None
        for m in range(16):
            s = m % 2
            dstw = wo[s].ap().rearrange("p (k c) -> p k c", k=32)
            lt = p.dma("gpsimd", dstw,
                       W["w_outT"][:, 128 * m:128 * (m + 1)]
                       .rearrange("(k p) c -> p k c", p=128),
                       after=[prev_wouse[s]], stream=f"wo{s}")
            last = None
            for k in range(32):
                deps = [lt, yn_toks[k]]
                if k == 0:
                    deps.append(r_upd[m])
                    deps.append(prev_rupd)
                    if m < 2:
                        deps.append(ev_toks[66])
                last = p.op("tensor", lambda e, k=k, s=s: e.matmul(
                    psC.ap()[:, 0:256], wo[s].ap()[:, 128 * k:128 * (k + 1)],
                    ysb.ap()[:, 256 * k:256 * (k + 1)],
                    start=(k == 0), stop=(k == 31)), after=deps,
                    inc=(k == 31))
            prev_wouse[s] = last
            r_upd[m] = p.op("vector", lambda e, m=m, cix=cix: e.tensor_add(
                out=R[m].ap()[:, 3 + 256 * cix:3 + 256 * (cix + 1)],
                in0=psC.ap()[:, 0:256],
                in1=R[m].ap()[:, 3 + 256 * cix:3 + 256 * (cix + 1)]),
                after=[last], inc=True)
            prev_rupd = r_upd[m]

    for i in range(NF):
        p.dma("sync", xo_d[128 * i:128 * (i + 1), :], R[i].ap()[:, 3:1027],
              after=[r_upd[i]], stream="qout")
    p.build()
    return nc




def build_a1a_launch():
    nc = bass.Bass("TRN2", target_bir_lowering=False, debug=False)
    x_d = nc.dram_tensor("x", [2048, 2048], F32, kind="ExternalInput").ap()
    wq_d = nc.dram_tensor("wqT", [2048, 1024], BF16,
                          kind="ExternalInput").ap()
    wk_d = nc.dram_tensor("wkT", [2048, 1024], BF16,
                          kind="ExternalInput").ap()
    wv_d = nc.dram_tensor("wvT", [2048, 1024], BF16,
                          kind="ExternalInput").ap()
    sl_d = nc.dram_tensor("slopes8", [1, 8], F32, kind="ExternalInput").ap()
    oraw_d = nc.dram_tensor("oraw", [1024, 2048], BF16,
                            kind="ExternalOutput").ap()
    sqp_d = nc.dram_tensor("sqp", [1, 2048], F32, kind="ExternalOutput").ap()
    ar0 = nc.inline_tensor(np.arange(256, dtype=np.float32)[None, :],
                           name="ar0")
    ar1 = nc.inline_tensor((np.arange(256, dtype=np.float32) + 1)[None, :],
                           name="ar1")
    onesrow256 = nc.inline_tensor(np.ones((1, 256), np.float32), name="o256")
    mknp = []
    for sc in range(2):
        m = (np.arange(256)[None, :] >= (np.arange(128)[:, None] + 128 * sc)
             ).astype(BF)
        mknp.append(nc.inline_tensor(np.ascontiguousarray(m),
                                     name=f"a1mk{sc}"))

    p = Plan(nc)
    C = Common(nc, p, 256, ps_shapes=(512, 512, 512, 512))
    PS = C.PS
    PSTB = nc.alloc_psum_tensor("b_pstb", [128, 512], BF16)

    NF = 16
    Y = [nc.alloc_sbuf_tensor(f"Y{i}", [128, 2048], BF16) for i in range(NF)]
    xblk = [nc.alloc_sbuf_tensor(f"b_x{i}", [128, 256], BF16)
            for i in range(NF)]
    qf = [nc.alloc_sbuf_tensor(f"b_q{i}", [128, 2048], BF16)
          for i in range(8)]
    kf = [nc.alloc_sbuf_tensor(f"b_k{i}", [128, 2048], BF16)
          for i in range(8)]
    vt = [nc.alloc_sbuf_tensor(f"b_v{i}", [128, 1024], BF16)
          for i in range(16)]
    OUT = Y[0:8]
    wsl = [nc.alloc_sbuf_tensor(f"b_wsl{i}", [128, 2048], BF16)
           for i in range(2)]
    dd = [nc.alloc_sbuf_tensor(f"b_dd{i}", [128, 8 * 256], BF16)
          for i in range(2)]
    qd = nc.alloc_sbuf_tensor("b_qd", [128, 8 * 256], BF16)
    kdT = nc.alloc_sbuf_tensor("b_kdT", [128, 16], F32)
    bdc = nc.alloc_sbuf_tensor("b_bdc", [128, 8], F32)
    s8 = nc.alloc_sbuf_tensor("b_s8", [1, 8], F32)
    srow = nc.alloc_sbuf_tensor("b_srow", [1, 256], F32)
    nsrow = nc.alloc_sbuf_tensor("b_nsrow", [1, 256], F32)
    ar0s = nc.alloc_sbuf_tensor("b_ar0", [1, 256], F32)
    ar1s = nc.alloc_sbuf_tensor("b_ar1", [1, 256], F32)
    ones256 = nc.alloc_sbuf_tensor("b_o256", [1, 256], F32)
    l2 = nc.alloc_sbuf_tensor("b_l2", [2, 256], F32)
    r2 = nc.alloc_sbuf_tensor("b_r2", [2, 256], F32)
    mks = [nc.alloc_sbuf_tensor(f"b_mk{i}", [128, 256], BF16)
           for i in range(2)]
    ddscr = nc.alloc_sbuf_tensor("b_ddscr", [128, 256], F32)
    kv = nc.alloc_sbuf_tensor("b_kv", [128, 128], F32)
    kvb = nc.alloc_sbuf_tensor("b_kvb", [128, 128], BF16)
    qs_sb = nc.alloc_sbuf_tensor("b_qs", [128, 256], BF16)
    p_sb = nc.alloc_sbuf_tensor("b_p", [128, 512], BF16)
    kst = nc.alloc_sbuf_tensor("b_kst", [128, 256], BF16)
    sqp = nc.alloc_sbuf_tensor("b_sqp", [1, 512], F32)

    sm = [p.dma("sync", s8.ap(), sl_d, stream="qsm"),
          p.dma("sync", ar0s.ap(), ar0.ap(), stream="qsm"),
          p.dma("sync", ar1s.ap(), ar1.ap(), stream="qsm"),
          p.dma("sync", ones256.ap(), onesrow256.ap(), stream="qsm"),
          p.dma("sync", mks[0].ap(), mknp[0].ap(), stream="qsm"),
          p.dma("sync", mks[1].ap(), mknp[1].ap(), stream="qsm"),
          p.dma("sync", l2.ap()[1:2, :], onesrow256.ap(), stream="qsm"),
          p.dma("sync", r2.ap()[0:1, :], onesrow256.ap(), stream="qsm")]
    sm_tok = sm[-1]

    y_toks = []
    prev_blk = [None] * NF
    for blk in range(8):
        xt = [p.dma("gpsimd", xblk[i].ap(),
                    x_d[128 * i:128 * (i + 1), 256 * blk:256 * (blk + 1)],
                    after=[prev_blk[i]], stream=f"qx{i % 4}")
              for i in range(NF)]
        toks = C.layernorm(
            [xblk[i].ap() for i in range(NF)],
            [Y[i].ap()[:, 256 * blk:256 * (blk + 1)] for i in range(NF)],
            T=256, eps=1e-5, after=xt)
        for i in range(NF):
            prev_blk[i] = toks[i]
        y_toks.append(toks)
    yall = [y_toks[7][i] for i in range(NF)]

    stoks = []
    prev_ps3 = Tok("evector", p.ecnt["vector"])
    prev_mmr = None
    for h in range(8):
        t1 = p.op("vector", lambda e, h=h: e.tensor_scalar_mul(
            out=srow.ap(), in0=ar1s.ap(), scalar1=s8.ap()[0:1, h:h + 1]),
            after=[sm_tok], inc=True)
        t2 = p.op("scalar", lambda e: e.activation(
            out=srow.ap(), in_=srow.ap(), func=AF.Exp, scale=-1.0),
            after=[t1], inc=True)
        bm = p.op("tensor", lambda e: e.matmul(
            PS[3].ap()[:, 0:256], C.onesrow_f.ap(), srow.ap(),
            start=True, stop=True), after=[t2, prev_ps3, C.const_tok],
            inc=True)
        ev = p.op("vector", lambda e, h=h: e.tensor_copy(
            out=qd.ap()[:, 256 * h:256 * (h + 1)], in_=PS[3].ap()[:, 0:256]),
            after=[bm], inc=True)
        prev_ps3 = ev
        t3 = p.op("vector", lambda e, h=h: e.tensor_scalar(
            out=nsrow.ap(), in0=ar0s.ap(), scalar1=-1.0, scalar2=255.0,
            op0=ALU.mult, op1=ALU.add), after=[sm_tok], inc=True)
        t4 = p.op("vector", lambda e, h=h: e.tensor_scalar_mul(
            out=nsrow.ap(), in0=nsrow.ap(), scalar1=s8.ap()[0:1, h:h + 1]),
            after=[t3], inc=True)
        t5 = p.op("scalar", lambda e: e.activation(
            out=nsrow.ap(), in_=nsrow.ap(), func=AF.Exp, scale=-1.0),
            after=[t4], inc=True)
        for sc2 in range(2):
            tt = p.op("tensor", lambda e, sc2=sc2: e.transpose(
                PS[3].ap()[:, 256:257],
                nsrow.ap()[0:1, 128 * sc2:128 * (sc2 + 1)],
                C.ident_f.ap()[0:1, 0:1]),
                after=[t5, prev_ps3, C.ident_tok], inc=True)
            ev = p.op("vector", lambda e, h=h, sc2=sc2: e.tensor_copy(
                out=kdT.ap()[:, 2 * h + sc2:2 * h + sc2 + 1],
                in_=PS[3].ap()[:, 256:257]), after=[tt], inc=True)
            prev_ps3 = ev
        tn = p.op("vector", lambda e, h=h: e.tensor_scalar_mul(
            out=nsrow.ap(), in0=ar0s.ap(), scalar1=s8.ap()[0:1, h:h + 1]),
            after=[prev_ps3, prev_mmr], inc=True)
        tneg = p.op("vector", lambda e: e.tensor_scalar_mul(
            out=srow.ap(), in0=nsrow.ap(), scalar1=-1.0), after=[tn],
            inc=True)
        c1 = p.op("vector", lambda e: e.tensor_copy(
            out=l2.ap()[0:1, :], in_=nsrow.ap()), after=[tneg], inc=True)
        cdma = p.dma("sync", r2.ap()[1:2, :], srow.ap(),
                     after=[tneg, prev_mmr], stream="qr2")
        for sc2 in range(2):
            mmr = p.op("tensor", lambda e, sc2=sc2: e.matmul(
                PS[3].ap()[:, 0:256],
                l2.ap()[:, 128 * sc2:128 * (sc2 + 1)], r2.ap(),
                start=True, stop=True), after=[c1, cdma, sm_tok, prev_ps3],
                inc=True)
            prev_mmr = mmr
            tb = p.op("vector", lambda e: e.tensor_scalar_min(
                out=ddscr.ap(), in0=PS[3].ap()[:, 0:256], scalar1=0.0),
                after=[mmr], inc=True)
            prev_ps3 = tb
            te = p.op("scalar", lambda e: e.activation(
                out=ddscr.ap(), in_=ddscr.ap(), func=AF.Exp), after=[tb],
                inc=True)
            stoks.append(p.op("vector", lambda e, h=h, sc2=sc2: e.tensor_mul(
                out=dd[sc2].ap()[:, 256 * h:256 * (h + 1)],
                in0=ddscr.ap(), in1=mks[sc2].ap()), after=[te, sm_tok],
                inc=True))
    t1 = p.op("vector", lambda e: e.tensor_scalar_mul(
        out=srow.ap()[0:1, 0:8], in0=s8.ap(), scalar1=-256.0),
        after=[sm_tok, prev_mmr], inc=True)
    t2 = p.op("scalar", lambda e: e.activation(
        out=srow.ap()[0:1, 0:8], in_=srow.ap()[0:1, 0:8], func=AF.Exp),
        after=[t1], inc=True)
    bmv = p.op("tensor", lambda e: e.matmul(
        PS[3].ap()[:, 0:8], C.onesrow_f.ap(), srow.ap()[0:1, 0:8],
        start=True, stop=True), after=[t2, prev_ps3, C.const_tok], inc=True)
    bd_tok = p.op("vector", lambda e: e.tensor_copy(
        out=bdc.ap(), in_=PS[3].ap()[:, 0:8]), after=[bmv], inc=True)
    prev_ps3 = bd_tok
    dd_tok = stoks[-1]

    prev_w = [None, None]
    v_toks = [None] * 16
    wv_tok = []
    for k in range(16):
        qb, off = qf[k % 8], 1024 * (k // 8)
        wv_tok.append(p.dma("gpsimd",
                      qb.ap()[:, off:off + 1024],
                      wv_d[128 * k:128 * (k + 1), :], stream="qwv"))
    wv_all = wv_tok[-1]
    for t in range(16):
        last = None
        for k in range(16):
            qb, off = qf[k % 8], 1024 * (k // 8)
            for n0 in (0, 512):
                reg = PS[0] if n0 == 0 else PS[1]
                deps = [wv_all, yall[k]]
                if k == 0 and t > 0:
                    deps.append(v_toks[t - 1])
                last = p.op("tensor",
                    lambda e, k=k, n0=n0, t=t, qb=qb, off=off, reg=reg:
                    e.matmul(
                        reg.ap()[:, 0:512],
                        Y[k].ap()[:, 128 * t:128 * (t + 1)],
                        qb.ap()[:, off + n0:off + n0 + 512],
                        start=(k == 0), stop=(k == 15)),
                    after=deps, inc=(k == 15))
        e1 = p.op("scalar", lambda e, t=t: e.activation(
            out=vt[t].ap()[:, 0:512], in_=PS[0].ap()[:, 0:512],
            func=AF.Silu), after=[last], inc=True)
        v_toks[t] = p.op("scalar", lambda e, t=t: e.activation(
            out=vt[t].ap()[:, 512:1024], in_=PS[1].ap()[:, 0:512],
            func=AF.Silu), after=[e1], inc=True)
    v_all = v_toks[-1]

    q_toks, k_toks = [None] * 8, [None] * 8
    prev_ev = v_toks[15]
    for which, dst, wd, toks in (("q", qf, wq_d, q_toks),
                                 ("k", kf, wk_d, k_toks)):
        for m in range(8):
            s = m % 2
            dstw = wsl[s].ap().rearrange("p (k c) -> p k c", k=16)
            wdeps = [prev_w[s]]
            if which == "q":
                wdeps.append(v_all)
            lt = p.dma("gpsimd", dstw,
                       wd[:, 128 * m:128 * (m + 1)]
                       .rearrange("(k p) c -> p k c", p=128),
                       after=wdeps, stream=f"w{s}")
            for tg in range(4):
                last = None
                for k in range(16):
                    deps = [lt, yall[k]]
                    if k == 0:
                        deps.append(prev_ev)
                    last = p.op("tensor", lambda e, k=k, tg=tg, s=s:
                        e.matmul(
                            PS[0].ap()[:, 0:512],
                            wsl[s].ap()[:, 128 * k:128 * (k + 1)],
                            Y[k].ap()[:, 512 * tg:512 * (tg + 1)],
                            start=(k == 0), stop=(k == 15)),
                        after=deps, inc=(k == 15))
                prev_ev = p.op("scalar", lambda e, m=m, tg=tg, dst=dst:
                    e.activation(
                        out=dst[m].ap()[:, 512 * tg:512 * (tg + 1)],
                        in_=PS[0].ap()[:, 0:512], func=AF.Silu),
                    after=[last], inc=True)
            prev_w[s] = last
            toks[m] = prev_ev

    out_toks = {}
    prev_sc = k_toks[7]
    prev_o = v_toks[15]
    prev_kvps = None
    for h in range(8):
        kv_tok = None
        kvb_tok = None
        for blk in range(8):
            m0 = 256 * blk
            p_toks = []
            for nch in range(2):
                mm = p.op("tensor", lambda e, h=h, nch=nch, m0=m0: e.matmul(
                    PS[0].ap()[:, 0:256],
                    kf[h].ap()[:, m0 + 128 * nch:m0 + 128 * (nch + 1)],
                    qf[h].ap()[:, m0:m0 + 256],
                    start=True, stop=True),
                    after=[q_toks[h], k_toks[h], prev_sc], inc=True)
                pt = p.op("vector", lambda e, h=h, nch=nch: e.tensor_mul(
                    out=p_sb.ap()[:, 256 * nch:256 * (nch + 1)],
                    in0=PS[0].ap()[:, 0:256],
                    in1=dd[nch].ap()[:, 256 * h:256 * (h + 1)]),
                    after=[mm, dd_tok], inc=True)
                prev_sc = pt
                p_toks.append(pt)
            qs_tok = p.op("vector", lambda e, h=h, m0=m0: e.tensor_mul(
                out=qs_sb.ap(), in0=qf[h].ap()[:, m0:m0 + 256],
                in1=qd.ap()[:, 256 * h:256 * (h + 1)]),
                after=[q_toks[h], dd_tok, prev_o], inc=True)
            last = None
            for nch in range(2):
                tch = 2 * blk + nch
                last = p.op("tensor", lambda e, h=h, nch=nch, tch=tch,
                            blk=blk: e.matmul(
                        PS[1].ap()[:, 0:256],
                        vt[tch].ap()[:, 128 * h:128 * (h + 1)],
                        p_sb.ap()[:, 256 * nch:256 * (nch + 1)],
                        start=(nch == 0),
                        stop=(nch == 1 and blk == 0)),
                    after=[p_toks[nch], v_toks[tch], prev_o],
                    inc=(nch == 1 and blk == 0))
            if blk > 0:
                last = p.op("tensor", lambda e, h=h: e.matmul(
                    PS[1].ap()[:, 0:256], kvb.ap(), qs_sb.ap(),
                    start=False, stop=True),
                    after=[qs_tok, kvb_tok], inc=True)
            oev_deps = [last]
            if blk == 0:
                oev_deps += [q_toks[7], k_toks[7]]
            oev = p.op("vector", lambda e, h=h, m0=m0: e.tensor_copy(
                out=OUT[h].ap()[:, m0:m0 + 256], in_=PS[1].ap()[:, 0:256]),
                after=oev_deps, inc=True)
            prev_o = oev
            out_toks[(h, blk)] = oev
            if blk < 7:
                kst_toks = []
                for nch in range(2):
                    tt = p.op("tensor", lambda e, h=h, nch=nch, m0=m0:
                        e.transpose(
                            PSTB.ap()[:, 0:128],
                            kf[h].ap()[:, m0 + 128 * nch:
                                       m0 + 128 * (nch + 1)],
                            C.ident_b.ap()),
                        after=[k_toks[h], C.ident_tok,
                               kst_toks[-1] if kst_toks else prev_kvps],
                        inc=True)
                    kst_toks.append(p.op("vector",
                        lambda e, h=h, nch=nch: e.tensor_scalar_mul(
                            out=kst.ap()[:, 128 * nch:128 * (nch + 1)],
                            in0=PSTB.ap()[:, 0:128],
                            scalar1=kdT.ap()[:, 2 * h + nch:
                                             2 * h + nch + 1]),
                        after=[tt], inc=True))
                mmk = None
                for nch in range(2):
                    tch = 2 * blk + nch
                    mmk = p.op("tensor", lambda e, h=h, nch=nch, tch=tch:
                        e.matmul(
                            PS[2].ap()[:, 0:128],
                            kst.ap()[:, 128 * nch:128 * (nch + 1)],
                            vt[tch].ap()[:, 128 * h:128 * (h + 1)],
                            start=(nch == 0), stop=(nch == 1)),
                        after=[kst_toks[nch], v_toks[tch], prev_kvps],
                        inc=(nch == 1))
                if blk == 0:
                    kv_tok = p.op("vector", lambda e: e.tensor_copy(
                        out=kv.ap(), in_=PS[2].ap()[:, 0:128]),
                        after=[mmk], inc=True)
                else:
                    kv_tok = p.op("vector", lambda e, h=h:
                        e.scalar_tensor_tensor(
                            out=kv.ap(), in0=kv.ap(),
                            scalar=bdc.ap()[:, h:h + 1],
                            in1=PS[2].ap()[:, 0:128],
                            op0=ALU.mult, op1=ALU.add),
                        after=[mmk, bd_tok, kv_tok], inc=True)
                prev_kvps = kv_tok
                kvb_tok = p.op("vector", lambda e: e.tensor_copy(
                    out=kvb.ap(), in_=kv.ap()), after=[kv_tok], inc=True)

    sq_toks = []
    prev_mq = None
    for tg in range(4):
        last = None
        for h in range(8):
            tq = p.op("scalar", lambda e, h=h, tg=tg: e.activation(
                out=C.xb.ap()[:, 0:256],
                in_=OUT[h].ap()[:, 512 * tg:512 * tg + 256],
                func=AF.Square),
                after=[out_toks[(h, 7)], last, prev_mq], inc=True)
            tq2 = p.op("scalar", lambda e, h=h, tg=tg: e.activation(
                out=C.sq.ap()[:, 0:256],
                in_=OUT[h].ap()[:, 512 * tg + 256:512 * (tg + 1)],
                func=AF.Square), after=[tq], inc=True)
            mq = p.op("tensor", lambda e, h=h, tg=tg: e.matmul(
                PS[3].ap()[0:1, 0:256], C.onescol.ap(),
                C.xb.ap()[:, 0:256],
                start=(h == 0), stop=(h == 7)),
                after=[tq, C.const_tok], inc=True)
            prev_mq = mq
            mq2 = p.op("tensor", lambda e, h=h, tg=tg: e.matmul(
                PS[2].ap()[0:1, 0:256], C.onescol.ap(),
                C.sq.ap()[:, 0:256],
                start=(h == 0), stop=(h == 7)),
                after=[tq2, C.const_tok], inc=True)
            last = mq2
        ev0 = p.op("vector", lambda e, tg=tg: e.tensor_copy(
            out=sqp.ap()[0:1, 0:256],
            in_=PS[3].ap()[0:1, 0:256]),
            after=[last, sq_toks[-1] if sq_toks else None], inc=True)
        ev = p.op("vector", lambda e, tg=tg: e.tensor_copy(
            out=sqp.ap()[0:1, 256:512],
            in_=PS[2].ap()[0:1, 0:256]), after=[ev0], inc=True)
        prev_mq = ev
        sq_toks.append(p.dma("sync", sqp_d[0:1, 512 * tg:512 * (tg + 1)],
                             sqp.ap(), after=[ev], stream="qsq"))

    for h in range(8):
        p.dma("sync", oraw_d[128 * h:128 * (h + 1), :], OUT[h].ap(),
              after=[out_toks[(h, 7)], p.now("vector"), p.now("scalar")],
              stream="qout")
    p.build()
    return nc


def build_a1b_launch():
    nc = bass.Bass("TRN2", target_bir_lowering=False, debug=False)
    T = 1024
    x_d = nc.dram_tensor("x", [2048, T], F32, kind="ExternalInput").ap()
    or_d = nc.dram_tensor("oraw", [2048, T], BF16, kind="ExternalInput").ap()
    sqa_d = nc.dram_tensor("sqa", [1, T], F32, kind="ExternalInput").ap()
    sqb_d = nc.dram_tensor("sqb", [1, T], F32, kind="ExternalInput").ap()
    wg_d = nc.dram_tensor("wgT", [2048, 2048], BF16,
                          kind="ExternalInput").ap()
    wo_d = nc.dram_tensor("woT", [2048, 2048], BF16,
                          kind="ExternalInput").ap()
    nw_d = nc.dram_tensor("norm_w", [128, 16], F32, kind="ExternalInput").ap()
    xo_d = nc.dram_tensor("xo", [2048, T], F32, kind="ExternalOutput").ap()
    p = Plan(nc)
    C = Common(nc, p, T)
    PS = C.PS
    NF = 16
    R = [nc.alloc_sbuf_tensor(f"R{i}", [128, T], F32) for i in range(NF)]
    Y = [nc.alloc_sbuf_tensor(f"Yb{i}", [128, T], BF16) for i in range(NF)]
    ON = [nc.alloc_sbuf_tensor(f"c_on{i}", [128, T], BF16)
          for i in range(NF)]
    g_sb = nc.alloc_sbuf_tensor("c_gsb", [128, T], BF16)
    nw = nc.alloc_sbuf_tensor("c_nw", [128, 16], F32)
    sqr = nc.alloc_sbuf_tensor("c_sqr", [1, T], F32)
    sqr2 = nc.alloc_sbuf_tensor("c_sqr2", [1, T], F32)
    wsl = [nc.alloc_sbuf_tensor(f"c_wsl{i}", [128, 2048], BF16)
           for i in range(2)]

    xt = [p.dma("sync", R[i].ap(), x_d[128 * i:128 * (i + 1), :],
                stream="qx") for i in range(NF)]
    ot = [p.dma("sync", ON[i].ap(), or_d[128 * i:128 * (i + 1), :],
                stream="qor") for i in range(NF)]
    s1 = p.dma("sync", sqr.ap(), sqa_d, stream="qsm")
    s2 = p.dma("sync", sqr2.ap(), sqb_d, stream="qsm")
    s3 = p.dma("sync", nw.ap(), nw_d, stream="qsm")

    ln_toks = C.layernorm([R[i].ap() for i in range(NF)],
                          [Y[i].ap() for i in range(NF)],
                          eps=1e-5, after=xt)
    t1 = p.op("vector", lambda e: e.tensor_add(
        out=sqr.ap(), in0=sqr.ap(), in1=sqr2.ap()),
        after=[s1, s2, s3], inc=True)
    t2 = p.op("scalar", lambda e: e.activation(
        out=sqr.ap(), in_=sqr.ap(), func=AF.Ln,
        bias=C.constf(1e-6 * 2048)), after=[t1], inc=True)
    t3 = p.op("scalar", lambda e: e.activation(
        out=sqr.ap(), in_=sqr.ap(), func=AF.Exp, scale=-0.5),
        after=[t2], inc=True)
    t4 = p.op("vector", lambda e: e.tensor_scalar_mul(
        out=sqr.ap(), in0=sqr.ap(), scalar1=float(np.sqrt(2048.0))),
        after=[t3], inc=True)
    bts = []
    for n0 in range(0, T, 512):
        bts.append(p.op("tensor", lambda e, n0=n0: e.matmul(
            PS[2].ap()[:, n0:n0 + 512], C.onesrow_f.ap(),
            sqr.ap()[0:1, n0:n0 + 512], start=True, stop=True),
            after=[t4, ln_toks[-1], C.const_tok], inc=True))
    on_toks = []
    for i in range(NF):
        on_toks.append(p.op("vector", lambda e, i=i:
            e.scalar_tensor_tensor(
                out=ON[i].ap(), in0=ON[i].ap(), scalar=nw.ap()[:, i:i + 1],
                in1=PS[2].ap()[:, :T], op0=ALU.mult, op1=ALU.mult),
            after=bts + [ot[-1], s3], inc=True))
    prev_w = [None, None]
    prev_sig = None
    go_toks = []
    for m in range(NF):
        s = m % 2
        dstw = wsl[s].ap().rearrange("p (k c) -> p k c", k=16)
        lt = p.dma("gpsimd", dstw,
                   wg_d[:, 128 * m:128 * (m + 1)]
                   .rearrange("(k p) c -> p k c", p=128),
                   after=[prev_w[s]], stream=f"wg{s}")
        last = None
        for k in range(16):
            for n0 in (0, 512):
                deps = [lt, ln_toks[k]]
                if k == 0 and n0 == 0:
                    deps.append(prev_sig)
                last = p.op("tensor", lambda e, k=k, n0=n0, s=s: e.matmul(
                    PS[3].ap()[:, n0:n0 + 512],
                    wsl[s].ap()[:, 128 * k:128 * (k + 1)],
                    Y[k].ap()[:, n0:n0 + 512],
                    start=(k == 0), stop=(k == 15)),
                    after=deps, inc=(k == 15 and n0 == 512))
        prev_w[s] = last
        tsig = p.op("scalar", lambda e: e.activation(
            out=g_sb.ap(), in_=PS[3].ap()[:, :T], func=AF.Sigmoid),
            after=[last], inc=True)
        prev_sig = tsig
        go_toks.append(p.op("vector", lambda e, m=m: e.tensor_mul(
            out=ON[m].ap(), in0=ON[m].ap(), in1=g_sb.ap()),
            after=[tsig, on_toks[m]], inc=True))
        prev_sig = go_toks[-1]
    r_toks = []
    prev_ev = None
    for m in range(NF):
        s = m % 2
        dstw = wsl[s].ap().rearrange("p (k c) -> p k c", k=16)
        lt = p.dma("gpsimd", dstw,
                   wo_d[:, 128 * m:128 * (m + 1)]
                   .rearrange("(k p) c -> p k c", p=128),
                   after=[prev_w[s]], stream=f"wg{s}")
        last = None
        for k in range(16):
            for n0 in (0, 512):
                deps = [lt, go_toks[k]]
                if k == 0 and n0 == 0:
                    deps.append(prev_ev)
                last = p.op("tensor", lambda e, k=k, n0=n0, s=s: e.matmul(
                    PS[0].ap()[:, n0:n0 + 512],
                    wsl[s].ap()[:, 128 * k:128 * (k + 1)],
                    ON[k].ap()[:, n0:n0 + 512],
                    start=(k == 0), stop=(k == 15)),
                    after=deps, inc=(k == 15 and n0 == 512))
        prev_w[s] = last
        prev_ev = p.op("vector", lambda e, m=m: e.tensor_add(
            out=R[m].ap(), in0=PS[0].ap()[:, :T], in1=R[m].ap()),
            after=[last], inc=True)
        r_toks.append(prev_ev)
    for i in range(NF):
        p.dma("sync", xo_d[128 * i:128 * (i + 1), :], R[i].ap(),
              after=[r_toks[i]], stream="qout")
    p.build()
    return nc


def run_a1(x_shards, kw, prefix="L1_attn_", cast=None):
    if cast is None:
        cast = run_cast(kw)
    g = lambda k: np.asarray(kw[prefix + k], np.float32)
    wqkvT = cast["a1_wqkv"]
    slopes = np.asarray(kw[prefix + "slope_rate"], np.float32).reshape(-1)
    nc_a = build_a1a_launch()
    in_a = []
    for c in range(8):
        b, hh = c // 2, c % 2
        heads = range(8 * hh, 8 * hh + 8)
        xfull = np.ascontiguousarray(
            np.concatenate([x_shards[2 * b], x_shards[2 * b + 1]], axis=1))
        in_a.append({
            "x": xfull,
            "wqT": np.ascontiguousarray(np.concatenate(
                [wqkvT[:, 384 * h:384 * h + 128] for h in heads], axis=1)),
            "wkT": np.ascontiguousarray(np.concatenate(
                [wqkvT[:, 384 * h + 128:384 * h + 256] for h in heads],
                axis=1)),
            "wvT": np.ascontiguousarray(np.concatenate(
                [wqkvT[:, 384 * h + 256:384 * h + 384] for h in heads],
                axis=1)),
            "slopes8": np.ascontiguousarray(
                slopes[8 * hh:8 * hh + 8][None, :]),
        })
    res = run_bass_kernel_spmd(nc_a, in_a, core_ids=list(range(8)))
    oraw = [res.results[c]["oraw"] for c in range(8)]
    sqp = [res.results[c]["sqp"] for c in range(8)]

    nc_b = build_a1b_launch()
    wgT = cast["a1_wg"]
    woT = cast["a1_wo"]
    nwfm = np.ascontiguousarray(g("norm_w").reshape(16, 128).T)
    in_b = []
    for c in range(8):
        b, hh = c // 2, c % 2
        sl = slice(1024 * hh, 1024 * (hh + 1))
        in_b.append({
            "x": x_shards[c],
            "oraw": np.ascontiguousarray(np.concatenate(
                [oraw[2 * b][:, sl], oraw[2 * b + 1][:, sl]], axis=0)),
            "sqa": np.ascontiguousarray(sqp[2 * b][:, sl]),
            "sqb": np.ascontiguousarray(sqp[2 * b + 1][:, sl]),
            "wgT": wgT, "woT": woT, "norm_w": nwfm,
        })
    res = run_bass_kernel_spmd(nc_b, in_b, core_ids=list(range(8)))
    return [res.results[c]["xo"] for c in range(8)]


def build_f1_launch(T=1024):
    nc = bass.Bass("TRN2", target_bir_lowering=False, debug=False)
    x_d = nc.dram_tensor("x", [2048, T], F32, kind="ExternalInput").ap()
    W = {
        "gate_wT": nc.dram_tensor("gate_wT", [2048, 8], F32,
                                  kind="ExternalInput").ap(),
        "w1": nc.dram_tensor("w1", [8, 2048, 1408], BF16,
                             kind="ExternalInput").ap(),
        "w3": nc.dram_tensor("w3", [8, 2048, 1408], BF16,
                             kind="ExternalInput").ap(),
        "w2": nc.dram_tensor("w2", [8, 1408, 2048], BF16,
                             kind="ExternalInput").ap(),
        "sw1": nc.dram_tensor("sw1", [2048, 1408], BF16,
                              kind="ExternalInput").ap(),
        "sw3": nc.dram_tensor("sw3", [2048, 1408], BF16,
                              kind="ExternalInput").ap(),
        "sw2": nc.dram_tensor("sw2", [1408, 2048], BF16,
                              kind="ExternalInput").ap(),
    }
    xo_d = nc.dram_tensor("xo", [2048, T], F32, kind="ExternalOutput").ap()

    p = Plan(nc)
    C = Common(nc, p, T)
    NF = 16
    R = [nc.alloc_sbuf_tensor(f"R{i}", [128, T], F32) for i in range(NF)]
    Y = [nc.alloc_sbuf_tensor(f"Yb{i}", [128, T], BF16) for i in range(NF)]

    xt = [p.dma("sync", R[i].ap(), x_d[128 * i:128 * (i + 1), :],
                stream="qx") for i in range(NF)]
    ln_toks = C.layernorm([R[i].ap() for i in range(NF)],
                          [Y[i].ap() for i in range(NF)],
                          eps=1e-5, after=xt)
    emit_f1(nc, p, C, Y, R, T, W, y_toks=ln_toks)
    for i in range(NF):
        p.dma("sync", xo_d[128 * i:128 * (i + 1), :], R[i].ap(),
              after=[p.now("vector")], stream="qout")
    p.build()
    return nc



CAST_SPECS = [
    ("f1_w1", 16384, 1408),
    ("f1_w3", 16384, 1408),
    ("f1_w2", 11264, 2048),
    ("f1_sw1", 2048, 1408),
    ("f1_sw3", 2048, 1408),
    ("f1_sw2", 1408, 2048),
    ("f0_agg", 4096, 1024),
    ("f0_d1", 16384, 1024),
    ("f0_d2", 16384, 1024),
    ("f0_bc", 4096, 1024),
    ("f0_gate", 4096, 1024),
    ("a0_win", 2048, 8512),
    ("a0_wout", 4096, 2048),
    ("a1_wqkv", 2048, 6144),
    ("a1_wg", 2048, 2048),
    ("a1_wo", 2048, 2048),
]


def build_cast_launch():
    nc = bass.Bass("TRN2", target_bir_lowering=False, debug=False)
    p = Plan(nc)
    NB = 6
    bufs = [nc.alloc_sbuf_tensor(f"cb{i}", [128, 8512], BF16)
            for i in range(NB)]
    prev_out = [None] * NB
    b = 0
    for nm, rows, cols in CAST_SPECS:
        r8 = rows // 8
        in_d = nc.dram_tensor(nm + "_f", [r8, cols], F32,
                              kind="ExternalInput").ap()
        out_d = nc.dram_tensor(nm + "_b", [r8, cols], BF16,
                               kind="ExternalOutput").ap()
        for r0 in range(0, r8, 128):
            rp = min(128, r8 - r0)
            lt = p.dma("gpsimd", bufs[b].ap()[0:rp, 0:cols],
                       in_d[r0:r0 + rp, :], after=[prev_out[b]],
                       stream=f"ci{b}")
            prev_out[b] = p.dma("sync", out_d[r0:r0 + rp, :],
                                bufs[b].ap()[0:rp, 0:cols], after=[lt],
                                stream=f"co{b}")
            b = (b + 1) % NB
    p.build()
    return nc


def _tile_mmtm(a, kin, nout):
    a = _c(a)
    K, Q = kin // 128, nout // 2048
    t = a.reshape(K, 128, Q, 2, 1024).transpose(2, 0, 3, 1, 4)
    return np.ascontiguousarray(t.reshape(Q * K * 2 * 128, 1024))


def _tile_gate(a):
    a = _c(a)
    t = a.reshape(2, 8, 128, 16, 128).transpose(3, 0, 2, 1, 4)
    return np.ascontiguousarray(t.reshape(16 * 2 * 128, 8 * 128))


def cast_arrays(kw):
    return {
        "f1_w1": _c(kw["L1_ffn_w1"]).reshape(16384, 1408),
        "f1_w3": _c(kw["L1_ffn_w3"]).reshape(16384, 1408),
        "f1_w2": _c(kw["L1_ffn_w2"]).reshape(11264, 2048),
        "f1_sw1": _c(kw["L1_ffn_sw1"]),
        "f1_sw3": _c(kw["L1_ffn_sw3"]),
        "f1_sw2": _c(kw["L1_ffn_sw2"]),
        "f0_agg": _tile_mmtm(kw["L0_ffn_agg_w"], 2048, 2048),
        "f0_d1": _tile_mmtm(kw["L0_ffn_d1_w"], 2048, 8192),
        "f0_d2": _tile_mmtm(kw["L0_ffn_d2_w"], 8192, 2048),
        "f0_bc": _tile_mmtm(kw["L0_ffn_bc_w"], 2048, 2048),
        "f0_gate": _tile_gate(kw["L0_ffn_gate_w"]),
        "a0_win": _t(kw["L0_attn_W_in"]),
        "a0_wout": _t(kw["L0_attn_W_out"]),
        "a1_wqkv": _t(kw["L1_attn_w_qkv"]),
        "a1_wg": _t(kw["L1_attn_w_gate"]),
        "a1_wo": _t(kw["L1_attn_w_out"]),
    }


def run_cast(kw):
    nc = build_cast_launch()
    arrs = cast_arrays(kw)
    in_maps = []
    for c in range(8):
        m = {}
        for nm, rows, cols in CAST_SPECS:
            r8 = rows // 8
            m[nm + "_f"] = np.ascontiguousarray(
                arrs[nm][c * r8:(c + 1) * r8])
        in_maps.append(m)
    res = run_bass_kernel_spmd(nc, in_maps, core_ids=list(range(8)))
    return {nm: np.concatenate(
                [res.results[c][nm + "_b"] for c in range(8)], axis=0)
            for nm, rows, cols in CAST_SPECS}



def _t(a):
    return np.ascontiguousarray(np.asarray(a, dtype=np.float32).T)


def _c(a):
    return np.ascontiguousarray(np.asarray(a, dtype=np.float32))


def shard_tokens_fm(x):
    out = []
    for c in range(8):
        b, h = c // 2, c % 2
        out.append(_t(x[b, 1024 * h:1024 * h + 1024, :]))
    return out


def unshard_tokens_fm(shards):
    x = np.empty((4, 2048, 2048), dtype=np.float32)
    for c in range(8):
        b, h = c // 2, c % 2
        x[b, 1024 * h:1024 * h + 1024, :] = shards[c].T
    return x


def a0_inputs(kw, prefix="L0_attn_"):
    g = lambda k: np.asarray(kw[prefix + k], np.float32)
    conv_w = g("conv_w").reshape(34, 128, 4).transpose(1, 0, 2).reshape(128, 136)
    return {
        "conv_w": np.ascontiguousarray(conv_w),
        "conv_b": np.ascontiguousarray(g("conv_b").reshape(34, 128).T),
        "dt_bias": np.ascontiguousarray(g("dt_bias")[:, None]),
        "A_log": np.ascontiguousarray(g("A_log")[:, None]),
        "D_fm": np.ascontiguousarray(np.repeat(g("D"), 64).reshape(32, 128).T),
        "norm_w": np.ascontiguousarray(g("norm_w").reshape(32, 128).T),
    }


def run_a0(x_shards, kw, prefix="L0_attn_", cast=None):
    if cast is None:
        cast = run_cast(kw)
    nc = build_a0_launch()
    wmap = a0_inputs(kw, prefix)
    wmap["w_inT"] = cast["a0_win"]
    wmap["w_outT"] = cast["a0_wout"]
    in_maps = []
    for c in range(8):
        halo = (np.zeros((2048, 3), np.float32) if c % 2 == 0
                else np.ascontiguousarray(x_shards[c - 1][:, 1021:1024]))
        m = dict(wmap)
        m["x"] = np.ascontiguousarray(
            np.concatenate([halo, x_shards[c]], axis=1))
        in_maps.append(m)
    res = run_bass_kernel_spmd(nc, in_maps, core_ids=list(range(8)))
    return [res.results[c]["xo"] for c in range(8)]


def f1_inputs(weights_prefix, kw, cast):
    return {
        "gate_wT": _t(kw[weights_prefix + "gate_w"]),
        "w1": cast["f1_w1"].reshape(8, 2048, 1408),
        "w3": cast["f1_w3"].reshape(8, 2048, 1408),
        "w2": cast["f1_w2"].reshape(8, 1408, 2048),
        "sw1": cast["f1_sw1"],
        "sw3": cast["f1_sw3"],
        "sw2": cast["f1_sw2"],
    }


def run_f1(x_shards, kw, prefix="L1_ffn_", cast=None):
    if cast is None:
        cast = run_cast(kw)
    nc = build_f1_launch()
    wmap = f1_inputs(prefix, kw, cast)
    in_maps = [dict(wmap, x=x_shards[c]) for c in range(8)]
    res = run_bass_kernel_spmd(nc, in_maps, core_ids=list(range(8)))
    return [res.results[c]["xo"] for c in range(8)]


def _xlast64(shards):
    xl = np.ascontiguousarray(
        np.stack([shards[2 * b + 1][:, 1023] for b in range(4)], axis=1))
    return np.ascontiguousarray(
        xl.reshape(16, 128, 4).transpose(1, 0, 2).reshape(128, 64))


def kernel(**kw):
    x = np.asarray(kw["L0_attn_input_states"], dtype=np.float32)
    cast = run_cast(kw)
    shards = shard_tokens_fm(x)
    shards = run_f0(shards, _xlast64(shards), kw, "L0_ffn_", cast)
    shards = run_a0(shards, kw, "L0_attn_", cast)
    shards = run_f0(shards, _xlast64(shards), kw, "L0_ffn_", cast)
    shards = run_f1(shards, kw, "L1_ffn_", cast)
    shards = run_a1(shards, kw, "L1_attn_", cast)
    shards = run_f1(shards, kw, "L1_ffn_", cast)
    return unshard_tokens_fm(shards)
